# version 02
# baseline (speedup 1.0000x reference)
import sys
if '/opt/trn_rl_repo' not in sys.path:
    sys.path.insert(0, '/opt/trn_rl_repo')
import math
from contextlib import ExitStack

import numpy as np
import ml_dtypes

import concourse.bass as bass
import concourse.mybir as mybir
from concourse.bass_utils import run_bass_kernel_spmd

F32 = mybir.dt.float32
BF16 = mybir.dt.bfloat16
I32 = mybir.dt.int32
AF = mybir.ActivationFunctionType
ALU = mybir.AluOpType
AX = mybir.AxisListType.X

D = 4096
HD = 128
NHG = 32
KC = D // 128
QTW = 512
INV_SQRT_HD = np.float32(1.0 / math.sqrt(HD))
C_BOUND = float(-1.0 / (2.0 * math.sqrt(HD)))
EPS_F = np.float32(1e-5)


class Ctr:

    def __init__(self, sem):
        self.sem = sem
        self.n = 0

    def bump(self, inst, k=1):
        inst.then_inc(self.sem, k)
        self.n += k
        return self.n


def host_tables(T):
    pos = np.arange(T, dtype=np.float32)
    thetas = (1.0 / (10000.0 ** (np.arange(0, HD, 2, dtype=np.float32) / HD))).astype(np.float32)
    pt = np.outer(pos, thetas).astype(np.float32)
    pt = np.concatenate([pt, pt], axis=-1)
    sinT = np.ascontiguousarray(np.sin(pt).astype(np.float32).T)
    cosT = np.ascontiguousarray(np.cos(pt).astype(np.float32).T)
    j = np.arange(128)[:, None]
    u = np.arange(896)[None, :]
    maskneg = np.where(j <= u - 384, 0.0, -1e9).astype(np.float32)
    return sinT, cosT, maskneg


class Builder:
    def __init__(self, T, dbg=False):
        self.T = T
        self.TO = T // 2
        self.NH = NHG // 2
        self.FO = self.NH * HD
        self.NQ = T // QTW
        self.dbg = dbg
        nc = bass.Bass("TRN2", target_bir_lowering=False,
                       detect_race_conditions=False)
        self.nc = nc
        self.ctx = ExitStack()
        TO, FO = self.TO, self.FO

        di = lambda name, shape, dt: nc.dram_tensor(name, shape, dt, kind="ExternalInput").ap()
        self.xin = di("xin", [TO, D], F32)
        self.wkT = di("wkT", [D, FO], F32)
        self.wvT = di("wvT", [D, FO], F32)
        self.wqT = di("wqT", [D, FO], F32)
        self.woT = di("woT", [FO, D], F32)
        self.pakT = di("pakT", [D // 16, D], I32)
        self.sclx = di("sclx", [D // 8, D], F32)
        self.cent = di("cent", [1, 16], F32)
        self.sinTd = di("sinT", [HD, T], F32)
        self.cosTd = di("cosT", [HD, T], F32)
        self.maskd = di("maskneg", [128, 896], F32)
        self.cinfo = di("cinfo", [1, 8], I32)
        self.xout = nc.dram_tensor("xout", [TO, D], F32, kind="ExternalOutput").ap()

        dsc = lambda name, shape, dt: nc.dram_tensor(name, shape, dt, kind="Internal").ap()
        self.wdq_in = dsc("wdq_in", [D // 8, D], BF16)
        self.wdq = dsc("wdq", [D, D], BF16)
        self.xabs = dsc("xabs", [TO, D], BF16)
        self.xbo = dsc("xbo", [TO, D], BF16)
        self.xbf = dsc("xbf", [T, D], BF16)
        self.x1s = dsc("x1s", [TO, D], F32)
        self.x2s = dsc("x2s", [TO, D], F32)
        self.kts = dsc("kts", [FO, T], BF16)
        self.qts = dsc("qts", [FO, T], BF16)
        self.vts = dsc("vts", [T, FO], BF16)
        self.atns = dsc("atns", [FO, T], BF16)
        self.op4 = dsc("op4", [T, D], F32)
        self.x2p = dsc("x2p", [TO, D], F32)
        self.dbg_views = {}
        if dbg:
            for nm, ap_ in [("wdq", self.wdq), ("xbf", self.xbf),
                            ("x1s", self.x1s), ("x2s", self.x2s),
                            ("kts", self.kts), ("qts", self.qts),
                            ("vts", self.vts), ("atns", self.atns)]:
                self.dbg_views[nm] = nc.dram_tensor(
                    "dbg_" + nm, list(ap_.tensor.shape), ap_.dtype,
                    kind="ExternalOutput").ap()

        sem = lambda name: self.ctx.enter_context(nc.semaphore(name))
        self.qm = Ctr(sem("qm"))
        self.qx = [Ctr(sem("qx0")), Ctr(sem("qx1"))]
        self.qw = [Ctr(sem("qw0")), Ctr(sem("qw1"))]
        self.qv = [Ctr(sem("qv0")), Ctr(sem("qv1")), Ctr(sem("qv2"))]
        self.gm = Ctr(sem("gm"))
        self.gw = [Ctr(sem("gw0")), Ctr(sem("gw1"))]
        self.gs = [Ctr(sem("gs0")), Ctr(sem("gs1"))]
        self.am = Ctr(sem("am"))
        self.asx = [Ctr(sem("as0")), Ctr(sem("as1"))]
        self.sMM = Ctr(sem("sMM"))
        self.sEV = Ctr(sem("sEV"))
        self.sDV = Ctr(sem("sDV"))
        self.sPL = Ctr(sem("sPL"))
        self.sCC = Ctr(sem("sCC"))

        sb = lambda name, shape, dt: self.ctx.enter_context(nc.sbuf_tensor(name, shape, dt))
        self.ones_col = sb("ones_col", [128, 1], F32)
        self.ones_row = sb("ones_row", [1, 128], F32)
        self.mask_sb = sb("mask_sb", [128, 896], F32)
        self.cent_bc = sb("cent_bc", [128, 16], F32)
        self.cinfo_sb = sb("cinfo_sb", [1, 8], I32)

    @staticmethod
    def flush(eng, ctrs):
        for c in ctrs:
            if c.n > 0:
                eng.wait_ge(c.sem, c.n)

    def prologue(self):
        nc = self.nc
        qm, gm, sMM, sEV, sDV = self.qm, self.gm, self.sMM, self.sEV, self.sDV
        ctxp = ExitStack()
        pb = ctxp.enter_context(nc.psum_tensor("pb_cent", [128, 16], F32))
        with nc.Block("prologue") as b:
            @b.sync
            def _(sync):
                qm.bump(sync.dma_start(self.mask_sb[:, :], self.maskd[:, :]), 16)
                qm.bump(sync.dma_start(self.cinfo_sb[:, :], self.cinfo[:, :]), 16)
                sync.wait_ge(qm.sem, qm.n)

            @b.vector
            def _(v):
                v.memset(self.ones_col[:, :], 1.0)
                sDV.bump(v.memset(self.ones_row[:, :], 1.0), 1)

            @b.gpsimd
            def _(g):
                gm.bump(g.dma_start(self.cent_bc[0:1, 0:16], self.cent[:, :]), 16)

            @b.tensor
            def _(t):
                t.wait_ge(gm.sem, gm.n)
                t.wait_ge(sDV.sem, sDV.n)
                sMM.bump(t.matmul(pb[:, :], self.ones_row[:, :],
                                  self.cent_bc[0:1, 0:16], start=True, stop=True), 1)

            @b.scalar
            def _(s):
                s.wait_ge(sMM.sem, sMM.n)
                sEV.bump(s.copy(self.cent_bc[:, :], pb[:, :]), 1)
        ctxp.close()
        self.ev_cent = sEV.n

    def _deq_slab(self, eng, ctr, bufset, c0, c1, whl, nib, sclh, scll,
                  q_ready, gb_st):
        gm = self.gm
        terms, accs = bufset
        w = c1 - c0
        eng.wait_ge(q_ready[0].sem, q_ready[1])
        if gb_st is not None:
            eng.wait_ge(gm.sem, gb_st)
        nh = nib[:, c0:c1]
        nl = nib[:, D + c0:D + c1]
        tb = lambda j: terms[j % 4]
        ab = lambda j: accs[j % 2]
        done = 0
        for j in range(16):
            eng.tensor_scalar(tb(j)[:, 0:w], nh, float(j),
                              self.cent_bc[:, j:j + 1], ALU.is_equal, ALU.mult)
            eng.tensor_scalar(tb(j)[:, w:2 * w], nl, float(j),
                              self.cent_bc[:, j:j + 1], ALU.is_equal, ALU.mult)
            if j >= 2:
                jj = done
                if jj < 2:
                    eng.tensor_copy(ab(jj)[:, 0:2 * w], tb(jj)[:, 0:2 * w])
                else:
                    eng.tensor_tensor(ab(jj)[:, 0:2 * w], ab(jj)[:, 0:2 * w],
                                      tb(jj)[:, 0:2 * w], ALU.add)
                done += 1
        v = ctr.bump(eng.engine_nop(), 1)
        eng.wait_ge(ctr.sem, v)
        while done < 16:
            jj = done
            eng.tensor_tensor(ab(jj)[:, 0:2 * w], ab(jj)[:, 0:2 * w],
                              tb(jj)[:, 0:2 * w], ALU.add)
            done += 1
        v = ctr.bump(eng.engine_nop(), 1)
        eng.wait_ge(ctr.sem, v)
        eng.tensor_tensor(accs[0][:, 0:2 * w], accs[0][:, 0:2 * w],
                          accs[1][:, 0:2 * w], ALU.add)
        v = ctr.bump(eng.engine_nop(), 1)
        eng.wait_ge(ctr.sem, v)
        eng.tensor_tensor(whl[:, c0:c1], accs[0][:, 0:w], sclh[:, c0:c1], ALU.mult)
        ctr.bump(eng.tensor_tensor(whl[:, D + c0:D + c1], accs[0][:, w:2 * w],
                                   scll[:, c0:c1], ALU.mult), 1)

    def p0_dequant(self):
        nc = self.nc
        qm, gm, sDV, sPL, sCC = self.qm, self.gm, self.sDV, self.sPL, self.sCC
        CW = 512
        dv_slabs = [(c, c + CW) for c in range(0, 3072, CW)]
        pl_slabs = [(c, c + CW) for c in range(3072, D, CW)]
        ctx0 = ExitStack()
        sb = lambda n, s, d: ctx0.enter_context(nc.sbuf_tensor(n, s, d))
        pak = sb("pak", [128, D], I32)
        nibi = sb("nibi", [128, 2 * D], I32)
        nib = sb("nib", [128, 2 * D], F32)
        whl = sb("whl", [128, 2 * D], BF16)
        sclh = sb("sclh", [128, D], F32)
        scll = sb("scll", [128, D], F32)

        def mkset(tag):
            terms = [sb(f"{tag}_t{i}", [128, 2 * CW], F32) for i in range(4)]
            accs = [sb(f"{tag}_a{i}", [128, 2 * CW], F32) for i in range(2)]
            return (terms, accs)

        set_dv = mkset("ddv")
        set_pl = mkset("dpl")

        qb, gb = qm.n, gm.n
        db, plb = sDV.n, sPL.n
        ND, NP = len(dv_slabs), len(pl_slabs)

        q_grp = lambda ci: qb + 48 * (ci + 1)
        dv_ext = lambda ci: db + ci * (1 + 4 * ND) + 1
        dv_done = lambda ci, k: db + ci * (1 + 4 * ND) + 1 + 4 * k
        pl_done = lambda ci, k: plb + 4 * (ci * NP + k)
        gm_st = lambda ci: gb + 32 * (ci + 1)

        with nc.Block("p0") as b:
            @b.sync
            def _(sync):
                for ci in range(2):
                    if ci == 1:
                        sync.wait_ge(sDV.sem, dv_done(0, ND))
                        sync.wait_ge(sPL.sem, pl_done(0, NP))
                    qm.bump(sync.dma_start(pak[:, :],
                                           self.pakT[ci * 128:(ci + 1) * 128, :]), 16)
                    qm.bump(sync.dma_start(sclh[:, :],
                                           self.sclx[ci * 256:(ci + 1) * 256:2, :]), 16)
                    qm.bump(sync.dma_start(scll[:, :],
                                           self.sclx[ci * 256 + 1:(ci + 1) * 256:2, :]), 16)

            @b.vector
            def _(v):
                for ci in range(2):
                    v.wait_ge(qm.sem, q_grp(ci))
                    v.wait_ge(self.sEV.sem, self.ev_cent)
                    if ci == 1:
                        v.wait_ge(sPL.sem, pl_done(0, NP))
                        v.wait_ge(gm.sem, gm_st(0))
                    v.tensor_scalar(nibi[:, 0:D], pak[:, :], 4, 15,
                                    ALU.logical_shift_right, ALU.bitwise_and)
                    v.tensor_scalar(nibi[:, D:2 * D], pak[:, :], 15, None,
                                    ALU.bitwise_and)
                    v.tensor_copy(nib[:, 0:D], nibi[:, 0:D])
                    sDV.bump(v.tensor_copy(nib[:, D:2 * D], nibi[:, D:2 * D]), 1)
                    for k, (c0, c1) in enumerate(dv_slabs):
                        self._deq_slab(v, sDV, set_dv, c0, c1, whl, nib,
                                       sclh, scll, (sDV, dv_ext(ci)),
                                       gm_st(0) if ci == 1 else None)

            @b.gpsimd
            def _(g):
                for ci in range(2):
                    for k, (c0, c1) in enumerate(pl_slabs):
                        self._deq_slab(g, sPL, set_pl, c0, c1, whl, nib,
                                       sclh, scll, (sDV, dv_ext(ci)),
                                       gm_st(0) if ci == 1 else None)
                    g.wait_ge(sDV.sem, dv_done(ci, ND))
                    g.wait_ge(sPL.sem, pl_done(ci, NP))
                    base = ci * 256
                    gm.bump(g.dma_start(self.wdq_in[base:base + 256:2, :],
                                        whl[:, 0:D]), 16)
                    gm.bump(g.dma_start(self.wdq_in[base + 1:base + 256:2, :],
                                        whl[:, D:2 * D]), 16)
                g.wait_ge(gm.sem, gm.n)
                sCC.bump(g.collective_compute(
                    "AllGather", ALU.bypass,
                    replica_groups=[list(range(8))],
                    ins=[self.wdq_in.opt()],
                    outs=[self.wdq.opt()]), 1)
        ctx0.close()
        self.cc_wdq = sCC.n

    def ln_phase(self, name, src, dst, guards=(), src2=None, sum_dst=None):
        nc = self.nc
        qx, qw, gs, sEV, sDV = self.qx, self.qw, self.gs, self.sEV, self.sDV
        nt = self.TO // 128
        KI = 5 if src2 is not None else 4
        NSR = 2 if src2 is not None else 1
        ctx0 = ExitStack()
        sb = lambda n, s, d: ctx0.enter_context(nc.sbuf_tensor(n, s, d))
        xln = [sb(f"{name}_x{i}", [128, D], F32) for i in range(2)]
        xln2 = [sb(f"{name}_y{i}", [128, D], F32) for i in range(2)] if src2 is not None else None
        xsum = xln if src2 is None else [sb(f"{name}_u{i}", [128, D], F32) for i in range(2)]
        xa = [sb(f"{name}_a{i}", [128, D], BF16) for i in range(2)]
        stats = [sb(f"{name}_s{i}", [128, 48], F32) for i in range(2)]
        aggr = [sb(f"{name}_g{i}", [128, 2], F32) for i in range(2)]
        lnv = [sb(f"{name}_l{i}", [128, 1], F32) for i in range(2)]
        rstd = [sb(f"{name}_r{i}", [128, 1], F32) for i in range(2)]
        nmr = [sb(f"{name}_n{i}", [128, 1], F32) for i in range(2)]
        eb, db = sEV.n, sDV.n
        qxb = [qx[0].n, qx[1].n]
        qwb = [qw[0].n, qw[1].n]
        gsb = [gs[0].n, gs[1].n]
        xt = lambda i: qxb[i % 2] + 16 * (i // 2 + 1)
        x2t = lambda i: qwb[i % 2] + 16 * (i // 2 + 1)
        st = lambda i: gsb[i % 2] + 16 * NSR * (i // 2 + 1)
        d_sum = lambda i: db + KI * i + 1
        d_statsnop = lambda i: db + KI * i + KI - 3
        d_aggr = lambda i: db + KI * i + KI - 2
        d_nmrnop = lambda i: db + KI * i + KI - 1
        d_norm = lambda i: db + KI * i + KI
        with nc.Block(name) as b:
            @b.sync
            def _(sync):
                self.flush(sync, qx + (qw if src2 is not None else []))
                for gd in guards:
                    sync.wait_ge(gd[0].sem, gd[1])
                for i in range(nt):
                    if i >= 2:
                        sync.wait_ge(sDV.sem, d_norm(i - 2))
                    qx[i % 2].bump(sync.dma_start(
                        xln[i % 2][:, :], src[i * 128:(i + 1) * 128, :]), 16)
                    if src2 is not None:
                        qw[i % 2].bump(sync.dma_start(
                            xln2[i % 2][:, :], src2[i * 128:(i + 1) * 128, :]), 16)

            @b.vector
            def _(v):
                for i in range(nt):
                    p = i % 2
                    v.wait_ge(qx[p].sem, xt(i))
                    if src2 is not None:
                        v.wait_ge(qw[p].sem, x2t(i))
                        if i >= 2:
                            v.wait_ge(gs[p].sem, st(i - 2))
                        w = sDV.bump(v.tensor_tensor(xsum[p][:, :], xln[p][:, :],
                                                     xln2[p][:, :], ALU.add), 1)
                        v.wait_ge(sDV.sem, w)
                    for c in range(8):
                        v.bn_stats(stats[p][:, 6 * c:6 * c + 6],
                                   xsum[p][:, 512 * c:512 * (c + 1)])
                    w = sDV.bump(v.engine_nop(), 1)
                    v.wait_ge(sDV.sem, w)
                    sDV.bump(v.bn_aggr(aggr[p][:, :], stats[p][:, :]), 1)
                    v.wait_ge(sEV.sem, eb + 2 * i + 2)
                    v.scalar_tensor_tensor(nmr[p][:, :], aggr[p][:, 0:1], -1.0,
                                           rstd[p][:, 0:1], ALU.mult, ALU.mult)
                    w = sDV.bump(v.engine_nop(), 1)
                    v.wait_ge(sDV.sem, w)
                    if i >= 2:
                        v.wait_ge(gs[p].sem, st(i - 2))
                    sDV.bump(v.tensor_scalar(xa[p][:, :], xsum[p][:, :],
                                             rstd[p][:, 0:1], nmr[p][:, 0:1],
                                             ALU.mult, ALU.add), 1)

            @b.scalar
            def _(s):
                for i in range(nt):
                    p = i % 2
                    s.wait_ge(sDV.sem, d_aggr(i))
                    w = sEV.bump(s.activation(lnv[p][:, :], aggr[p][:, 1:2],
                                              AF.Ln, bias=EPS_F), 1)
                    s.wait_ge(sEV.sem, w)
                    sEV.bump(s.activation(rstd[p][:, :], lnv[p][:, :],
                                          AF.Exp, scale=np.float32(-0.5)), 1)

            @b.gpsimd
            def _(g):
                self.flush(g, gs)
                for i in range(nt):
                    g.wait_ge(sDV.sem, d_norm(i))
                    if src2 is not None:
                        gs[i % 2].bump(g.dma_start(
                            sum_dst[i * 128:(i + 1) * 128, :], xsum[i % 2][:, :]), 16)
                    gs[i % 2].bump(g.dma_start(
                        dst[i * 128:(i + 1) * 128, :], xa[i % 2][:, :]), 16)
        ctx0.close()
        self.last_ln_dve = sDV.n
        return [(gs[0], gs[0].n), (gs[1], gs[1].n)]

    def transpose_resident(self, name, src, resident, rows, src_ready):
        nc = self.nc
        am = self.am
        with nc.Block(name) as b:
            @b.scalar
            def _(s):
                self.flush(s, [am])
                for rd in src_ready:
                    s.wait_ge(rd[0].sem, rd[1])
                for kc in range(KC):
                    am.bump(s.dma_start(resident[:, kc * rows:(kc + 1) * rows],
                                        src[0:rows, kc * 128:(kc + 1) * 128],
                                        transpose=True), 16)
        return [(am, am.n)]

    def gemm_token(self, name, resident, res_ready, wsrc_kind, wsrc, resid_src,
                   out_dst, scale, sync_guard=(), rows=None, nkc=KC):
        nc = self.nc
        qx, qw, gw, asx = self.qx, self.qw, self.gw, self.asx
        sMM, sDV = self.sMM, self.sDV
        rows = rows if rows is not None else self.TO
        MC = rows // 128
        NS = D // 512
        ctx0 = ExitStack()
        sb = lambda n, s, d: ctx0.enter_context(nc.sbuf_tensor(n, s, d))
        wst = [sb(f"{name}_w{i}", [128, nkc * 512], BF16) for i in range(2)]
        rbuf = [sb(f"{name}_r{i}", [128, 512], F32) for i in range(2)] \
            if resid_src is not None else None
        obuf = [sb(f"{name}_o{i}", [128, 512], F32) for i in range(2)]
        ps = [ctx0.enter_context(nc.psum_tensor(f"{name}_p{i}", [128, 512], F32))
              for i in range(2)]
        mb, db = sMM.n, sDV.n
        qwb = [qw[0].n, qw[1].n]
        gwb = [gw[0].n, gw[1].n]
        qxb = [qx[0].n, qx[1].n]
        asb = [asx[0].n, asx[1].n]
        strip_t = lambda ns: (qwb if wsrc_kind == 'bf16' else gwb)[ns % 2] + 16 * (ns // 2 + 1)
        resid_t = lambda g: qxb[g % 2] + 16 * (g // 2 + 1)
        store_t = lambda g: asb[g % 2] + 16 * (g // 2 + 1)

        def strip_dma(eng, ns):
            return eng.dma_start(
                wst[ns % 2][:, :].rearrange("p (kc n) -> p kc n", kc=nkc),
                wsrc[:, ns * 512:(ns + 1) * 512].rearrange(
                    "(kc p) n -> p kc n", p=128))

        with nc.Block(name) as b:
            @b.sync
            def _(sync):
                self.flush(sync, qx + (qw if wsrc_kind == 'bf16' else []))
                for gd in sync_guard:
                    sync.wait_ge(gd[0].sem, gd[1])
                for ns in range(NS):
                    if wsrc_kind == 'bf16':
                        if ns >= 2:
                            sync.wait_ge(sMM.sem, mb + (ns - 1) * MC)
                        qw[ns % 2].bump(strip_dma(sync, ns), 16)
                    if resid_src is not None:
                        for m in range(MC):
                            g = ns * MC + m
                            if g >= 2:
                                sync.wait_ge(sDV.sem, db + g - 1)
                            qx[g % 2].bump(sync.dma_start(
                                rbuf[g % 2][:, :],
                                resid_src[m * 128:(m + 1) * 128,
                                          ns * 512:(ns + 1) * 512]), 16)

            if wsrc_kind == 'f32':
                @b.gpsimd
                def _(gp):
                    self.flush(gp, gw)
                    for ns in range(NS):
                        if ns >= 2:
                            gp.wait_ge(sMM.sem, mb + (ns - 1) * MC)
                        gw[ns % 2].bump(strip_dma(gp, ns), 16)

            @b.tensor
            def _(t):
                for rd in res_ready:
                    t.wait_ge(rd[0].sem, rd[1])
                for ns in range(NS):
                    sctr = (qw if wsrc_kind == 'bf16' else gw)[ns % 2]
                    t.wait_ge(sctr.sem, strip_t(ns))
                    for m in range(MC):
                        g = ns * MC + m
                        if g >= 2:
                            t.wait_ge(sDV.sem, db + g - 1)
                        for k in range(nkc):
                            inst = t.matmul(
                                ps[g % 2][:, :],
                                resident[:, k * rows + m * 128: k * rows + (m + 1) * 128],
                                wst[ns % 2][:, k * 512:(k + 1) * 512],
                                start=(k == 0), stop=(k == nkc - 1))
                        sMM.bump(inst, 1)

            @b.vector
            def _(v):
                for ns in range(NS):
                    for m in range(MC):
                        g = ns * MC + m
                        v.wait_ge(sMM.sem, mb + g + 1)
                        if resid_src is not None:
                            v.wait_ge(qx[g % 2].sem, resid_t(g))
                        if g >= 2:
                            v.wait_ge(asx[g % 2].sem, store_t(g - 2))
                        if resid_src is not None:
                            sDV.bump(v.scalar_tensor_tensor(
                                obuf[g % 2][:, :], ps[g % 2][:, :], scale,
                                rbuf[g % 2][:, :], ALU.mult, ALU.add), 1)
                        else:
                            sDV.bump(v.tensor_scalar(
                                obuf[g % 2][:, :], ps[g % 2][:, :], scale, None,
                                ALU.mult), 1)

            @b.scalar
            def _(s):
                self.flush(s, asx)
                for ns in range(NS):
                    for m in range(MC):
                        g = ns * MC + m
                        s.wait_ge(sDV.sem, db + g + 1)
                        asx[g % 2].bump(s.dma_start(
                            out_dst[m * 128:(m + 1) * 128, ns * 512:(ns + 1) * 512],
                            obuf[g % 2][:, :]), 16)
        ctx0.close()
        return [(asx[0], asx[0].n), (asx[1], asx[1].n)]

    def p2c_half(self, th, xbT, xbT_ready):
        nc = self.nc
        T, TH = self.T, self.T // 2
        FO, NH = self.FO, self.NH
        qm, gw, asx = self.qm, self.gw, self.asx
        sMM, sEV, sDV = self.sMM, self.sEV, self.sDV
        tc0 = th * TH

        jobs = [("k", hl) for hl in range(NH)] + [("q", hl) for hl in range(NH)]
        ctxb = ExitStack()
        sbb = lambda n, s, d: ctxb.enter_context(nc.sbuf_tensor(n, s, d))
        sinb = sbb(f"sin{th}", [128, TH], F32)
        cosb = sbb(f"cos{th}", [128, TH], F32)
        wst = [sbb(f"j{th}w{i}", [128, KC * HD], BF16) for i in range(2)]
        kqo = [sbb(f"j{th}o{i}", [128, TH], BF16) for i in range(2)]
        rt = [sbb(f"j{th}t{i}", [64, TH], F32) for i in range(4)]
        psK = [ctxb.enter_context(nc.psum_tensor(f"j{th}p{i}", [128, TH], F32))
               for i in range(2)]
        mb, db = sMM.n, sDV.n
        gwb = [gw[0].n, gw[1].n]
        asb = [asx[0].n, asx[1].n]
        strip_t = lambda i: gwb[i % 2] + 16 * (i // 2 + 1)
        store_t = lambda i: asb[i % 2] + 16 * (i // 2 + 1)

        with nc.Block(f"p2b{th}") as b:
            @b.sync
            def _(sync):
                self.flush(sync, [qm])
                qm.bump(sync.dma_start(sinb[:, :],
                                       self.sinTd[:, tc0:tc0 + TH]), 16)
                qm.bump(sync.dma_start(cosb[:, :],
                                       self.cosTd[:, tc0:tc0 + TH]), 16)
            sin_ready = [(qm, qm.n)]

            @b.gpsimd
            def _(gp):
                self.flush(gp, gw)
                for i, (kind, hl) in enumerate(jobs):
                    wsrc = self.wkT if kind == "k" else self.wqT
                    if i >= 2:
                        gp.wait_ge(sMM.sem, mb + i - 1)
                    gw[i % 2].bump(gp.dma_start(
                        wst[i % 2][:, :].rearrange("p (kc n) -> p kc n", kc=KC),
                        wsrc[:, hl * HD:(hl + 1) * HD].rearrange(
                            "(kc p) n -> p kc n", p=128)), 16)

            @b.tensor
            def _(t):
                for rd in xbT_ready:
                    t.wait_ge(rd[0].sem, rd[1])
                for i, (kind, hl) in enumerate(jobs):
                    t.wait_ge(gw[i % 2].sem, strip_t(i))
                    if i >= 2:
                        t.wait_ge(sDV.sem, db + i - 1)
                    for k in range(KC):
                        for nb in range(TH // 512):
                            inst = t.matmul(
                                psK[i % 2][:, nb * 512:(nb + 1) * 512],
                                wst[i % 2][:, k * HD:(k + 1) * HD],
                                xbT[:, k * TH + nb * 512: k * TH + (nb + 1) * 512],
                                start=(k == 0), stop=(k == KC - 1))
                    sMM.bump(inst, 1)

            @b.vector
            def _(v):
                for rd in sin_ready:
                    v.wait_ge(rd[0].sem, rd[1])
                for i, (kind, hl) in enumerate(jobs):
                    p = i % 2
                    v.wait_ge(sMM.sem, mb + i + 1)
                    if i >= 2:
                        v.wait_ge(asx[p].sem, store_t(i - 2))
                    pk = psK[p]
                    v.tensor_tensor(rt[0][:, :], pk[64:128, :], sinb[0:64, :], ALU.mult)
                    v.tensor_tensor(rt[1][:, :], pk[0:64, :], cosb[0:64, :], ALU.mult)
                    v.tensor_tensor(rt[2][:, :], pk[0:64, :], sinb[64:128, :], ALU.mult)
                    v.tensor_tensor(rt[3][:, :], pk[64:128, :], cosb[64:128, :], ALU.mult)
                    v.tensor_tensor(kqo[p][0:64, :], rt[1][:, :], rt[0][:, :], ALU.subtract)
                    sDV.bump(v.tensor_tensor(kqo[p][64:128, :], rt[3][:, :],
                                             rt[2][:, :], ALU.add), 1)

            @b.scalar
            def _(s):
                self.flush(s, asx)
                for i, (kind, hl) in enumerate(jobs):
                    s.wait_ge(sDV.sem, db + i + 1)
                    dst = self.kts if kind == "k" else self.qts
                    asx[i % 2].bump(s.dma_start(
                        dst[hl * HD:(hl + 1) * HD, tc0:tc0 + TH],
                        kqo[i % 2][:, :]), 16)
        ctxb.close()

        ctxc = ExitStack()
        sbc = lambda n, s, d: ctxc.enter_context(nc.sbuf_tensor(n, s, d))
        vw = [sbc(f"v{th}w{i}", [128, KC * 512], BF16) for i in range(2)]
        vo = [sbc(f"v{th}o{i}", [128, 512], BF16) for i in range(2)]
        psV = [ctxc.enter_context(nc.psum_tensor(f"v{th}p{i}", [128, 512], F32))
               for i in range(2)]
        NSv = FO // 512
        MCv = TH // 128
        mb2, eb2 = sMM.n, sEV.n
        gwb2 = [gw[0].n, gw[1].n]
        asb2 = [asx[0].n, asx[1].n]
        vstrip_t = lambda ns: gwb2[ns % 2] + 16 * (ns // 2 + 1)
        vstore_t = lambda g: asb2[g % 2] + 16 * (g // 2 + 1)
        with nc.Block(f"p2v{th}") as b:
            @b.gpsimd
            def _(gp):
                self.flush(gp, gw)
                for ns in range(NSv):
                    if ns >= 2:
                        gp.wait_ge(sMM.sem, mb2 + (ns - 1) * MCv)
                    gw[ns % 2].bump(gp.dma_start(
                        vw[ns % 2][:, :].rearrange("p (kc n) -> p kc n", kc=KC),
                        self.wvT[:, ns * 512:(ns + 1) * 512].rearrange(
                            "(kc p) n -> p kc n", p=128)), 16)

            @b.tensor
            def _(t):
                for ns in range(NSv):
                    t.wait_ge(gw[ns % 2].sem, vstrip_t(ns))
                    for m in range(MCv):
                        g = ns * MCv + m
                        if g >= 2:
                            t.wait_ge(sEV.sem, eb2 + g - 1)
                        for k in range(KC):
                            inst = t.matmul(
                                psV[g % 2][:, :],
                                xbT[:, k * TH + m * 128: k * TH + (m + 1) * 128],
                                vw[ns % 2][:, k * 512:(k + 1) * 512],
                                start=(k == 0), stop=(k == KC - 1))
                        sMM.bump(inst, 1)

            @b.scalar
            def _(s):
                self.flush(s, asx)
                for ns in range(NSv):
                    for m in range(MCv):
                        g = ns * MCv + m
                        s.wait_ge(sMM.sem, mb2 + g + 1)
                        if g >= 2:
                            s.wait_ge(asx[g % 2].sem, vstore_t(g - 2))
                        w = sEV.bump(s.copy(vo[g % 2][:, :], psV[g % 2][:, :]), 1)
                        s.wait_ge(sEV.sem, w)
                        asx[g % 2].bump(s.dma_start(
                            self.vts[tc0 + m * 128:tc0 + (m + 1) * 128,
                                     ns * 512:(ns + 1) * 512],
                            vo[g % 2][:, :]), 16)
        ctxc.close()

    def p3_attention(self, max_groups=99, max_qt=99):
        nc = self.nc
        T, NH, FO = self.T, self.NH, self.FO
        NQ = min(self.NQ, max_qt)
        qm, qv, asx = self.qm, self.qv, self.asx
        sMM, sEV, sDV, sPL = self.sMM, self.sEV, self.sDV, self.sPL
        groups = [list(range(0, 6)), list(range(6, 12)), list(range(12, 16))]
        groups = groups[:max_groups]
        JPQ = QTW // 128
        ctx0 = ExitStack()
        sb = lambda n, s, d: ctx0.enter_context(nc.sbuf_tensor(n, s, d))
        Ksb = [sb(f"at_k{i}", [128, T], BF16) for i in range(6)]
        Qsb = [sb(f"at_q{i}", [128, 512], BF16) for i in range(6)]
        Vsb = [sb(f"at_v{i}", [128, FO], BF16) for i in range(3)]
        Esb = [sb(f"at_e{i}", [128, 512], BF16) for i in range(6)]
        Zac = [[sb(f"at_z{i}_{p}", [128, 512], F32) for p in range(2)] for i in range(6)]
        Zfin = [sb(f"at_zf{i}", [128, 512], F32) for i in range(6)]
        Qsq = [sb(f"at_qs{i}", [128, 512], F32) for i in range(2)]
        Ksq = [sb(f"at_ks{i}", [128, 512], F32) for i in range(2)]
        qnr = [sb(f"at_qn{i}", [1, 512], F32) for i in range(6)]
        knr = [sb(f"at_kn{i}", [1, 512], F32) for i in range(6)]
        mq2 = sb("at_mq2", [1, 6], F32)
        mk2 = sb("at_mk2", [1, 6], F32)
        knt = sb("at_knt", [1, 6], F32)
        sumb = sb("at_sumb", [1, 6], F32)
        negbf = sb("at_negbf", [1, 6], F32)
        negbc = [sb(f"at_nb{i}", [128, 1], F32) for i in range(6)]
        zrow = [sb(f"at_zr{i}", [1, 512], F32) for i in range(6)]
        rzrow = [sb(f"at_rz{i}", [1, 512], F32) for i in range(6)]
        rzbc = [sb(f"at_rb{i}", [128, 512], F32) for i in range(6)]
        aout = [sb(f"at_ao{i}", [128, 512], BF16) for i in range(2)]
        psO = [ctx0.enter_context(nc.psum_tensor(f"at_pO{i}", [128, 512], F32))
               for i in range(6)]
        psS = [ctx0.enter_context(nc.psum_tensor(f"at_pS{i}", [128, 512], F32))
               for i in range(2)]

        def round_seq(H):
            seq = []
            for sl in range(H):
                seq.append(("S", sl))
                if sl >= 2:
                    seq.append(("PV", sl - 2))
            seq.append(("PV", H - 2))
            seq.append(("PV", H - 1))
            return seq

        zeng = lambda slot: "dve" if slot % 2 == 0 else "pl"

        C = {}
        q = {"pe": sMM.n, "act": sEV.n, "dve": sDV.n, "pl": sPL.n,
             "qm": qm.n, "qv": [qv[0].n, qv[1].n, qv[2].n],
             "as": [asx[0].n, asx[1].n]}
        last_sto = {0: None, 1: None}
        all_rounds = []
        for gi, hg in enumerate(groups):
            H = len(hg)
            for qt in range(NQ):
                J = (qt + 1) * JPQ
                q["qm"] += 16 * 2 * H
                C[("kq", gi, qt)] = q["qm"]
                for jc in range(J):
                    sl3 = jc % 3
                    q["qv"][sl3] += 16
                    C[("ldv", gi, qt, jc)] = (sl3, q["qv"][sl3])
                    all_rounds.append((gi, qt, jc, H))
                for sl in range(H):
                    q["pe"] += 1
                    C[("qn", gi, qt, sl)] = q["pe"]
                    q["pe"] += 1
                    C[("kn", gi, qt, sl)] = q["pe"]
                for sl in range(H):
                    q["pe"] += 1
                    C[("nb", gi, qt, sl)] = q["pe"]
                for jc in range(J):
                    for kind, sl in round_seq(H):
                        q["pe"] += 1
                        C[(kind, gi, qt, jc, sl)] = q["pe"]
                for sl in range(H):
                    q["pe"] += 1
                    C[("zmm", gi, qt, sl)] = q["pe"]
                for sl in range(H):
                    q["pe"] += 1
                    C[("rzb", gi, qt, sl)] = q["pe"]
                for sl in range(H):
                    q["act"] += 1
                    C[("eqn", gi, qt, sl)] = q["act"]
                    q["act"] += 1
                    C[("ekn", gi, qt, sl)] = q["act"]
                for sl in range(H):
                    q["act"] += 1
                    C[("enb", gi, qt, sl)] = q["act"]
                for jc in range(J):
                    for sl in range(H):
                        q["act"] += 1
                        C[("exp", gi, qt, jc, sl)] = q["act"]
                for sl in range(H):
                    q["act"] += 1
                    C[("ezr", gi, qt, sl)] = q["act"]
                for sl in range(H):
                    q["act"] += 1
                    C[("erz", gi, qt, sl)] = q["act"]
                for sl in range(H):
                    q["dve"] += 1
                    C[("dqs", gi, qt, sl)] = q["dve"]
                    q["dve"] += 1
                    C[("dks", gi, qt, sl)] = q["dve"]
                for sl in range(H):
                    q["dve"] += 1
                    C[("dnb", gi, qt, sl)] = q["dve"]
                for jc in range(J):
                    diag = jc >= qt * JPQ
                    for sl in range(H):
                        if diag:
                            q["dve"] += 1
                            C[("pm", gi, qt, jc, sl)] = q["dve"]
                        e = zeng(sl)
                        q[e] += 1
                        C[("za", gi, qt, jc, sl)] = q[e]
                for sl in range(H):
                    e = zeng(sl)
                    q[e] += 1
                    C[("zf", gi, qt, sl)] = q[e]
                for sl in range(H):
                    q["dve"] += 1
                    C[("rcp", gi, qt, sl)] = q["dve"]
                for sl in range(H):
                    q["dve"] += 1
                    C[("emu", gi, qt, sl)] = q["dve"]
                for sl in range(H):
                    p = sl % 2
                    q["as"][p] += 16
                    C[("sto", gi, qt, sl)] = (p, q["as"][p])
                for sl in range(H):
                    C[("aoutw", gi, qt, sl)] = last_sto[sl % 2]
                    last_sto[sl % 2] = C[("sto", gi, qt, sl)]

        vprev = {}
        by_slot = {0: None, 1: None, 2: None}
        for (gi, qt, jc, H) in all_rounds:
            vprev[(gi, qt, jc)] = by_slot[jc % 3]
            by_slot[jc % 3] = (gi, qt, jc, H)

        sem_of = {"pe": sMM, "act": sEV, "dve": sDV, "pl": sPL}

        with nc.Block("p3") as b:
            @b.sync
            def _(sync):
                self.flush(sync, [qm] + qv)
                for gd in self.p2_stores_final:
                    sync.wait_ge(gd[0].sem, gd[1])
                for gi, hg in enumerate(groups):
                    H = len(hg)
                    for qt in range(NQ):
                        J = (qt + 1) * JPQ
                        keys0 = qt * QTW
                        if qt >= 1:
                            sync.wait_ge(sMM.sem,
                                         C[("PV", gi, qt - 1, qt * JPQ - 1, H - 1)])
                        elif gi >= 1:
                            Hp = len(groups[gi - 1])
                            sync.wait_ge(sMM.sem,
                                         C[("rzb", gi - 1, NQ - 1, Hp - 1)])
                        for sl in range(H):
                            hl = hg[sl]
                            qm.bump(sync.dma_start(
                                Ksb[sl][:, keys0:keys0 + QTW],
                                self.kts[hl * HD:(hl + 1) * HD, keys0:keys0 + QTW]
                            ), 16)
                            qm.bump(sync.dma_start(
                                Qsb[sl][:, :],
                                self.qts[hl * HD:(hl + 1) * HD, keys0:keys0 + QTW]
                            ), 16)
                        assert qm.n == C[("kq", gi, qt)]
                        for jc in range(J):
                            pv = vprev[(gi, qt, jc)]
                            if pv is not None:
                                pgi, pqt, pjc, pH = pv
                                sync.wait_ge(sMM.sem,
                                             C[("PV", pgi, pqt, pjc, pH - 1)])
                            sl3, tgt = C[("ldv", gi, qt, jc)]
                            qv[sl3].bump(sync.dma_start(
                                Vsb[sl3][:, :],
                                self.vts[jc * 128:(jc + 1) * 128, :]), 16)
                            assert qv[sl3].n == tgt

            @b.tensor
            def _(t):
                psS_last_evac = [(None, 0), (None, 0)]
                psO_free = [None] * 6
                for gi, hg in enumerate(groups):
                    H = len(hg)
                    for qt in range(NQ):
                        J = (qt + 1) * JPQ
                        for sl in range(H):
                            t.wait_ge(sDV.sem, C[("dqs", gi, qt, sl)])
                            if psO_free[sl] is not None:
                                e, cval = psO_free[sl]
                                t.wait_ge(sem_of[e].sem, cval)
                            t.matmul(psO[sl][0:1, :], self.ones_col[:, :],
                                     Qsq[sl % 2][:, :], start=True, stop=True
                                     ).then_inc(sMM.sem, 1)
                            t.wait_ge(sDV.sem, C[("dks", gi, qt, sl)])
                            t.matmul(psO[sl][64:65, :], self.ones_col[:, :],
                                     Ksq[sl % 2][:, :], start=True, stop=True
                                     ).then_inc(sMM.sem, 1)
                        for sl in range(H):
                            t.wait_ge(sEV.sem, C[("ekn", gi, qt, sl)])
                            t.wait_ge(sDV.sem, C[("dnb", gi, qt, sl)])
                            t.matmul(psO[sl][:, 0:1], self.ones_row[:, :],
                                     negbf[0:1, sl:sl + 1], start=True, stop=True
                                     ).then_inc(sMM.sem, 1)
                        for jc in range(J):
                            for kind, sl in round_seq(H):
                                if kind == "S":
                                    e, cval = psS_last_evac[sl % 2]
                                    if e is not None:
                                        t.wait_ge(sem_of[e].sem, cval)
                                    t.wait_ge(qm.sem, C[("kq", gi, qt)])
                                    t.matmul(psS[sl % 2][:, :],
                                             Ksb[sl][:, jc * 128:(jc + 1) * 128],
                                             Qsb[sl][:, :], start=True, stop=True
                                             ).then_inc(sMM.sem, 1)
                                    psS_last_evac[sl % 2] = (
                                        "act", C[("exp", gi, qt, jc, sl)])
                                else:
                                    t.wait_ge(sEV.sem, C[("exp", gi, qt, jc, sl)])
                                    if jc == 0:
                                        t.wait_ge(sEV.sem, C[("enb", gi, qt, sl)])
                                    sl3, tgt = C[("ldv", gi, qt, jc)]
                                    t.wait_ge(qv[sl3].sem, tgt)
                                    hl = hg[sl]
                                    t.matmul(psO[sl][:, :],
                                             Vsb[sl3][:, hl * HD:(hl + 1) * HD],
                                             Esb[sl][:, :],
                                             start=(jc == 0), stop=(jc == J - 1)
                                             ).then_inc(sMM.sem, 1)
                        for sl in range(H):
                            e = zeng(sl)
                            t.wait_ge(sem_of[e].sem, C[("zf", gi, qt, sl)])
                            ep, cval = psS_last_evac[sl % 2]
                            if ep is not None:
                                t.wait_ge(sem_of[ep].sem, cval)
                            t.matmul(psS[sl % 2][0:1, :], self.ones_col[:, :],
                                     Zfin[sl][:, :], start=True, stop=True
                                     ).then_inc(sMM.sem, 1)
                            psS_last_evac[sl % 2] = ("act", C[("ezr", gi, qt, sl)])
                        for sl in range(H):
                            t.wait_ge(sDV.sem, C[("rcp", gi, qt, sl)])
                            ep, cval = psS_last_evac[sl % 2]
                            t.wait_ge(sem_of[ep].sem, cval)
                            t.matmul(psS[sl % 2][:, :], self.ones_row[:, :],
                                     rzrow[sl][0:1, :], start=True, stop=True
                                     ).then_inc(sMM.sem, 1)
                            psS_last_evac[sl % 2] = ("act", C[("erz", gi, qt, sl)])
                        for sl in range(H):
                            psO_free[sl] = ("dve", C[("emu", gi, qt, sl)])
                sMM.n = q["pe"]

            @b.scalar
            def _(s):
                self.flush(s, asx)
                for gi, hg in enumerate(groups):
                    H = len(hg)
                    for qt in range(NQ):
                        J = (qt + 1) * JPQ
                        for sl in range(H):
                            s.wait_ge(sMM.sem, C[("kn", gi, qt, sl)])
                            s.copy(qnr[sl][:, :], psO[sl][0:1, :]
                                   ).then_inc(sEV.sem, 1)
                            s.copy(knr[sl][:, :], psO[sl][64:65, :]
                                   ).then_inc(sEV.sem, 1)
                        for sl in range(H):
                            s.wait_ge(sMM.sem, C[("nb", gi, qt, sl)])
                            s.copy(negbc[sl][:, :], psO[sl][:, 0:1]
                                   ).then_inc(sEV.sem, 1)
                        for jc in range(J):
                            diag = jc >= qt * JPQ
                            for sl in range(H):
                                s.wait_ge(sMM.sem, C[("S", gi, qt, jc, sl)])
                                if diag:
                                    s.wait_ge(sDV.sem, C[("pm", gi, qt, jc, sl)])
                                if jc >= 1:
                                    s.wait_ge(sMM.sem, C[("PV", gi, qt, jc - 1, sl)])
                                    e = zeng(sl)
                                    s.wait_ge(sem_of[e].sem,
                                              C[("za", gi, qt, jc - 1, sl)])
                                elif qt >= 1:
                                    Jp = qt * JPQ
                                    s.wait_ge(sMM.sem, C[("PV", gi, qt - 1, Jp - 1, sl)])
                                    e = zeng(sl)
                                    s.wait_ge(sem_of[e].sem,
                                              C[("za", gi, qt - 1, Jp - 1, sl)])
                                elif gi >= 1 and sl < len(groups[gi - 1]):
                                    Jp = NQ * JPQ
                                    s.wait_ge(sMM.sem,
                                              C[("PV", gi - 1, NQ - 1, Jp - 1, sl)])
                                    e = zeng(sl)
                                    s.wait_ge(sem_of[e].sem,
                                              C[("za", gi - 1, NQ - 1, Jp - 1, sl)])
                                s.activation(Esb[sl][:, :], psS[sl % 2][:, :],
                                             AF.Exp, bias=negbc[sl][:, 0:1],
                                             scale=INV_SQRT_HD).then_inc(sEV.sem, 1)
                        for sl in range(H):
                            s.wait_ge(sMM.sem, C[("zmm", gi, qt, sl)])
                            s.copy(zrow[sl][:, :], psS[sl % 2][0:1, :]
                                   ).then_inc(sEV.sem, 1)
                        for sl in range(H):
                            s.wait_ge(sMM.sem, C[("rzb", gi, qt, sl)])
                            s.copy(rzbc[sl][:, :], psS[sl % 2][:, :]
                                   ).then_inc(sEV.sem, 1)
                        for sl in range(H):
                            hl = hg[sl]
                            s.wait_ge(sDV.sem, C[("emu", gi, qt, sl)])
                            p, tgt = C[("sto", gi, qt, sl)]
                            asx[p].bump(s.dma_start(
                                self.atns[hl * HD:(hl + 1) * HD,
                                          qt * QTW:(qt + 1) * QTW],
                                aout[sl % 2][:, :]), 16)
                            assert asx[p].n == tgt
                sEV.n = q["act"]

            @b.vector
            def _(v):
                for gi, hg in enumerate(groups):
                    H = len(hg)
                    for qt in range(NQ):
                        J = (qt + 1) * JPQ
                        keys0 = qt * QTW
                        for sl in range(H):
                            v.wait_ge(qm.sem, C[("kq", gi, qt)])
                            if sl >= 2:
                                v.wait_ge(sMM.sem, C[("qn", gi, qt, sl - 2)])
                            v.tensor_tensor(Qsq[sl % 2][:, :], Qsb[sl][:, :],
                                            Qsb[sl][:, :], ALU.mult
                                            ).then_inc(sDV.sem, 1)
                            if sl >= 2:
                                v.wait_ge(sMM.sem, C[("kn", gi, qt, sl - 2)])
                            v.tensor_tensor(Ksq[sl % 2][:, :],
                                            Ksb[sl][:, keys0:keys0 + QTW],
                                            Ksb[sl][:, keys0:keys0 + QTW], ALU.mult
                                            ).then_inc(sDV.sem, 1)
                        for sl in range(H):
                            v.wait_ge(sEV.sem, C[("eqn", gi, qt, sl)])
                            v.reduce_max(mq2[0:1, sl:sl + 1], qnr[sl][:, :], AX)
                            v.wait_ge(sEV.sem, C[("ekn", gi, qt, sl)])
                            v.reduce_max(knt[0:1, sl:sl + 1], knr[sl][:, :], AX)
                        for sl in range(H):
                            if qt == 0:
                                v.tensor_copy(mk2[0:1, sl:sl + 1],
                                              knt[0:1, sl:sl + 1])
                            else:
                                v.tensor_tensor(mk2[0:1, sl:sl + 1],
                                                mk2[0:1, sl:sl + 1],
                                                knt[0:1, sl:sl + 1], ALU.max)
                        for sl in range(H):
                            v.tensor_tensor(sumb[0:1, sl:sl + 1],
                                            mq2[0:1, sl:sl + 1],
                                            mk2[0:1, sl:sl + 1], ALU.add)
                        for sl in range(H):
                            v.tensor_scalar(negbf[0:1, sl:sl + 1],
                                            sumb[0:1, sl:sl + 1], C_BOUND, None,
                                            ALU.mult).then_inc(sDV.sem, 1)
                        for jc in range(J):
                            diag = jc >= qt * JPQ
                            for sl in range(H):
                                if diag:
                                    dlt = jc * 128 - qt * QTW
                                    v.wait_ge(sMM.sem, C[("S", gi, qt, jc, sl)])
                                    v.tensor_tensor(
                                        psS[sl % 2][:, :], psS[sl % 2][:, :],
                                        self.mask_sb[:, 384 - dlt:896 - dlt],
                                        ALU.add).then_inc(sDV.sem, 1)
                                if zeng(sl) == "dve":
                                    v.wait_ge(sEV.sem, C[("exp", gi, qt, jc, sl)])
                                    if jc < 2:
                                        v.tensor_copy(Zac[sl][jc % 2][:, :],
                                                      Esb[sl][:, :]
                                                      ).then_inc(sDV.sem, 1)
                                    else:
                                        v.tensor_tensor(Zac[sl][jc % 2][:, :],
                                                        Esb[sl][:, :],
                                                        Zac[sl][jc % 2][:, :],
                                                        ALU.add).then_inc(sDV.sem, 1)
                        for sl in range(H):
                            if zeng(sl) == "dve":
                                v.tensor_tensor(Zfin[sl][:, :], Zac[sl][0][:, :],
                                                Zac[sl][1][:, :], ALU.add
                                                ).then_inc(sDV.sem, 1)
                        for sl in range(H):
                            v.wait_ge(sEV.sem, C[("ezr", gi, qt, sl)])
                            v.reciprocal(rzrow[sl][:, :], zrow[sl][:, :]
                                         ).then_inc(sDV.sem, 1)
                        for sl in range(H):
                            v.wait_ge(sEV.sem, C[("erz", gi, qt, sl)])
                            v.wait_ge(sMM.sem, C[("PV", gi, qt, J - 1, sl)])
                            if C[("aoutw", gi, qt, sl)] is not None:
                                p, tgt = C[("aoutw", gi, qt, sl)]
                                v.wait_ge(asx[p].sem, tgt)
                            v.tensor_tensor(aout[sl % 2][:, :], psO[sl][:, :],
                                            rzbc[sl][:, :], ALU.mult
                                            ).then_inc(sDV.sem, 1)
                sDV.n = q["dve"]

            @b.gpsimd
            def _(g):
                for gi, hg in enumerate(groups):
                    H = len(hg)
                    for qt in range(NQ):
                        J = (qt + 1) * JPQ
                        for jc in range(J):
                            for sl in range(H):
                                if zeng(sl) != "pl":
                                    continue
                                g.wait_ge(sEV.sem, C[("exp", gi, qt, jc, sl)])
                                if jc < 2:
                                    g.tensor_copy(Zac[sl][jc % 2][:, :],
                                                  Esb[sl][:, :]).then_inc(sPL.sem, 1)
                                else:
                                    g.tensor_tensor(Zac[sl][jc % 2][:, :],
                                                    Zac[sl][jc % 2][:, :],
                                                    Esb[sl][:, :], ALU.add
                                                    ).then_inc(sPL.sem, 1)
                        for sl in range(H):
                            if zeng(sl) == "pl":
                                g.tensor_tensor(Zfin[sl][:, :], Zac[sl][0][:, :],
                                                Zac[sl][1][:, :], ALU.add
                                                ).then_inc(sPL.sem, 1)
                sPL.n = q["pl"]
        ctx0.close()
        self.p3_stores_final = [(asx[0], asx[0].n), (asx[1], asx[1].n)]

    def build(self, upto=99):
        nc = self.nc
        T, TO, FO = self.T, self.TO, self.FO
        qm, gm, sCC = self.qm, self.gm, self.sCC

        self.prologue()
        if upto >= 1:
            self.p0_dequant()
        else:
            self.cc_wdq = 0
        if upto < 2:
            return self.finish(None)

        ln1_done = self.ln_phase("ln1", self.xin, self.xabs)
        ctxr = ExitStack()
        resident = ctxr.enter_context(nc.sbuf_tensor("res1", [128, KC * TO], BF16))
        rr = self.transpose_resident("tr1", self.xabs, resident, TO, ln1_done)
        with nc.Block("p1wait") as b:
            @b.sync
            def _(sync):
                sync.wait_ge(sCC.sem, self.cc_wdq)
        p1_done = self.gemm_token("p1", resident, rr, 'bf16', self.wdq,
                                  self.xin, self.x1s, 0.5,
                                  sync_guard=[(self.sDV, self.last_ln_dve)])
        ctxr.close()
        if upto < 3:
            return self.finish(p1_done)

        ln2_done = self.ln_phase("ln2", self.x1s, self.xbo, guards=p1_done)
        with nc.Block("p2x") as b:
            @b.gpsimd
            def _(g):
                for gd in ln2_done:
                    g.wait_ge(gd[0].sem, gd[1])
                sCC.bump(g.collective_compute(
                    "AllGather", ALU.bypass,
                    replica_groups=[[0, 1], [2, 3], [4, 5], [6, 7]],
                    ins=[self.xbo.opt()],
                    outs=[self.xbf.opt()]), 1)
        cc_xbf = sCC.n
        if upto < 4:
            return self.finish(None)

        for th in range(2):
            ctxh = ExitStack()
            xbT = ctxh.enter_context(
                nc.sbuf_tensor(f"xbT{th}", [128, KC * (T // 2)], BF16))
            TH = T // 2
            xbT_ready = self.transpose_resident(
                f"tr2{th}", self.xbf[th * TH:(th + 1) * TH, :],
                xbT, TH, [(sCC, cc_xbf)])
            self.p2c_half(th, xbT, xbT_ready)
            ctxh.close()
        self.p2_stores_final = [(self.asx[0], self.asx[0].n),
                                (self.asx[1], self.asx[1].n)]
        if upto < 5:
            return self.finish(None)

        self.p3_attention(max_groups=getattr(self, 'p3_groups', 99),
                          max_qt=getattr(self, 'p3_qt', 99))
        if upto < 6:
            return self.finish(self.p3_stores_final)

        ctxr4 = ExitStack()
        res4 = ctxr4.enter_context(
            nc.sbuf_tensor("res4", [128, (FO // 128) * T], BF16))
        with nc.Block("p4r") as b:
            @b.sync
            def _(sync):
                self.flush(sync, [qm])
                for gd in self.p3_stores_final:
                    sync.wait_ge(gd[0].sem, gd[1])
                for kc in range(FO // 128):
                    qm.bump(sync.dma_start(
                        res4[:, kc * T:(kc + 1) * T],
                        self.atns[kc * 128:(kc + 1) * 128, :]), 16)
        p4_done = self.gemm_token("p4", res4, [(qm, qm.n)], 'f32', self.woT,
                                  None, self.op4, 1.0, rows=T, nkc=FO // 128)
        ctxr4.close()
        with nc.Block("p4s") as b:
            @b.gpsimd
            def _(g):
                for gd in p4_done:
                    g.wait_ge(gd[0].sem, gd[1])
                sCC.bump(g.collective_compute(
                    "ReduceScatter", ALU.add,
                    replica_groups=[[0, 1], [2, 3], [4, 5], [6, 7]],
                    ins=[self.op4.opt()],
                    outs=[self.x2p.opt()]), 1)
        cc_rs = sCC.n
        if upto < 7:
            return self.finish(None)

        ln3_done = self.ln_phase("ln3", self.x2p, self.xabs,
                                 guards=[(sCC, cc_rs)],
                                 src2=self.x1s, sum_dst=self.x2s)
        ctxr5 = ExitStack()
        res5 = ctxr5.enter_context(nc.sbuf_tensor("res5", [128, KC * TO], BF16))
        rr5 = self.transpose_resident("tr5", self.xabs, res5, TO, ln3_done)
        p5_done = self.gemm_token("p5", res5, rr5, 'bf16', self.wdq,
                                  self.x2s, self.xout, 0.5,
                                  sync_guard=[(self.sDV, self.last_ln_dve)] + ln3_done)
        ctxr5.close()
        return self.finish(p5_done)

    def finish(self, out_done):
        nc = self.nc
        gm = self.gm
        with nc.Block("fin") as b:
            @b.scalar
            def _(s):
                if out_done is not None:
                    for gd in out_done:
                        s.wait_ge(gd[0].sem, gd[1])

            if self.dbg_views:
                @b.gpsimd
                def _(g):
                    if out_done is not None:
                        for gd in out_done:
                            g.wait_ge(gd[0].sem, gd[1])
                    srcs = {"wdq": self.wdq, "xbf": self.xbf, "x1s": self.x1s,
                            "x2s": self.x2s, "kts": self.kts, "qts": self.qts,
                            "vts": self.vts, "atns": self.atns}
                    for nm, dst in self.dbg_views.items():
                        gm.bump(g.dma_start(dst[:, :], srcs[nm][:, :]), 16)
                    g.wait_ge(gm.sem, gm.n)
        self.ctx.close()
        return nc


def build_program(T, dbg=False):
    return Builder(T, dbg)


def host_prep(L0_attn_x, L0_attn_wq, L0_attn_wk, L0_attn_wv, L0_attn_wo,
              L0_ffn_weight_packed, L0_ffn_centroids, L0_ffn_scales):
    T = L0_attn_x.shape[1]
    TO = T // 2
    x = np.ascontiguousarray(np.asarray(L0_attn_x))
    wqT = np.ascontiguousarray(np.asarray(L0_attn_wq).T)
    wkT = np.ascontiguousarray(np.asarray(L0_attn_wk).T)
    wvT = np.ascontiguousarray(np.asarray(L0_attn_wv).T)
    woT = np.ascontiguousarray(np.asarray(L0_attn_wo).T)
    pakT = np.ascontiguousarray(np.asarray(L0_ffn_weight_packed).T)
    scales = np.asarray(L0_ffn_scales)
    cent = np.asarray(L0_ffn_centroids).reshape(1, 16).astype(np.float32)
    sinT, cosT, maskneg = host_tables(T)

    in_maps = []
    for c in range(8):
        s, par = c // 2, c % 2
        fsl = np.arange(c * (D // 8), (c + 1) * (D // 8))
        sclx = np.ascontiguousarray(scales.T[fsl // 64, :])
        hb = par * (NHG // 2)
        hc = slice(hb * HD, (hb + NHG // 2) * HD)
        in_maps.append({
            "xin": np.ascontiguousarray(x[s, par * TO:(par + 1) * TO, :]),
            "wkT": np.ascontiguousarray(wkT[:, hc]),
            "wvT": np.ascontiguousarray(wvT[:, hc]),
            "wqT": np.ascontiguousarray(wqT[:, hc]),
            "woT": np.ascontiguousarray(woT[hb * HD:(hb + NHG // 2) * HD, :]),
            "pakT": np.ascontiguousarray(pakT[c * (D // 16):(c + 1) * (D // 16), :]),
            "sclx": sclx,
            "cent": cent,
            "sinT": sinT,
            "cosT": cosT,
            "maskneg": maskneg,
            "cinfo": np.array([[par * TO, par, c, s, 0, 0, 0, 0]], np.int32),
        })
    return T, in_maps


def kernel(L0_attn_x, L0_attn_wq, L0_attn_wk, L0_attn_wv, L0_attn_wo,
           L0_ffn_weight_packed, L0_ffn_centroids, L0_ffn_scales):
    T, in_maps = host_prep(L0_attn_x, L0_attn_wq, L0_attn_wk, L0_attn_wv,
                           L0_attn_wo, L0_ffn_weight_packed, L0_ffn_centroids,
                           L0_ffn_scales)
    TO = T // 2
    bld = build_program(T, dbg=False)
    nc = bld.build()
    res = run_bass_kernel_spmd(nc, in_maps, core_ids=list(range(8)))
    out = np.empty((4, T, D), np.float32)
    for c in range(8):
        s, par = c // 2, c % 2
        out[s, par * TO:(par + 1) * TO, :] = res.results[c]["xout"]
    return out


# version 03
# speedup vs baseline: 1.0275x; 1.0275x over previous
import sys
if '/opt/trn_rl_repo' not in sys.path:
    sys.path.insert(0, '/opt/trn_rl_repo')
import math
from contextlib import ExitStack

import numpy as np
import ml_dtypes

import concourse.bass as bass
import concourse.mybir as mybir
from concourse.bass_utils import run_bass_kernel_spmd

F32 = mybir.dt.float32
BF16 = mybir.dt.bfloat16
I32 = mybir.dt.int32
AF = mybir.ActivationFunctionType
ALU = mybir.AluOpType
AX = mybir.AxisListType.X

D = 4096
HD = 128
NHG = 32
KC = D // 128
QTW = 512
INV_SQRT_HD = np.float32(1.0 / math.sqrt(HD))
C_BOUND = float(-1.0 / (2.0 * math.sqrt(HD)))
EPS_F = np.float32(1e-5)


class Ctr:

    def __init__(self, sem):
        self.sem = sem
        self.n = 0

    def bump(self, inst, k=1):
        inst.then_inc(self.sem, k)
        self.n += k
        return self.n


def host_tables(T):
    pos = np.arange(T, dtype=np.float32)
    thetas = (1.0 / (10000.0 ** (np.arange(0, HD, 2, dtype=np.float32) / HD))).astype(np.float32)
    pt = np.outer(pos, thetas).astype(np.float32)
    pt = np.concatenate([pt, pt], axis=-1)
    sinT = np.ascontiguousarray(np.sin(pt).astype(np.float32).T)
    cosT = np.ascontiguousarray(np.cos(pt).astype(np.float32).T)
    j = np.arange(128)[:, None]
    u = np.arange(896)[None, :]
    maskneg = np.where(j <= u - 384, 0.0, -1e9).astype(np.float32)
    return sinT, cosT, maskneg


class Builder:
    def __init__(self, T, dbg=False):
        self.T = T
        self.TO = T // 2
        self.NH = NHG // 2
        self.FO = self.NH * HD
        self.NQ = T // QTW
        self.dbg = dbg
        nc = bass.Bass("TRN2", target_bir_lowering=False,
                       detect_race_conditions=False)
        self.nc = nc
        self.ctx = ExitStack()
        TO, FO = self.TO, self.FO

        di = lambda name, shape, dt: nc.dram_tensor(name, shape, dt, kind="ExternalInput").ap()
        self.xin = di("xin", [TO, D], F32)
        self.wkT = di("wkT", [D, FO], F32)
        self.wvT = di("wvT", [D, FO], F32)
        self.wqT = di("wqT", [D, FO], F32)
        self.woT = di("woT", [FO, D], F32)
        self.pakT = di("pakT", [D // 16, D], I32)
        self.sclx = di("sclx", [D // 8, D], F32)
        self.cent = di("cent", [1, 16], F32)
        self.sinTd = di("sinT", [HD, T], F32)
        self.cosTd = di("cosT", [HD, T], F32)
        self.maskd = di("maskneg", [128, 896], F32)
        self.cinfo = di("cinfo", [1, 8], I32)
        self.xout = nc.dram_tensor("xout", [TO, D], F32, kind="ExternalOutput").ap()

        dsc = lambda name, shape, dt: nc.dram_tensor(name, shape, dt, kind="Internal").ap()
        self.wdq_in = dsc("wdq_in", [D // 8, D], BF16)
        self.wdq = dsc("wdq", [D, D], BF16)
        self.xabs = dsc("xabs", [TO, D], BF16)
        self.xbo = dsc("xbo", [TO, D], BF16)
        self.xbf = dsc("xbf", [T, D], BF16)
        self.x1s = dsc("x1s", [TO, D], F32)
        self.x2s = dsc("x2s", [TO, D], F32)
        self.kts = dsc("kts", [FO, T], BF16)
        self.qts = dsc("qts", [FO, T], BF16)
        self.vts = dsc("vts", [T, FO], BF16)
        self.atns = dsc("atns", [FO, T], BF16)
        self.op4 = dsc("op4", [T, D], BF16)
        self.x2p = dsc("x2p", [TO, D], BF16)
        self.dbg_views = {}
        if dbg:
            for nm, ap_ in [("wdq", self.wdq), ("xbf", self.xbf),
                            ("x1s", self.x1s), ("x2s", self.x2s),
                            ("kts", self.kts), ("qts", self.qts),
                            ("vts", self.vts), ("atns", self.atns)]:
                self.dbg_views[nm] = nc.dram_tensor(
                    "dbg_" + nm, list(ap_.tensor.shape), ap_.dtype,
                    kind="ExternalOutput").ap()

        sem = lambda name: self.ctx.enter_context(nc.semaphore(name))
        self.qm = Ctr(sem("qm"))
        self.qx = [Ctr(sem("qx0")), Ctr(sem("qx1"))]
        self.qw = [Ctr(sem("qw0")), Ctr(sem("qw1"))]
        self.qv = [Ctr(sem("qv0")), Ctr(sem("qv1")), Ctr(sem("qv2"))]
        self.gm = Ctr(sem("gm"))
        self.gw = [Ctr(sem("gw0")), Ctr(sem("gw1")), Ctr(sem("gw2"))]
        self.gs = [Ctr(sem("gs0")), Ctr(sem("gs1"))]
        self.am = Ctr(sem("am"))
        self.asx = [Ctr(sem("as0")), Ctr(sem("as1"))]
        self.sMM = Ctr(sem("sMM"))
        self.sEV = Ctr(sem("sEV"))
        self.sDV = Ctr(sem("sDV"))
        self.sPL = Ctr(sem("sPL"))
        self.sCC = Ctr(sem("sCC"))

        sb = lambda name, shape, dt: self.ctx.enter_context(nc.sbuf_tensor(name, shape, dt))
        self.ones_col = sb("ones_col", [128, 1], F32)
        self.ones_col_bf = sb("ones_col_bf", [128, 1], BF16)
        self.ones_row = sb("ones_row", [1, 128], F32)
        self.mask_sb = sb("mask_sb", [128, 896], F32)
        self.cent_bc = sb("cent_bc", [128, 16], F32)
        self.cinfo_sb = sb("cinfo_sb", [1, 8], I32)

    @staticmethod
    def flush(eng, ctrs):
        for c in ctrs:
            if c.n > 0:
                eng.wait_ge(c.sem, c.n)

    def prologue(self):
        nc = self.nc
        qm, gm, sMM, sEV, sDV = self.qm, self.gm, self.sMM, self.sEV, self.sDV
        ctxp = ExitStack()
        pb = ctxp.enter_context(nc.psum_tensor("pb_cent", [128, 16], F32))
        with nc.Block("prologue") as b:
            @b.sync
            def _(sync):
                qm.bump(sync.dma_start(self.mask_sb[:, :], self.maskd[:, :]), 16)
                qm.bump(sync.dma_start(self.cinfo_sb[:, :], self.cinfo[:, :]), 16)
                sync.wait_ge(qm.sem, qm.n)

            @b.vector
            def _(v):
                v.memset(self.ones_col[:, :], 1.0)
                v.memset(self.ones_col_bf[:, :], 1.0)
                sDV.bump(v.memset(self.ones_row[:, :], 1.0), 1)

            @b.gpsimd
            def _(g):
                gm.bump(g.dma_start(self.cent_bc[0:1, 0:16], self.cent[:, :]), 16)

            @b.tensor
            def _(t):
                t.wait_ge(gm.sem, gm.n)
                t.wait_ge(sDV.sem, sDV.n)
                sMM.bump(t.matmul(pb[:, :], self.ones_row[:, :],
                                  self.cent_bc[0:1, 0:16], start=True, stop=True), 1)

            @b.scalar
            def _(s):
                s.wait_ge(sMM.sem, sMM.n)
                sEV.bump(s.copy(self.cent_bc[:, :], pb[:, :]), 1)
        ctxp.close()
        self.ev_cent = sEV.n

    def _deq_slab(self, eng, ctr, bufset, c0, c1, whl, nib, sclh, scll,
                  q_ready, gb_st):
        gm = self.gm
        terms, accs = bufset
        w = c1 - c0
        eng.wait_ge(q_ready[0].sem, q_ready[1])
        if gb_st is not None:
            eng.wait_ge(gm.sem, gb_st)
        nh = nib[:, c0:c1]
        nl = nib[:, D + c0:D + c1]
        tb = lambda j: terms[j % 4]
        ab = lambda j: accs[j % 2]
        done = 0
        for j in range(16):
            eng.tensor_scalar(tb(j)[:, 0:w], nh, float(j),
                              self.cent_bc[:, j:j + 1], ALU.is_equal, ALU.mult)
            eng.tensor_scalar(tb(j)[:, w:2 * w], nl, float(j),
                              self.cent_bc[:, j:j + 1], ALU.is_equal, ALU.mult)
            if j >= 2:
                jj = done
                if jj < 2:
                    eng.tensor_copy(ab(jj)[:, 0:2 * w], tb(jj)[:, 0:2 * w])
                else:
                    eng.tensor_tensor(ab(jj)[:, 0:2 * w], ab(jj)[:, 0:2 * w],
                                      tb(jj)[:, 0:2 * w], ALU.add)
                done += 1
        v = ctr.bump(eng.engine_nop(), 1)
        eng.wait_ge(ctr.sem, v)
        while done < 16:
            jj = done
            eng.tensor_tensor(ab(jj)[:, 0:2 * w], ab(jj)[:, 0:2 * w],
                              tb(jj)[:, 0:2 * w], ALU.add)
            done += 1
        v = ctr.bump(eng.engine_nop(), 1)
        eng.wait_ge(ctr.sem, v)
        eng.tensor_tensor(accs[0][:, 0:2 * w], accs[0][:, 0:2 * w],
                          accs[1][:, 0:2 * w], ALU.add)
        v = ctr.bump(eng.engine_nop(), 1)
        eng.wait_ge(ctr.sem, v)
        eng.tensor_tensor(whl[:, c0:c1], accs[0][:, 0:w], sclh[:, c0:c1], ALU.mult)
        ctr.bump(eng.tensor_tensor(whl[:, D + c0:D + c1], accs[0][:, w:2 * w],
                                   scll[:, c0:c1], ALU.mult), 1)

    def p0_dequant(self):
        nc = self.nc
        qm, gm, sDV, sPL, sCC = self.qm, self.gm, self.sDV, self.sPL, self.sCC
        CW = 512
        dv_slabs = [(c, c + CW) for c in range(0, 3072, CW)]
        pl_slabs = [(c, c + CW) for c in range(3072, D, CW)]
        ctx0 = ExitStack()
        sb = lambda n, s, d: ctx0.enter_context(nc.sbuf_tensor(n, s, d))
        pak = sb("pak", [128, D], I32)
        nibi = sb("nibi", [128, 2 * D], I32)
        nib = sb("nib", [128, 2 * D], F32)
        whl = sb("whl", [128, 2 * D], BF16)
        sclh = sb("sclh", [128, D], F32)
        scll = sb("scll", [128, D], F32)

        def mkset(tag):
            terms = [sb(f"{tag}_t{i}", [128, 2 * CW], F32) for i in range(4)]
            accs = [sb(f"{tag}_a{i}", [128, 2 * CW], F32) for i in range(2)]
            return (terms, accs)

        set_dv = mkset("ddv")
        set_pl = mkset("dpl")

        qb, gb = qm.n, gm.n
        db, plb = sDV.n, sPL.n
        ND, NP = len(dv_slabs), len(pl_slabs)

        q_grp = lambda ci: qb + 48 * (ci + 1)
        dv_ext = lambda ci: db + ci * (1 + 4 * ND) + 1
        dv_done = lambda ci, k: db + ci * (1 + 4 * ND) + 1 + 4 * k
        pl_done = lambda ci, k: plb + 4 * (ci * NP + k)
        gm_st = lambda ci: gb + 32 * (ci + 1)

        with nc.Block("p0") as b:
            @b.sync
            def _(sync):
                for ci in range(2):
                    if ci == 1:
                        sync.wait_ge(sDV.sem, dv_done(0, ND))
                        sync.wait_ge(sPL.sem, pl_done(0, NP))
                    qm.bump(sync.dma_start(pak[:, :],
                                           self.pakT[ci * 128:(ci + 1) * 128, :]), 16)
                    qm.bump(sync.dma_start(sclh[:, :],
                                           self.sclx[ci * 256:(ci + 1) * 256:2, :]), 16)
                    qm.bump(sync.dma_start(scll[:, :],
                                           self.sclx[ci * 256 + 1:(ci + 1) * 256:2, :]), 16)

            @b.vector
            def _(v):
                for ci in range(2):
                    v.wait_ge(qm.sem, q_grp(ci))
                    v.wait_ge(self.sEV.sem, self.ev_cent)
                    if ci == 1:
                        v.wait_ge(sPL.sem, pl_done(0, NP))
                        v.wait_ge(gm.sem, gm_st(0))
                    v.tensor_scalar(nibi[:, 0:D], pak[:, :], 4, 15,
                                    ALU.logical_shift_right, ALU.bitwise_and)
                    v.tensor_scalar(nibi[:, D:2 * D], pak[:, :], 15, None,
                                    ALU.bitwise_and)
                    v.tensor_copy(nib[:, 0:D], nibi[:, 0:D])
                    sDV.bump(v.tensor_copy(nib[:, D:2 * D], nibi[:, D:2 * D]), 1)
                    for k, (c0, c1) in enumerate(dv_slabs):
                        self._deq_slab(v, sDV, set_dv, c0, c1, whl, nib,
                                       sclh, scll, (sDV, dv_ext(ci)),
                                       gm_st(0) if ci == 1 else None)

            @b.gpsimd
            def _(g):
                for ci in range(2):
                    for k, (c0, c1) in enumerate(pl_slabs):
                        self._deq_slab(g, sPL, set_pl, c0, c1, whl, nib,
                                       sclh, scll, (sDV, dv_ext(ci)),
                                       gm_st(0) if ci == 1 else None)
                    g.wait_ge(sDV.sem, dv_done(ci, ND))
                    g.wait_ge(sPL.sem, pl_done(ci, NP))
                    base = ci * 256
                    gm.bump(g.dma_start(self.wdq_in[base:base + 256:2, :],
                                        whl[:, 0:D]), 16)
                    gm.bump(g.dma_start(self.wdq_in[base + 1:base + 256:2, :],
                                        whl[:, D:2 * D]), 16)
                g.wait_ge(gm.sem, gm.n)
                sCC.bump(g.collective_compute(
                    "AllGather", ALU.bypass,
                    replica_groups=[list(range(8))],
                    ins=[self.wdq_in.opt()],
                    outs=[self.wdq.opt()]), 1)
        ctx0.close()
        self.cc_wdq = sCC.n

    def ln_phase(self, name, src, dst, guards=(), src2=None, sum_dst=None):
        nc = self.nc
        qx, qw, gs, sEV, sDV = self.qx, self.qw, self.gs, self.sEV, self.sDV
        nt = self.TO // 128
        KI = 5 if src2 is not None else 4
        NSR = 2 if src2 is not None else 1
        ctx0 = ExitStack()
        sb = lambda n, s, d: ctx0.enter_context(nc.sbuf_tensor(n, s, d))
        src_dt = F32 if src.dtype == F32 else BF16
        xln = [sb(f"{name}_x{i}", [128, D], src_dt) for i in range(2)]
        xln2 = [sb(f"{name}_y{i}", [128, D], F32) for i in range(2)] if src2 is not None else None
        xsum = xln if src2 is None else [sb(f"{name}_u{i}", [128, D], F32) for i in range(2)]
        xa = [sb(f"{name}_a{i}", [128, D], BF16) for i in range(2)]
        stats = [sb(f"{name}_s{i}", [128, 48], F32) for i in range(2)]
        aggr = [sb(f"{name}_g{i}", [128, 2], F32) for i in range(2)]
        lnv = [sb(f"{name}_l{i}", [128, 1], F32) for i in range(2)]
        rstd = [sb(f"{name}_r{i}", [128, 1], F32) for i in range(2)]
        nmr = [sb(f"{name}_n{i}", [128, 1], F32) for i in range(2)]
        eb, db = sEV.n, sDV.n
        qxb = [qx[0].n, qx[1].n]
        qwb = [qw[0].n, qw[1].n]
        gsb = [gs[0].n, gs[1].n]
        xt = lambda i: qxb[i % 2] + 16 * (i // 2 + 1)
        x2t = lambda i: qwb[i % 2] + 16 * (i // 2 + 1)
        st = lambda i: gsb[i % 2] + 16 * NSR * (i // 2 + 1)
        d_sum = lambda i: db + KI * i + 1
        d_statsnop = lambda i: db + KI * i + KI - 3
        d_aggr = lambda i: db + KI * i + KI - 2
        d_nmrnop = lambda i: db + KI * i + KI - 1
        d_norm = lambda i: db + KI * i + KI
        with nc.Block(name) as b:
            @b.sync
            def _(sync):
                self.flush(sync, qx + (qw if src2 is not None else []))
                for gd in guards:
                    sync.wait_ge(gd[0].sem, gd[1])
                for i in range(nt):
                    if i >= 2:
                        sync.wait_ge(sDV.sem, d_norm(i - 2))
                    qx[i % 2].bump(sync.dma_start(
                        xln[i % 2][:, :], src[i * 128:(i + 1) * 128, :]), 16)
                    if src2 is not None:
                        qw[i % 2].bump(sync.dma_start(
                            xln2[i % 2][:, :], src2[i * 128:(i + 1) * 128, :]), 16)

            @b.vector
            def _(v):
                for i in range(nt):
                    p = i % 2
                    v.wait_ge(qx[p].sem, xt(i))
                    if src2 is not None:
                        v.wait_ge(qw[p].sem, x2t(i))
                        if i >= 2:
                            v.wait_ge(gs[p].sem, st(i - 2))
                        w = sDV.bump(v.tensor_tensor(xsum[p][:, :], xln[p][:, :],
                                                     xln2[p][:, :], ALU.add), 1)
                        v.wait_ge(sDV.sem, w)
                    for c in range(8):
                        v.bn_stats(stats[p][:, 6 * c:6 * c + 6],
                                   xsum[p][:, 512 * c:512 * (c + 1)])
                    w = sDV.bump(v.engine_nop(), 1)
                    v.wait_ge(sDV.sem, w)
                    sDV.bump(v.bn_aggr(aggr[p][:, :], stats[p][:, :]), 1)
                    v.wait_ge(sEV.sem, eb + 2 * i + 2)
                    v.scalar_tensor_tensor(nmr[p][:, :], aggr[p][:, 0:1], -1.0,
                                           rstd[p][:, 0:1], ALU.mult, ALU.mult)
                    w = sDV.bump(v.engine_nop(), 1)
                    v.wait_ge(sDV.sem, w)
                    if i >= 2:
                        v.wait_ge(gs[p].sem, st(i - 2))
                    sDV.bump(v.tensor_scalar(xa[p][:, :], xsum[p][:, :],
                                             rstd[p][:, 0:1], nmr[p][:, 0:1],
                                             ALU.mult, ALU.add), 1)

            @b.scalar
            def _(s):
                for i in range(nt):
                    p = i % 2
                    s.wait_ge(sDV.sem, d_aggr(i))
                    w = sEV.bump(s.activation(lnv[p][:, :], aggr[p][:, 1:2],
                                              AF.Ln, bias=EPS_F), 1)
                    s.wait_ge(sEV.sem, w)
                    sEV.bump(s.activation(rstd[p][:, :], lnv[p][:, :],
                                          AF.Exp, scale=np.float32(-0.5)), 1)

            @b.gpsimd
            def _(g):
                self.flush(g, gs)
                for i in range(nt):
                    g.wait_ge(sDV.sem, d_norm(i))
                    if src2 is not None:
                        gs[i % 2].bump(g.dma_start(
                            sum_dst[i * 128:(i + 1) * 128, :], xsum[i % 2][:, :]), 16)
                    gs[i % 2].bump(g.dma_start(
                        dst[i * 128:(i + 1) * 128, :], xa[i % 2][:, :]), 16)
        ctx0.close()
        self.last_ln_dve = sDV.n
        return [(gs[0], gs[0].n), (gs[1], gs[1].n)]

    def transpose_resident(self, name, src, resident, rows, src_ready):
        nc = self.nc
        am = self.am
        with nc.Block(name) as b:
            @b.scalar
            def _(s):
                self.flush(s, [am])
                for rd in src_ready:
                    s.wait_ge(rd[0].sem, rd[1])
                for kc in range(KC):
                    am.bump(s.dma_start(resident[:, kc * rows:(kc + 1) * rows],
                                        src[0:rows, kc * 128:(kc + 1) * 128],
                                        transpose=True), 16)
        return [(am, am.n)]

    def gemm_token(self, name, resident, res_ready, wsrc_kind, wsrc, resid_src,
                   out_dst, scale, sync_guard=(), rows=None, nkc=KC, out_dt=F32):
        nc = self.nc
        qx, qw, gw, asx = self.qx, self.qw, self.gw, self.asx
        sMM, sDV = self.sMM, self.sDV
        rows = rows if rows is not None else self.TO
        MC = rows // 128
        NS = D // 512
        ctx0 = ExitStack()
        sb = lambda n, s, d: ctx0.enter_context(nc.sbuf_tensor(n, s, d))
        wst = [sb(f"{name}_w{i}", [128, nkc * 512], BF16) for i in range(2)]
        rbuf = [sb(f"{name}_r{i}", [128, 512], F32) for i in range(2)] \
            if resid_src is not None else None
        obuf = [sb(f"{name}_o{i}", [128, 512], out_dt) for i in range(2)]
        ps = [ctx0.enter_context(nc.psum_tensor(f"{name}_p{i}", [128, 512], F32))
              for i in range(2)]
        mb, db = sMM.n, sDV.n
        qwb = [qw[0].n, qw[1].n]
        gwb = [gw[0].n, gw[1].n]
        qxb = [qx[0].n, qx[1].n]
        asb = [asx[0].n, asx[1].n]
        strip_t = lambda ns: (qwb if wsrc_kind == 'bf16' else gwb)[ns % 2] + 16 * (ns // 2 + 1)
        resid_t = lambda g: qxb[g % 2] + 16 * (g // 2 + 1)
        store_t = lambda g: asb[g % 2] + 16 * (g // 2 + 1)

        def strip_dma(eng, ns):
            return eng.dma_start(
                wst[ns % 2][:, :].rearrange("p (kc n) -> p kc n", kc=nkc),
                wsrc[:, ns * 512:(ns + 1) * 512].rearrange(
                    "(kc p) n -> p kc n", p=128))

        with nc.Block(name) as b:
            @b.sync
            def _(sync):
                self.flush(sync, qx + (qw if wsrc_kind == 'bf16' else []))
                for gd in sync_guard:
                    sync.wait_ge(gd[0].sem, gd[1])
                for ns in range(NS):
                    if wsrc_kind == 'bf16':
                        if ns >= 2:
                            sync.wait_ge(sMM.sem, mb + (ns - 1) * MC)
                        qw[ns % 2].bump(strip_dma(sync, ns), 16)
                    if resid_src is not None:
                        for m in range(MC):
                            g = ns * MC + m
                            if g >= 2:
                                sync.wait_ge(sDV.sem, db + g - 1)
                            qx[g % 2].bump(sync.dma_start(
                                rbuf[g % 2][:, :],
                                resid_src[m * 128:(m + 1) * 128,
                                          ns * 512:(ns + 1) * 512]), 16)

            if wsrc_kind == 'f32':
                @b.gpsimd
                def _(gp):
                    self.flush(gp, gw)
                    for ns in range(NS):
                        if ns >= 2:
                            gp.wait_ge(sMM.sem, mb + (ns - 1) * MC)
                        gw[ns % 2].bump(strip_dma(gp, ns), 16)

            @b.tensor
            def _(t):
                for rd in res_ready:
                    t.wait_ge(rd[0].sem, rd[1])
                for ns in range(NS):
                    sctr = (qw if wsrc_kind == 'bf16' else gw)[ns % 2]
                    t.wait_ge(sctr.sem, strip_t(ns))
                    for m in range(MC):
                        g = ns * MC + m
                        if g >= 2:
                            t.wait_ge(sDV.sem, db + g - 1)
                        for k in range(nkc):
                            inst = t.matmul(
                                ps[g % 2][:, :],
                                resident[:, k * rows + m * 128: k * rows + (m + 1) * 128],
                                wst[ns % 2][:, k * 512:(k + 1) * 512],
                                start=(k == 0), stop=(k == nkc - 1))
                        sMM.bump(inst, 1)

            @b.vector
            def _(v):
                for ns in range(NS):
                    for m in range(MC):
                        g = ns * MC + m
                        v.wait_ge(sMM.sem, mb + g + 1)
                        if resid_src is not None:
                            v.wait_ge(qx[g % 2].sem, resid_t(g))
                        if g >= 2:
                            v.wait_ge(asx[g % 2].sem, store_t(g - 2))
                        if resid_src is not None:
                            sDV.bump(v.scalar_tensor_tensor(
                                obuf[g % 2][:, :], ps[g % 2][:, :], scale,
                                rbuf[g % 2][:, :], ALU.mult, ALU.add), 1)
                        else:
                            sDV.bump(v.tensor_scalar(
                                obuf[g % 2][:, :], ps[g % 2][:, :], scale, None,
                                ALU.mult), 1)

            @b.scalar
            def _(s):
                self.flush(s, asx)
                for ns in range(NS):
                    for m in range(MC):
                        g = ns * MC + m
                        s.wait_ge(sDV.sem, db + g + 1)
                        asx[g % 2].bump(s.dma_start(
                            out_dst[m * 128:(m + 1) * 128, ns * 512:(ns + 1) * 512],
                            obuf[g % 2][:, :]), 16)
        ctx0.close()
        return [(asx[0], asx[0].n), (asx[1], asx[1].n)]

    def p2c_half(self, th, xbT, xbT_ready):
        nc = self.nc
        T, TH = self.T, self.T // 2
        FO, NH = self.FO, self.NH
        qm, gw, asx = self.qm, self.gw, self.asx
        sMM, sEV, sDV = self.sMM, self.sEV, self.sDV
        tc0 = th * TH

        jobs = [("k", hl) for hl in range(NH)] + [("q", hl) for hl in range(NH)]
        sin_ready = []
        ctxb = ExitStack()
        sbb = lambda n, s, d: ctxb.enter_context(nc.sbuf_tensor(n, s, d))
        sinb = sbb(f"sin{th}", [128, TH], BF16)
        cosb = sbb(f"cos{th}", [128, TH], BF16)
        wst = [sbb(f"j{th}w{i}", [128, KC * HD], BF16) for i in range(3)]
        kqo = [sbb(f"j{th}o{i}", [128, TH], BF16) for i in range(2)]
        rt = [sbb(f"j{th}t{i}", [64, TH], F32) for i in range(4)]
        psK = [ctxb.enter_context(nc.psum_tensor(f"j{th}p{i}", [128, TH], F32))
               for i in range(2)]
        mb, db = sMM.n, sDV.n
        gwb = [gw[0].n, gw[1].n, gw[2].n]
        asb = [asx[0].n, asx[1].n]
        strip_t = lambda i: gwb[i % 3] + 16 * (i // 3 + 1)
        store_t = lambda i: asb[i % 2] + 16 * (i // 2 + 1)

        gm = self.gm
        with nc.Block(f"p2b{th}") as b:
            @b.gpsimd
            def _(gp):
                self.flush(gp, [gm])
                gm.bump(gp.dma_start(sinb[:, :],
                                     self.sinTd[:, tc0:tc0 + TH]), 16)
                gm.bump(gp.dma_start(cosb[:, :],
                                     self.cosTd[:, tc0:tc0 + TH]), 16)
                sin_ready.append((gm, gm.n))
                self.flush(gp, gw)
                for i, (kind, hl) in enumerate(jobs):
                    wsrc = self.wkT if kind == "k" else self.wqT
                    if i >= 3:
                        gp.wait_ge(sMM.sem, mb + i - 2)
                    gw[i % 3].bump(gp.dma_start(
                        wst[i % 3][:, :].rearrange("p (kc n) -> p kc n", kc=KC),
                        wsrc[:, hl * HD:(hl + 1) * HD].rearrange(
                            "(kc p) n -> p kc n", p=128)), 16)

            @b.tensor
            def _(t):
                for rd in xbT_ready:
                    t.wait_ge(rd[0].sem, rd[1])
                for i, (kind, hl) in enumerate(jobs):
                    t.wait_ge(gw[i % 3].sem, strip_t(i))
                    if i >= 2:
                        t.wait_ge(sDV.sem, db + i - 1)
                    for k in range(KC):
                        for nb in range(TH // 512):
                            inst = t.matmul(
                                psK[i % 2][:, nb * 512:(nb + 1) * 512],
                                wst[i % 3][:, k * HD:(k + 1) * HD],
                                xbT[:, k * TH + nb * 512: k * TH + (nb + 1) * 512],
                                start=(k == 0), stop=(k == KC - 1))
                    sMM.bump(inst, 1)

            @b.vector
            def _(v):
                for rd in sin_ready:
                    v.wait_ge(rd[0].sem, rd[1])
                for i, (kind, hl) in enumerate(jobs):
                    p = i % 2
                    v.wait_ge(sMM.sem, mb + i + 1)
                    if i >= 2:
                        v.wait_ge(asx[p].sem, store_t(i - 2))
                    pk = psK[p]
                    v.tensor_tensor(rt[0][:, :], sinb[0:64, :], pk[64:128, :], ALU.mult)
                    v.tensor_tensor(rt[1][:, :], cosb[0:64, :], pk[0:64, :], ALU.mult)
                    v.tensor_tensor(rt[2][:, :], sinb[64:128, :], pk[0:64, :], ALU.mult)
                    v.tensor_tensor(rt[3][:, :], cosb[64:128, :], pk[64:128, :], ALU.mult)
                    v.tensor_tensor(kqo[p][0:64, :], rt[1][:, :], rt[0][:, :], ALU.subtract)
                    sDV.bump(v.tensor_tensor(kqo[p][64:128, :], rt[3][:, :],
                                             rt[2][:, :], ALU.add), 1)

            @b.scalar
            def _(s):
                self.flush(s, asx)
                for i, (kind, hl) in enumerate(jobs):
                    s.wait_ge(sDV.sem, db + i + 1)
                    dst = self.kts if kind == "k" else self.qts
                    asx[i % 2].bump(s.dma_start(
                        dst[hl * HD:(hl + 1) * HD, tc0:tc0 + TH],
                        kqo[i % 2][:, :]), 16)
        ctxb.close()

        ctxc = ExitStack()
        sbc = lambda n, s, d: ctxc.enter_context(nc.sbuf_tensor(n, s, d))
        vw = [sbc(f"v{th}w{i}", [128, KC * 512], BF16) for i in range(2)]
        vo = [sbc(f"v{th}o{i}", [128, 512], BF16) for i in range(2)]
        psV = [ctxc.enter_context(nc.psum_tensor(f"v{th}p{i}", [128, 512], F32))
               for i in range(2)]
        NSv = FO // 512
        MCv = TH // 128
        mb2, eb2 = sMM.n, sEV.n
        gwb2 = [gw[0].n, gw[1].n]
        asb2 = [asx[0].n, asx[1].n]
        vstrip_t = lambda ns: gwb2[ns % 2] + 16 * (ns // 2 + 1)
        vstore_t = lambda g: asb2[g % 2] + 16 * (g // 2 + 1)
        with nc.Block(f"p2v{th}") as b:
            @b.gpsimd
            def _(gp):
                self.flush(gp, gw)
                for ns in range(NSv):
                    if ns >= 2:
                        gp.wait_ge(sMM.sem, mb2 + (ns - 1) * MCv)
                    gw[ns % 2].bump(gp.dma_start(
                        vw[ns % 2][:, :].rearrange("p (kc n) -> p kc n", kc=KC),
                        self.wvT[:, ns * 512:(ns + 1) * 512].rearrange(
                            "(kc p) n -> p kc n", p=128)), 16)

            @b.tensor
            def _(t):
                for ns in range(NSv):
                    t.wait_ge(gw[ns % 2].sem, vstrip_t(ns))
                    for m in range(MCv):
                        g = ns * MCv + m
                        if g >= 2:
                            t.wait_ge(sEV.sem, eb2 + g - 1)
                        for k in range(KC):
                            inst = t.matmul(
                                psV[g % 2][:, :],
                                xbT[:, k * TH + m * 128: k * TH + (m + 1) * 128],
                                vw[ns % 2][:, k * 512:(k + 1) * 512],
                                start=(k == 0), stop=(k == KC - 1))
                        sMM.bump(inst, 1)

            @b.scalar
            def _(s):
                self.flush(s, asx)
                for ns in range(NSv):
                    for m in range(MCv):
                        g = ns * MCv + m
                        s.wait_ge(sMM.sem, mb2 + g + 1)
                        if g >= 2:
                            s.wait_ge(asx[g % 2].sem, vstore_t(g - 2))
                        w = sEV.bump(s.copy(vo[g % 2][:, :], psV[g % 2][:, :]), 1)
                        s.wait_ge(sEV.sem, w)
                        asx[g % 2].bump(s.dma_start(
                            self.vts[tc0 + m * 128:tc0 + (m + 1) * 128,
                                     ns * 512:(ns + 1) * 512],
                            vo[g % 2][:, :]), 16)
        ctxc.close()

    def p3_attention(self, max_groups=99, max_qt=99):
        nc = self.nc
        T, NH, FO = self.T, self.NH, self.FO
        NQ = min(self.NQ, max_qt)
        qm, qv, asx = self.qm, self.qv, self.asx
        sMM, sEV, sDV, sPL = self.sMM, self.sEV, self.sDV, self.sPL
        groups = [list(range(0, 6)), list(range(6, 12)), list(range(12, 16))]
        groups = groups[:max_groups]
        JPQ = QTW // 128
        NC8 = T // 512
        ctx0 = ExitStack()
        sb = lambda n, s, d: ctx0.enter_context(nc.sbuf_tensor(n, s, d))
        Ksb = [sb(f"at_k{i}", [128, T], BF16) for i in range(6)]
        Qful = [sb(f"at_q{i}", [128, T], BF16) for i in range(6)]
        Vsb = [sb(f"at_v{i}", [128, FO], BF16) for i in range(3)]
        Esb = [sb(f"at_e{i}", [128, 512], BF16) for i in range(6)]
        Zac = [[sb(f"at_z{i}_{p}", [128, 512], F32) for p in range(2)] for i in range(6)]
        Zfin = [sb(f"at_zf{i}", [128, 512], F32) for i in range(6)]
        Qsq = sb("at_qs", [128, T], BF16)
        Ksq = sb("at_ks", [128, T], BF16)
        qnr = sb("at_qn", [1, 512], F32)
        knr = sb("at_kn", [1, 512], F32)
        tmpmx = sb("at_tm", [1, 2], F32)
        mq2 = sb("at_mq2", [1, 6], F32)
        mk2 = sb("at_mk2", [1, 6], F32)
        sumb = sb("at_sumb", [1, 6], F32)
        negbf = sb("at_negbf", [1, 6], F32)
        negbc = [sb(f"at_nb{i}", [128, 1], F32) for i in range(6)]
        zrow = [sb(f"at_zr{i}", [1, 512], F32) for i in range(6)]
        rzrow = [sb(f"at_rz{i}", [1, 512], F32) for i in range(6)]
        rzbc = [sb(f"at_rb{i}", [128, 512], F32) for i in range(2)]
        aout = [sb(f"at_ao{i}", [128, 512], BF16) for i in range(2)]
        psO = [ctx0.enter_context(nc.psum_tensor(f"at_pO{i}", [128, 512], F32))
               for i in range(6)]
        psS = [ctx0.enter_context(nc.psum_tensor(f"at_pS{i}", [128, 512], F32))
               for i in range(2)]

        def round_seq(H):
            seq = []
            for sl in range(H):
                seq.append(("S", sl))
                if sl >= 2:
                    seq.append(("PV", sl - 2))
            seq.append(("PV", H - 2))
            seq.append(("PV", H - 1))
            return seq

        zeng = lambda slot: "dve" if slot % 2 == 0 else "pl"

        C = {}
        q = {"pe": sMM.n, "act": sEV.n, "dve": sDV.n, "pl": sPL.n,
             "qm": qm.n, "qv": [qv[0].n, qv[1].n, qv[2].n],
             "as": [asx[0].n, asx[1].n]}
        last_sto = {0: None, 1: None}
        all_rounds = []
        for gi, hg in enumerate(groups):
            H = len(hg)
            q["qm"] += 16 * 2 * H
            C[("kq", gi)] = q["qm"]
            for sl in range(H):
                for c in range(NC8):
                    q["pe"] += 1
                    C[("qnc", gi, sl, c)] = q["pe"]
                    q["pe"] += 1
                    C[("knc", gi, sl, c)] = q["pe"]
            for sl in range(H):
                q["pe"] += 1
                C[("nb", gi, sl)] = q["pe"]
            for sl in range(H):
                for c in range(NC8):
                    q["act"] += 1
                    C[("eqc", gi, sl, c)] = q["act"]
                    q["act"] += 1
                    C[("ekc", gi, sl, c)] = q["act"]
            for sl in range(H):
                q["act"] += 1
                C[("enb", gi, sl)] = q["act"]
            for sl in range(H):
                q["dve"] += 1
                C[("dqs", gi, sl)] = q["dve"]
                q["dve"] += 1
                C[("dks", gi, sl)] = q["dve"]
                for c in range(NC8):
                    q["dve"] += 1
                    C[("drq", gi, sl, c)] = q["dve"]
                    q["dve"] += 1
                    C[("drk", gi, sl, c)] = q["dve"]
            for sl in range(H):
                q["dve"] += 1
                C[("dnb", gi, sl)] = q["dve"]
            for qt in range(NQ):
                J = (qt + 1) * JPQ
                for jc in range(J):
                    sl3 = jc % 3
                    q["qv"][sl3] += 16
                    C[("ldv", gi, qt, jc)] = (sl3, q["qv"][sl3])
                    all_rounds.append((gi, qt, jc, H))
                for jc in range(J):
                    for kind, sl in round_seq(H):
                        q["pe"] += 1
                        C[(kind, gi, qt, jc, sl)] = q["pe"]
                for sl in range(H):
                    q["pe"] += 1
                    C[("zmm", gi, qt, sl)] = q["pe"]
                for sl in range(H):
                    q["pe"] += 1
                    C[("rzb", gi, qt, sl)] = q["pe"]
                for jc in range(J):
                    for sl in range(H):
                        q["act"] += 1
                        C[("exp", gi, qt, jc, sl)] = q["act"]
                for sl in range(H):
                    q["act"] += 1
                    C[("ezr", gi, qt, sl)] = q["act"]
                for sl in range(H):
                    q["act"] += 1
                    C[("erz", gi, qt, sl)] = q["act"]
                    p = sl % 2
                    q["as"][p] += 16
                    C[("sto", gi, qt, sl)] = (p, q["as"][p])
                for jc in range(J):
                    diag = jc >= qt * JPQ
                    for sl in range(H):
                        if diag:
                            q["dve"] += 1
                            C[("pm", gi, qt, jc, sl)] = q["dve"]
                        e = zeng(sl)
                        q[e] += 1
                        C[("za", gi, qt, jc, sl)] = q[e]
                for sl in range(H):
                    e = zeng(sl)
                    q[e] += 1
                    C[("zf", gi, qt, sl)] = q[e]
                for sl in range(H):
                    q["dve"] += 1
                    C[("rcp", gi, qt, sl)] = q["dve"]
                for sl in range(H):
                    q["dve"] += 1
                    C[("emu", gi, qt, sl)] = q["dve"]
                for sl in range(H):
                    C[("aoutw", gi, qt, sl)] = last_sto[sl % 2]
                    last_sto[sl % 2] = C[("sto", gi, qt, sl)]

        vprev = {}
        by_slot = {0: None, 1: None, 2: None}
        for (gi, qt, jc, H) in all_rounds:
            vprev[(gi, qt, jc)] = by_slot[jc % 3]
            by_slot[jc % 3] = (gi, qt, jc, H)

        sem_of = {"pe": sMM, "act": sEV, "dve": sDV, "pl": sPL}

        def prev_group_pe_done(gi):
            Hp = len(groups[gi - 1])
            return C[("rzb", gi - 1, NQ - 1, Hp - 1)]

        with nc.Block("p3") as b:
            @b.sync
            def _(sync):
                self.flush(sync, [qm] + qv)
                for gd in self.p2_stores_final:
                    sync.wait_ge(gd[0].sem, gd[1])
                for gi, hg in enumerate(groups):
                    H = len(hg)
                    if gi >= 1:
                        sync.wait_ge(sMM.sem, prev_group_pe_done(gi))
                    for sl in range(H):
                        hl = hg[sl]
                        qm.bump(sync.dma_start(
                            Ksb[sl][:, :], self.kts[hl * HD:(hl + 1) * HD, :]), 16)
                        qm.bump(sync.dma_start(
                            Qful[sl][:, :], self.qts[hl * HD:(hl + 1) * HD, :]), 16)
                    assert qm.n == C[("kq", gi)]
                    for qt in range(NQ):
                        J = (qt + 1) * JPQ
                        for jc in range(J):
                            pv = vprev[(gi, qt, jc)]
                            if pv is not None:
                                pgi, pqt, pjc, pH = pv
                                sync.wait_ge(sMM.sem,
                                             C[("PV", pgi, pqt, pjc, pH - 1)])
                            sl3, tgt = C[("ldv", gi, qt, jc)]
                            qv[sl3].bump(sync.dma_start(
                                Vsb[sl3][:, :],
                                self.vts[jc * 128:(jc + 1) * 128, :]), 16)
                            assert qv[sl3].n == tgt

            @b.tensor
            def _(t):
                psS_last_evac = [(None, 0), (None, 0)]
                psO_free = [None] * 6
                for gi, hg in enumerate(groups):
                    H = len(hg)
                    for bank in range(2):
                        if psO_free[bank] is not None:
                            e, cval = psO_free[bank]
                            t.wait_ge(sem_of[e].sem, cval)
                    for sl in range(H):
                        t.wait_ge(sDV.sem, C[("dqs", gi, sl)])
                        t.wait_ge(sDV.sem, C[("dks", gi, sl)])
                        for c in range(NC8):
                            if (sl, c) != (0, 0):
                                pc = (sl, c - 1) if c > 0 else (sl - 1, NC8 - 1)
                                t.wait_ge(sEV.sem, C[("eqc", gi) + pc])
                            t.matmul(psO[0][0:1, :], self.ones_col_bf[:, :],
                                     Qsq[:, c * 512:(c + 1) * 512],
                                     start=True, stop=True).then_inc(sMM.sem, 1)
                            if (sl, c) != (0, 0):
                                pc = (sl, c - 1) if c > 0 else (sl - 1, NC8 - 1)
                                t.wait_ge(sEV.sem, C[("ekc", gi) + pc])
                            t.matmul(psO[1][0:1, :], self.ones_col_bf[:, :],
                                     Ksq[:, c * 512:(c + 1) * 512],
                                     start=True, stop=True).then_inc(sMM.sem, 1)
                    for sl in range(H):
                        if sl == 0 and psO_free[2] is not None:
                            e, cval = psO_free[2]
                            t.wait_ge(sem_of[e].sem, cval)
                        t.wait_ge(sDV.sem, C[("dnb", gi, sl)])
                        if sl >= 1:
                            t.wait_ge(sEV.sem, C[("enb", gi, sl - 1)])
                        t.matmul(psO[2][:, 0:1], self.ones_row[:, :],
                                 negbf[0:1, sl:sl + 1], start=True, stop=True
                                 ).then_inc(sMM.sem, 1)
                    for qt in range(NQ):
                        J = (qt + 1) * JPQ
                        for jc in range(J):
                            for kind, sl in round_seq(H):
                                if kind == "S":
                                    e, cval = psS_last_evac[sl % 2]
                                    if e is not None:
                                        t.wait_ge(sem_of[e].sem, cval)
                                    t.wait_ge(qm.sem, C[("kq", gi)])
                                    t.matmul(psS[sl % 2][:, :],
                                             Ksb[sl][:, jc * 128:(jc + 1) * 128],
                                             Qful[sl][:, qt * QTW:(qt + 1) * QTW],
                                             start=True, stop=True
                                             ).then_inc(sMM.sem, 1)
                                    psS_last_evac[sl % 2] = (
                                        "act", C[("exp", gi, qt, jc, sl)])
                                else:
                                    t.wait_ge(sEV.sem, C[("exp", gi, qt, jc, sl)])
                                    if qt == 0 and jc == 0:
                                        t.wait_ge(sEV.sem, C[("enb", gi, H - 1)])
                                    sl3, tgt = C[("ldv", gi, qt, jc)]
                                    t.wait_ge(qv[sl3].sem, tgt)
                                    hl = hg[sl]
                                    t.matmul(psO[sl][:, :],
                                             Vsb[sl3][:, hl * HD:(hl + 1) * HD],
                                             Esb[sl][:, :],
                                             start=(jc == 0), stop=(jc == J - 1)
                                             ).then_inc(sMM.sem, 1)
                        for sl in range(H):
                            e = zeng(sl)
                            t.wait_ge(sem_of[e].sem, C[("zf", gi, qt, sl)])
                            ep, cval = psS_last_evac[sl % 2]
                            if ep is not None:
                                t.wait_ge(sem_of[ep].sem, cval)
                            t.matmul(psS[sl % 2][0:1, :], self.ones_col[:, :],
                                     Zfin[sl][:, :], start=True, stop=True
                                     ).then_inc(sMM.sem, 1)
                            psS_last_evac[sl % 2] = ("act", C[("ezr", gi, qt, sl)])
                        for sl in range(H):
                            t.wait_ge(sDV.sem, C[("rcp", gi, qt, sl)])
                            ep, cval = psS_last_evac[sl % 2]
                            t.wait_ge(sem_of[ep].sem, cval)
                            t.matmul(psS[sl % 2][:, :], self.ones_row[:, :],
                                     rzrow[sl][0:1, :], start=True, stop=True
                                     ).then_inc(sMM.sem, 1)
                            psS_last_evac[sl % 2] = ("act", C[("erz", gi, qt, sl)])
                        for sl in range(H):
                            psO_free[sl] = ("dve", C[("emu", gi, qt, sl)])
                sMM.n = q["pe"]

            @b.scalar
            def _(s):
                self.flush(s, asx)
                for gi, hg in enumerate(groups):
                    H = len(hg)
                    for sl in range(H):
                        for c in range(NC8):
                            s.wait_ge(sMM.sem, C[("qnc", gi, sl, c)])
                            pq = (sl, c - 1) if c > 0 else \
                                ((sl - 1, NC8 - 1) if sl >= 1 else None)
                            if pq is not None:
                                s.wait_ge(sDV.sem, C[("drq", gi) + pq])
                            s.copy(qnr[0:1, :], psO[0][0:1, :]
                                   ).then_inc(sEV.sem, 1)
                            s.wait_ge(sMM.sem, C[("knc", gi, sl, c)])
                            if pq is not None:
                                s.wait_ge(sDV.sem, C[("drk", gi) + pq])
                            s.copy(knr[0:1, :], psO[1][0:1, :]
                                   ).then_inc(sEV.sem, 1)
                    for sl in range(H):
                        s.wait_ge(sMM.sem, C[("nb", gi, sl)])
                        s.copy(negbc[sl][:, :], psO[2][:, 0:1]
                               ).then_inc(sEV.sem, 1)
                    for qt in range(NQ):
                        J = (qt + 1) * JPQ
                        for jc in range(J):
                            diag = jc >= qt * JPQ
                            for sl in range(H):
                                s.wait_ge(sMM.sem, C[("S", gi, qt, jc, sl)])
                                if diag:
                                    s.wait_ge(sDV.sem, C[("pm", gi, qt, jc, sl)])
                                if jc >= 1:
                                    s.wait_ge(sMM.sem, C[("PV", gi, qt, jc - 1, sl)])
                                    e = zeng(sl)
                                    s.wait_ge(sem_of[e].sem,
                                              C[("za", gi, qt, jc - 1, sl)])
                                elif qt >= 1:
                                    Jp = qt * JPQ
                                    s.wait_ge(sMM.sem, C[("PV", gi, qt - 1, Jp - 1, sl)])
                                    e = zeng(sl)
                                    s.wait_ge(sem_of[e].sem,
                                              C[("za", gi, qt - 1, Jp - 1, sl)])
                                elif gi >= 1 and sl < len(groups[gi - 1]):
                                    Jp = NQ * JPQ
                                    s.wait_ge(sMM.sem,
                                              C[("PV", gi - 1, NQ - 1, Jp - 1, sl)])
                                    e = zeng(sl)
                                    s.wait_ge(sem_of[e].sem,
                                              C[("za", gi - 1, NQ - 1, Jp - 1, sl)])
                                s.activation(Esb[sl][:, :], psS[sl % 2][:, :],
                                             AF.Exp, bias=negbc[sl][:, 0:1],
                                             scale=INV_SQRT_HD).then_inc(sEV.sem, 1)
                        for sl in range(H):
                            s.wait_ge(sMM.sem, C[("zmm", gi, qt, sl)])
                            s.copy(zrow[sl][:, :], psS[sl % 2][0:1, :]
                                   ).then_inc(sEV.sem, 1)
                        for sl in range(H):
                            hl = hg[sl]
                            s.wait_ge(sMM.sem, C[("rzb", gi, qt, sl)])
                            if sl >= 2:
                                s.wait_ge(sDV.sem, C[("emu", gi, qt, sl - 2)])
                            s.copy(rzbc[sl % 2][:, :], psS[sl % 2][:, :]
                                   ).then_inc(sEV.sem, 1)
                            s.wait_ge(sDV.sem, C[("emu", gi, qt, sl)])
                            p, tgt = C[("sto", gi, qt, sl)]
                            asx[p].bump(s.dma_start(
                                self.atns[hl * HD:(hl + 1) * HD,
                                          qt * QTW:(qt + 1) * QTW],
                                aout[sl % 2][:, :]), 16)
                            assert asx[p].n == tgt
                sEV.n = q["act"]

            @b.vector
            def _(v):
                for gi, hg in enumerate(groups):
                    H = len(hg)
                    for sl in range(H):
                        v.wait_ge(qm.sem, C[("kq", gi)])
                        if sl >= 1:
                            v.wait_ge(sMM.sem, C[("qnc", gi, sl - 1, NC8 - 1)])
                            v.wait_ge(sMM.sem, C[("knc", gi, sl - 1, NC8 - 1)])
                        v.tensor_tensor(Qsq[:, :], Qful[sl][:, :],
                                        Qful[sl][:, :], ALU.mult
                                        ).then_inc(sDV.sem, 1)
                        v.tensor_tensor(Ksq[:, :], Ksb[sl][:, :],
                                        Ksb[sl][:, :], ALU.mult
                                        ).then_inc(sDV.sem, 1)
                        for c in range(NC8):
                            v.wait_ge(sEV.sem, C[("eqc", gi, sl, c)])
                            v.reduce_max(tmpmx[0:1, 0:1], qnr[0:1, :], AX
                                         ).then_inc(sDV.sem, 1)
                            v.wait_ge(sEV.sem, C[("ekc", gi, sl, c)])
                            v.reduce_max(tmpmx[0:1, 1:2], knr[0:1, :], AX
                                         ).then_inc(sDV.sem, 1)
                            if c == 0:
                                v.tensor_copy(mq2[0:1, sl:sl + 1],
                                              tmpmx[0:1, 0:1])
                                v.tensor_copy(mk2[0:1, sl:sl + 1],
                                              tmpmx[0:1, 1:2])
                            else:
                                v.tensor_tensor(mq2[0:1, sl:sl + 1],
                                                mq2[0:1, sl:sl + 1],
                                                tmpmx[0:1, 0:1], ALU.max)
                                v.tensor_tensor(mk2[0:1, sl:sl + 1],
                                                mk2[0:1, sl:sl + 1],
                                                tmpmx[0:1, 1:2], ALU.max)
                    for sl in range(H):
                        v.tensor_tensor(sumb[0:1, sl:sl + 1],
                                        mq2[0:1, sl:sl + 1],
                                        mk2[0:1, sl:sl + 1], ALU.add)
                    for sl in range(H):
                        v.tensor_scalar(negbf[0:1, sl:sl + 1],
                                        sumb[0:1, sl:sl + 1], C_BOUND, None,
                                        ALU.mult).then_inc(sDV.sem, 1)
                    for qt in range(NQ):
                        J = (qt + 1) * JPQ
                        for jc in range(J):
                            diag = jc >= qt * JPQ
                            for sl in range(H):
                                if diag:
                                    dlt = jc * 128 - qt * QTW
                                    v.wait_ge(sMM.sem, C[("S", gi, qt, jc, sl)])
                                    v.tensor_tensor(
                                        psS[sl % 2][:, :], psS[sl % 2][:, :],
                                        self.mask_sb[:, 384 - dlt:896 - dlt],
                                        ALU.add).then_inc(sDV.sem, 1)
                                if zeng(sl) == "dve":
                                    v.wait_ge(sEV.sem, C[("exp", gi, qt, jc, sl)])
                                    if jc < 2:
                                        v.tensor_copy(Zac[sl][jc % 2][:, :],
                                                      Esb[sl][:, :]
                                                      ).then_inc(sDV.sem, 1)
                                    else:
                                        v.tensor_tensor(Zac[sl][jc % 2][:, :],
                                                        Esb[sl][:, :],
                                                        Zac[sl][jc % 2][:, :],
                                                        ALU.add).then_inc(sDV.sem, 1)
                        for sl in range(H):
                            if zeng(sl) == "dve":
                                v.tensor_tensor(Zfin[sl][:, :], Zac[sl][0][:, :],
                                                Zac[sl][1][:, :], ALU.add
                                                ).then_inc(sDV.sem, 1)
                        for sl in range(H):
                            v.wait_ge(sEV.sem, C[("ezr", gi, qt, sl)])
                            v.reciprocal(rzrow[sl][:, :], zrow[sl][:, :]
                                         ).then_inc(sDV.sem, 1)
                        for sl in range(H):
                            v.wait_ge(sEV.sem, C[("erz", gi, qt, sl)])
                            v.wait_ge(sMM.sem, C[("PV", gi, qt, J - 1, sl)])
                            if C[("aoutw", gi, qt, sl)] is not None:
                                p, tgt = C[("aoutw", gi, qt, sl)]
                                v.wait_ge(asx[p].sem, tgt)
                            v.tensor_tensor(aout[sl % 2][:, :], psO[sl][:, :],
                                            rzbc[sl % 2][:, :], ALU.mult
                                            ).then_inc(sDV.sem, 1)
                sDV.n = q["dve"]

            @b.gpsimd
            def _(g):
                for gi, hg in enumerate(groups):
                    H = len(hg)
                    for qt in range(NQ):
                        J = (qt + 1) * JPQ
                        for jc in range(J):
                            for sl in range(H):
                                if zeng(sl) != "pl":
                                    continue
                                g.wait_ge(sEV.sem, C[("exp", gi, qt, jc, sl)])
                                if jc < 2:
                                    g.tensor_copy(Zac[sl][jc % 2][:, :],
                                                  Esb[sl][:, :]).then_inc(sPL.sem, 1)
                                else:
                                    g.tensor_tensor(Zac[sl][jc % 2][:, :],
                                                    Esb[sl][:, :],
                                                    Zac[sl][jc % 2][:, :],
                                                    ALU.add).then_inc(sPL.sem, 1)
                        for sl in range(H):
                            if zeng(sl) == "pl":
                                g.tensor_tensor(Zfin[sl][:, :], Zac[sl][0][:, :],
                                                Zac[sl][1][:, :], ALU.add
                                                ).then_inc(sPL.sem, 1)
                sPL.n = q["pl"]
        ctx0.close()
        self.p3_stores_final = [(asx[0], asx[0].n), (asx[1], asx[1].n)]


    def build(self, upto=99):
        nc = self.nc
        T, TO, FO = self.T, self.TO, self.FO
        qm, gm, sCC = self.qm, self.gm, self.sCC

        self.prologue()
        if upto >= 1:
            self.p0_dequant()
        else:
            self.cc_wdq = 0
        if upto < 2:
            return self.finish(None)

        ln1_done = self.ln_phase("ln1", self.xin, self.xabs)
        ctxr = ExitStack()
        resident = ctxr.enter_context(nc.sbuf_tensor("res1", [128, KC * TO], BF16))
        rr = self.transpose_resident("tr1", self.xabs, resident, TO, ln1_done)
        with nc.Block("p1wait") as b:
            @b.sync
            def _(sync):
                sync.wait_ge(sCC.sem, self.cc_wdq)
        p1_done = self.gemm_token("p1", resident, rr, 'bf16', self.wdq,
                                  self.xin, self.x1s, 0.5,
                                  sync_guard=[(self.sDV, self.last_ln_dve)])
        ctxr.close()
        if upto < 3:
            return self.finish(p1_done)

        ln2_done = self.ln_phase("ln2", self.x1s, self.xbo, guards=p1_done)
        with nc.Block("p2x") as b:
            @b.gpsimd
            def _(g):
                for gd in ln2_done:
                    g.wait_ge(gd[0].sem, gd[1])
                sCC.bump(g.collective_compute(
                    "AllGather", ALU.bypass,
                    replica_groups=[[0, 1], [2, 3], [4, 5], [6, 7]],
                    ins=[self.xbo.opt()],
                    outs=[self.xbf.opt()]), 1)
        cc_xbf = sCC.n
        if upto < 4:
            return self.finish(None)

        for th in range(2):
            ctxh = ExitStack()
            xbT = ctxh.enter_context(
                nc.sbuf_tensor(f"xbT{th}", [128, KC * (T // 2)], BF16))
            TH = T // 2
            xbT_ready = self.transpose_resident(
                f"tr2{th}", self.xbf[th * TH:(th + 1) * TH, :],
                xbT, TH, [(sCC, cc_xbf)])
            self.p2c_half(th, xbT, xbT_ready)
            ctxh.close()
        self.p2_stores_final = [(self.asx[0], self.asx[0].n),
                                (self.asx[1], self.asx[1].n)]
        if upto < 5:
            return self.finish(None)

        self.p3_attention(max_groups=getattr(self, 'p3_groups', 99),
                          max_qt=getattr(self, 'p3_qt', 99))
        if upto < 6:
            return self.finish(self.p3_stores_final)

        ctxr4 = ExitStack()
        res4 = ctxr4.enter_context(
            nc.sbuf_tensor("res4", [128, (FO // 128) * T], BF16))
        with nc.Block("p4r") as b:
            @b.sync
            def _(sync):
                self.flush(sync, [qm])
                for gd in self.p3_stores_final:
                    sync.wait_ge(gd[0].sem, gd[1])
                for kc in range(FO // 128):
                    qm.bump(sync.dma_start(
                        res4[:, kc * T:(kc + 1) * T],
                        self.atns[kc * 128:(kc + 1) * 128, :]), 16)
        p4_done = self.gemm_token("p4", res4, [(qm, qm.n)], 'f32', self.woT,
                                  None, self.op4, 1.0, rows=T, nkc=FO // 128,
                                  out_dt=BF16)
        ctxr4.close()
        with nc.Block("p4s") as b:
            @b.gpsimd
            def _(g):
                for gd in p4_done:
                    g.wait_ge(gd[0].sem, gd[1])
                sCC.bump(g.collective_compute(
                    "ReduceScatter", ALU.add,
                    replica_groups=[[0, 1], [2, 3], [4, 5], [6, 7]],
                    ins=[self.op4.opt()],
                    outs=[self.x2p.opt()]), 1)
        cc_rs = sCC.n
        if upto < 7:
            return self.finish(None)

        ln3_done = self.ln_phase("ln3", self.x2p, self.xabs,
                                 guards=[(sCC, cc_rs)],
                                 src2=self.x1s, sum_dst=self.x2s)
        ctxr5 = ExitStack()
        res5 = ctxr5.enter_context(nc.sbuf_tensor("res5", [128, KC * TO], BF16))
        rr5 = self.transpose_resident("tr5", self.xabs, res5, TO, ln3_done)
        p5_done = self.gemm_token("p5", res5, rr5, 'bf16', self.wdq,
                                  self.x2s, self.xout, 0.5,
                                  sync_guard=[(self.sDV, self.last_ln_dve)] + ln3_done)
        ctxr5.close()
        return self.finish(p5_done)

    def finish(self, out_done):
        nc = self.nc
        gm = self.gm
        with nc.Block("fin") as b:
            @b.scalar
            def _(s):
                if out_done is not None:
                    for gd in out_done:
                        s.wait_ge(gd[0].sem, gd[1])

            if self.dbg_views:
                @b.gpsimd
                def _(g):
                    if out_done is not None:
                        for gd in out_done:
                            g.wait_ge(gd[0].sem, gd[1])
                    srcs = {"wdq": self.wdq, "xbf": self.xbf, "x1s": self.x1s,
                            "x2s": self.x2s, "kts": self.kts, "qts": self.qts,
                            "vts": self.vts, "atns": self.atns}
                    for nm, dst in self.dbg_views.items():
                        gm.bump(g.dma_start(dst[:, :], srcs[nm][:, :]), 16)
                    g.wait_ge(gm.sem, gm.n)
        self.ctx.close()
        return nc


def build_program(T, dbg=False):
    return Builder(T, dbg)


def host_prep(L0_attn_x, L0_attn_wq, L0_attn_wk, L0_attn_wv, L0_attn_wo,
              L0_ffn_weight_packed, L0_ffn_centroids, L0_ffn_scales):
    T = L0_attn_x.shape[1]
    TO = T // 2
    x = np.ascontiguousarray(np.asarray(L0_attn_x))
    wqT = np.ascontiguousarray(np.asarray(L0_attn_wq).T)
    wkT = np.ascontiguousarray(np.asarray(L0_attn_wk).T)
    wvT = np.ascontiguousarray(np.asarray(L0_attn_wv).T)
    woT = np.ascontiguousarray(np.asarray(L0_attn_wo).T)
    pakT = np.ascontiguousarray(np.asarray(L0_ffn_weight_packed).T)
    scales = np.asarray(L0_ffn_scales)
    cent = np.asarray(L0_ffn_centroids).reshape(1, 16).astype(np.float32)
    sinT, cosT, maskneg = host_tables(T)

    in_maps = []
    for c in range(8):
        s, par = c // 2, c % 2
        fsl = np.arange(c * (D // 8), (c + 1) * (D // 8))
        sclx = np.ascontiguousarray(scales.T[fsl // 64, :])
        hb = par * (NHG // 2)
        hc = slice(hb * HD, (hb + NHG // 2) * HD)
        in_maps.append({
            "xin": np.ascontiguousarray(x[s, par * TO:(par + 1) * TO, :]),
            "wkT": np.ascontiguousarray(wkT[:, hc]),
            "wvT": np.ascontiguousarray(wvT[:, hc]),
            "wqT": np.ascontiguousarray(wqT[:, hc]),
            "woT": np.ascontiguousarray(woT[hb * HD:(hb + NHG // 2) * HD, :]),
            "pakT": np.ascontiguousarray(pakT[c * (D // 16):(c + 1) * (D // 16), :]),
            "sclx": sclx,
            "cent": cent,
            "sinT": sinT,
            "cosT": cosT,
            "maskneg": maskneg,
            "cinfo": np.array([[par * TO, par, c, s, 0, 0, 0, 0]], np.int32),
        })
    return T, in_maps


def kernel(L0_attn_x, L0_attn_wq, L0_attn_wk, L0_attn_wv, L0_attn_wo,
           L0_ffn_weight_packed, L0_ffn_centroids, L0_ffn_scales):
    T, in_maps = host_prep(L0_attn_x, L0_attn_wq, L0_attn_wk, L0_attn_wv,
                           L0_attn_wo, L0_ffn_weight_packed, L0_ffn_centroids,
                           L0_ffn_scales)
    TO = T // 2
    bld = build_program(T, dbg=False)
    nc = bld.build()
    res = run_bass_kernel_spmd(nc, in_maps, core_ids=list(range(8)))
    out = np.empty((4, T, D), np.float32)
    for c in range(8):
        s, par = c // 2, c % 2
        out[s, par * TO:(par + 1) * TO, :] = res.results[c]["xout"]
    return out


# version 04
# speedup vs baseline: 1.0350x; 1.0073x over previous
import sys
if '/opt/trn_rl_repo' not in sys.path:
    sys.path.insert(0, '/opt/trn_rl_repo')
import math
from contextlib import ExitStack

import numpy as np
import ml_dtypes

import concourse.bass as bass
import concourse.mybir as mybir
from concourse.bass_utils import run_bass_kernel_spmd

F32 = mybir.dt.float32
BF16 = mybir.dt.bfloat16
I32 = mybir.dt.int32
AF = mybir.ActivationFunctionType
ALU = mybir.AluOpType
AX = mybir.AxisListType.X

D = 4096
HD = 128
NHG = 32
KC = D // 128
QTW = 512
INV_SQRT_HD = np.float32(1.0 / math.sqrt(HD))
C_BOUND = float(-1.0 / (2.0 * math.sqrt(HD)))
EPS_F = np.float32(1e-5)


class Ctr:

    def __init__(self, sem):
        self.sem = sem
        self.n = 0

    def bump(self, inst, k=1):
        inst.then_inc(self.sem, k)
        self.n += k
        return self.n


def host_tables(T):
    pos = np.arange(T, dtype=np.float32)
    thetas = (1.0 / (10000.0 ** (np.arange(0, HD, 2, dtype=np.float32) / HD))).astype(np.float32)
    pt = np.outer(pos, thetas).astype(np.float32)
    pt = np.concatenate([pt, pt], axis=-1)
    sinT = np.ascontiguousarray(np.sin(pt).astype(np.float32).T)
    cosT = np.ascontiguousarray(np.cos(pt).astype(np.float32).T)
    j = np.arange(128)[:, None]
    u = np.arange(896)[None, :]
    maskneg = np.where(j <= u - 384, 0.0, -1e9).astype(np.float32)
    return sinT, cosT, maskneg


class Builder:
    def __init__(self, T, dbg=False):
        self.T = T
        self.TO = T // 2
        self.NH = NHG // 2
        self.FO = self.NH * HD
        self.NQ = T // QTW
        self.dbg = dbg
        nc = bass.Bass("TRN2", target_bir_lowering=False,
                       detect_race_conditions=False)
        self.nc = nc
        self.ctx = ExitStack()
        TO, FO = self.TO, self.FO

        di = lambda name, shape, dt: nc.dram_tensor(name, shape, dt, kind="ExternalInput").ap()
        self.xin = di("xin", [TO, D], F32)
        self.wkT = di("wkT", [D, FO], F32)
        self.wvT = di("wvT", [D, FO], F32)
        self.wqT = di("wqT", [D, FO], F32)
        self.woT = di("woT", [FO, D], F32)
        self.pakT = di("pakT", [D // 16, D], I32)
        self.sclx = di("sclx", [D // 8, D], F32)
        self.cent = di("cent", [1, 16], F32)
        self.sinTd = di("sinT", [HD, T], F32)
        self.cosTd = di("cosT", [HD, T], F32)
        self.maskd = di("maskneg", [128, 896], F32)
        self.cinfo = di("cinfo", [1, 8], I32)
        self.xout = nc.dram_tensor("xout", [TO, D], F32, kind="ExternalOutput").ap()

        dsc = lambda name, shape, dt: nc.dram_tensor(name, shape, dt, kind="Internal").ap()
        self.wdq_inH = [dsc("wdq_in0", [D // 8, D // 2], BF16),
                        dsc("wdq_in1", [D // 8, D // 2], BF16)]
        self.wdqH = [dsc("wdq0", [D, D // 2], BF16),
                     dsc("wdq1", [D, D // 2], BF16)]
        self.xabs = dsc("xabs", [TO, D], BF16)
        self.xbo = dsc("xbo", [TO, D], BF16)
        self.xbf = dsc("xbf", [T, D], BF16)
        self.x1s = dsc("x1s", [TO, D], F32)
        self.x2s = dsc("x2s", [TO, D], F32)
        self.kts = dsc("kts", [FO, T], BF16)
        self.qts = dsc("qts", [FO, T], BF16)
        self.vts = dsc("vts", [T, FO], BF16)
        self.atns = dsc("atns", [FO, T], BF16)
        self.op4H = [dsc("op4l", [T, D // 2], BF16), dsc("op4r", [T, D // 2], BF16)]
        self.x2pH = [dsc("x2pl", [TO, D // 2], BF16), dsc("x2pr", [TO, D // 2], BF16)]
        self.dbg_views = {}
        if dbg:
            for nm, ap_ in [("xbf", self.xbf),
                            ("x1s", self.x1s), ("x2s", self.x2s),
                            ("kts", self.kts), ("qts", self.qts),
                            ("vts", self.vts), ("atns", self.atns)]:
                self.dbg_views[nm] = nc.dram_tensor(
                    "dbg_" + nm, list(ap_.tensor.shape), ap_.dtype,
                    kind="ExternalOutput").ap()

        sem = lambda name: self.ctx.enter_context(nc.semaphore(name))
        self.qm = Ctr(sem("qm"))
        self.qx = [Ctr(sem("qx0")), Ctr(sem("qx1"))]
        self.qw = [Ctr(sem("qw0")), Ctr(sem("qw1"))]
        self.qv = [Ctr(sem("qv0")), Ctr(sem("qv1")), Ctr(sem("qv2"))]
        self.gm = Ctr(sem("gm"))
        self.gw = [Ctr(sem("gw0")), Ctr(sem("gw1")), Ctr(sem("gw2"))]
        self.gs = [Ctr(sem("gs0")), Ctr(sem("gs1"))]
        self.am = Ctr(sem("am"))
        self.asx = [Ctr(sem("as0")), Ctr(sem("as1"))]
        self.sMM = Ctr(sem("sMM"))
        self.sEV = Ctr(sem("sEV"))
        self.sDV = Ctr(sem("sDV"))
        self.sPL = Ctr(sem("sPL"))
        self.sCC = Ctr(sem("sCC"))

        sb = lambda name, shape, dt: self.ctx.enter_context(nc.sbuf_tensor(name, shape, dt))
        self.ones_col = sb("ones_col", [128, 1], F32)
        self.ones_col_bf = sb("ones_col_bf", [128, 1], BF16)
        self.ones_row = sb("ones_row", [1, 128], F32)
        self.mask_sb = sb("mask_sb", [128, 896], F32)
        self.cent_bc = sb("cent_bc", [128, 16], F32)
        self.cinfo_sb = sb("cinfo_sb", [1, 8], I32)

    @staticmethod
    def flush(eng, ctrs):
        for c in ctrs:
            if c.n > 0:
                eng.wait_ge(c.sem, c.n)

    def prologue(self):
        nc = self.nc
        qm, gm, sMM, sEV, sDV = self.qm, self.gm, self.sMM, self.sEV, self.sDV
        ctxp = ExitStack()
        pb = ctxp.enter_context(nc.psum_tensor("pb_cent", [128, 16], F32))
        with nc.Block("prologue") as b:
            @b.sync
            def _(sync):
                qm.bump(sync.dma_start(self.mask_sb[:, :], self.maskd[:, :]), 16)
                qm.bump(sync.dma_start(self.cinfo_sb[:, :], self.cinfo[:, :]), 16)
                sync.wait_ge(qm.sem, qm.n)

            @b.vector
            def _(v):
                v.memset(self.ones_col[:, :], 1.0)
                v.memset(self.ones_col_bf[:, :], 1.0)
                sDV.bump(v.memset(self.ones_row[:, :], 1.0), 1)

            @b.gpsimd
            def _(g):
                gm.bump(g.dma_start(self.cent_bc[0:1, 0:16], self.cent[:, :]), 16)

            @b.tensor
            def _(t):
                t.wait_ge(gm.sem, gm.n)
                t.wait_ge(sDV.sem, sDV.n)
                sMM.bump(t.matmul(pb[:, :], self.ones_row[:, :],
                                  self.cent_bc[0:1, 0:16], start=True, stop=True), 1)

            @b.scalar
            def _(s):
                s.wait_ge(sMM.sem, sMM.n)
                sEV.bump(s.copy(self.cent_bc[:, :], pb[:, :]), 1)
        ctxp.close()
        self.ev_cent = sEV.n

    def _deq_slab(self, eng, ctr, bufset, nibb_ci, g0, g1, l0, whlh,
                  sclh, scll, q_scl, ext_t, gm_st):
        qm, gm = self.qm, self.gm
        terms, accs = bufset
        w = g1 - g0
        eng.wait_ge(qm.sem, q_scl)
        eng.wait_ge(self.sDV.sem, ext_t)
        eng.wait_ge(self.sEV.sem, self.ev_cent)
        if gm_st is not None:
            eng.wait_ge(gm.sem, gm_st)
        nh = nibb_ci[:, g0:g1]
        nl = nibb_ci[:, D + g0:D + g1]
        tb = lambda j: terms[j % 4]
        ab = lambda j: accs[j % 2]
        done = 0
        for j in range(16):
            eng.tensor_scalar(tb(j)[:, 0:w], nh, float(j),
                              self.cent_bc[:, j:j + 1], ALU.is_equal, ALU.mult)
            eng.tensor_scalar(tb(j)[:, w:2 * w], nl, float(j),
                              self.cent_bc[:, j:j + 1], ALU.is_equal, ALU.mult)
            if j >= 2:
                jj = done
                if jj < 2:
                    eng.tensor_copy(ab(jj)[:, 0:2 * w], tb(jj)[:, 0:2 * w])
                else:
                    eng.tensor_tensor(ab(jj)[:, 0:2 * w], ab(jj)[:, 0:2 * w],
                                      tb(jj)[:, 0:2 * w], ALU.add)
                done += 1
        v = ctr.bump(eng.engine_nop(), 1)
        eng.wait_ge(ctr.sem, v)
        while done < 16:
            jj = done
            eng.tensor_tensor(ab(jj)[:, 0:2 * w], ab(jj)[:, 0:2 * w],
                              tb(jj)[:, 0:2 * w], ALU.add)
            done += 1
        v = ctr.bump(eng.engine_nop(), 1)
        eng.wait_ge(ctr.sem, v)
        eng.tensor_tensor(accs[0][:, 0:2 * w], accs[0][:, 0:2 * w],
                          accs[1][:, 0:2 * w], ALU.add)
        v = ctr.bump(eng.engine_nop(), 1)
        eng.wait_ge(ctr.sem, v)
        eng.tensor_tensor(whlh[:, l0:l0 + w], accs[0][:, 0:w],
                          sclh[:, l0:l0 + w], ALU.mult)
        ctr.bump(eng.tensor_tensor(whlh[:, 2048 + l0:2048 + l0 + w],
                                   accs[0][:, w:2 * w],
                                   scll[:, l0:l0 + w], ALU.mult), 1)

    def p0_dequant(self):
        nc = self.nc
        qm, gm, sDV, sPL, sCC = self.qm, self.gm, self.sDV, self.sPL, self.sCC
        CW = 512
        HW2 = 2048
        ctx0 = ExitStack()
        sb = lambda n, s, d: ctx0.enter_context(nc.sbuf_tensor(n, s, d))
        pak = [sb(f"pak{i}", [128, D], I32) for i in range(2)]
        nibi = sb("nibi", [128, 2 * D], I32)
        nibb = [sb(f"nibb{i}", [128, 2 * D], BF16) for i in range(2)]
        whlh = sb("whlh", [128, 2 * HW2], BF16)
        sclh = sb("sclh", [128, HW2], F32)
        scll = sb("scll", [128, HW2], F32)

        def mkset(tag):
            terms = [sb(f"{tag}_t{i}", [128, 2 * CW], F32) for i in range(4)]
            accs = [sb(f"{tag}_a{i}", [128, 2 * CW], F32) for i in range(2)]
            return (terms, accs)

        set_dv = mkset("ddv")
        set_pl = mkset("dpl")

        qb, gb = qm.n, gm.n
        db, plb = sDV.n, sPL.n
        NDS, NPS = 3, 1
        ext_t = lambda ci: db + ci + 1
        dv_done = lambda s: db + 2 + 4 * NDS * (s + 1)
        pl_done = lambda s: plb + 4 * NPS * (s + 1)
        q_pak = lambda ci: qb + 16 * (ci + 1)
        q_scl = lambda s: qb + 32 + 32 * (s + 1)
        gm_st = lambda s: gb + 32 * (s + 1)

        with nc.Block("p0") as b:
            @b.sync
            def _(sync):
                for ci in range(2):
                    qm.bump(sync.dma_start(pak[ci][:, :],
                                           self.pakT[ci * 128:(ci + 1) * 128, :]), 16)
                for s in range(4):
                    hf, ci = s // 2, s % 2
                    if s >= 1:
                        sync.wait_ge(sDV.sem, dv_done(s - 1))
                        sync.wait_ge(sPL.sem, pl_done(s - 1))
                    qm.bump(sync.dma_start(
                        sclh[:, :],
                        self.sclx[ci * 256:(ci + 1) * 256:2,
                                  hf * HW2:(hf + 1) * HW2]), 16)
                    qm.bump(sync.dma_start(
                        scll[:, :],
                        self.sclx[ci * 256 + 1:(ci + 1) * 256:2,
                                  hf * HW2:(hf + 1) * HW2]), 16)

            @b.vector
            def _(v):
                for ci in range(2):
                    v.wait_ge(qm.sem, q_pak(ci))
                    v.tensor_scalar(nibi[:, 0:D], pak[ci][:, :], 4, 15,
                                    ALU.logical_shift_right, ALU.bitwise_and)
                    v.tensor_scalar(nibi[:, D:2 * D], pak[ci][:, :], 15, None,
                                    ALU.bitwise_and)
                    v.tensor_copy(nibb[ci][:, 0:D], nibi[:, 0:D])
                    sDV.bump(v.tensor_copy(nibb[ci][:, D:2 * D],
                                           nibi[:, D:2 * D]), 1)
                for s in range(4):
                    hf, ci = s // 2, s % 2
                    for k in range(NDS):
                        self._deq_slab(v, sDV, set_dv, nibb[ci],
                                       hf * HW2 + k * CW, hf * HW2 + (k + 1) * CW,
                                       k * CW, whlh, sclh, scll,
                                       q_scl(s), ext_t(ci),
                                       gm_st(s - 1) if s >= 1 else None)

            @b.gpsimd
            def _(g):
                for s in range(4):
                    hf, ci = s // 2, s % 2
                    for k in range(NPS):
                        self._deq_slab(g, sPL, set_pl, nibb[ci],
                                       hf * HW2 + (NDS + k) * CW,
                                       hf * HW2 + (NDS + k + 1) * CW,
                                       (NDS + k) * CW, whlh, sclh, scll,
                                       q_scl(s), ext_t(ci),
                                       gm_st(s - 1) if s >= 1 else None)
                    g.wait_ge(sDV.sem, dv_done(s))
                    g.wait_ge(sPL.sem, pl_done(s))
                    base = ci * 256
                    gm.bump(g.dma_start(
                        self.wdq_inH[hf][base:base + 256:2, :],
                        whlh[:, 0:HW2]), 16)
                    gm.bump(g.dma_start(
                        self.wdq_inH[hf][base + 1:base + 256:2, :],
                        whlh[:, HW2:2 * HW2]), 16)
                    if ci == 1:
                        g.wait_ge(gm.sem, gm_st(s))
                        sCC.bump(g.collective_compute(
                            "AllGather", ALU.bypass,
                            replica_groups=[list(range(8))],
                            ins=[self.wdq_inH[hf].opt()],
                            outs=[self.wdqH[hf].opt()]), 1)
        ctx0.close()
        self.cc_wdqL = sCC.n - 1
        self.cc_wdqR = sCC.n

    def ln_phase(self, name, src, dst, guards=(), src2=None, sum_dst=None):
        nc = self.nc
        qx, qw, gs, sEV, sDV = self.qx, self.qw, self.gs, self.sEV, self.sDV
        nt = self.TO // 128
        KI = 5 if src2 is not None else 4
        NSR = 2 if src2 is not None else 1
        ctx0 = ExitStack()
        sb = lambda n, s, d: ctx0.enter_context(nc.sbuf_tensor(n, s, d))
        src_halves = isinstance(src, list)
        sdt = (src[0] if src_halves else src).dtype
        src_dt = F32 if sdt == F32 else BF16
        xln = [sb(f"{name}_x{i}", [128, D], src_dt) for i in range(2)]
        xln2 = [sb(f"{name}_y{i}", [128, D], F32) for i in range(2)] if src2 is not None else None
        xsum = xln if src2 is None else [sb(f"{name}_u{i}", [128, D], F32) for i in range(2)]
        xa = [sb(f"{name}_a{i}", [128, D], BF16) for i in range(2)]
        stats = [sb(f"{name}_s{i}", [128, 48], F32) for i in range(2)]
        aggr = [sb(f"{name}_g{i}", [128, 2], F32) for i in range(2)]
        lnv = [sb(f"{name}_l{i}", [128, 1], F32) for i in range(2)]
        rstd = [sb(f"{name}_r{i}", [128, 1], F32) for i in range(2)]
        nmr = [sb(f"{name}_n{i}", [128, 1], F32) for i in range(2)]
        eb, db = sEV.n, sDV.n
        qxb = [qx[0].n, qx[1].n]
        qwb = [qw[0].n, qw[1].n]
        gsb = [gs[0].n, gs[1].n]
        NLD = 2 if src_halves else 1
        xt = lambda i: qxb[i % 2] + 16 * NLD * (i // 2 + 1)
        x2t = lambda i: qwb[i % 2] + 16 * (i // 2 + 1)
        st = lambda i: gsb[i % 2] + 16 * NSR * (i // 2 + 1)
        d_sum = lambda i: db + KI * i + 1
        d_statsnop = lambda i: db + KI * i + KI - 3
        d_aggr = lambda i: db + KI * i + KI - 2
        d_nmrnop = lambda i: db + KI * i + KI - 1
        d_norm = lambda i: db + KI * i + KI
        with nc.Block(name) as b:
            @b.sync
            def _(sync):
                self.flush(sync, qx + (qw if src2 is not None else []))
                for gd in guards:
                    sync.wait_ge(gd[0].sem, gd[1])
                for i in range(nt):
                    if i >= 2:
                        sync.wait_ge(sDV.sem, d_norm(i - 2))
                    if src_halves:
                        qx[i % 2].bump(sync.dma_start(
                            xln[i % 2][:, 0:D // 2],
                            src[0][i * 128:(i + 1) * 128, :]), 16)
                        qx[i % 2].bump(sync.dma_start(
                            xln[i % 2][:, D // 2:D],
                            src[1][i * 128:(i + 1) * 128, :]), 16)
                    else:
                        qx[i % 2].bump(sync.dma_start(
                            xln[i % 2][:, :], src[i * 128:(i + 1) * 128, :]), 16)
                    if src2 is not None:
                        qw[i % 2].bump(sync.dma_start(
                            xln2[i % 2][:, :], src2[i * 128:(i + 1) * 128, :]), 16)

            @b.vector
            def _(v):
                for i in range(nt):
                    p = i % 2
                    v.wait_ge(qx[p].sem, xt(i))
                    if src2 is not None:
                        v.wait_ge(qw[p].sem, x2t(i))
                        if i >= 2:
                            v.wait_ge(gs[p].sem, st(i - 2))
                        w = sDV.bump(v.tensor_tensor(xsum[p][:, :], xln[p][:, :],
                                                     xln2[p][:, :], ALU.add), 1)
                        v.wait_ge(sDV.sem, w)
                    for c in range(8):
                        v.bn_stats(stats[p][:, 6 * c:6 * c + 6],
                                   xsum[p][:, 512 * c:512 * (c + 1)])
                    w = sDV.bump(v.engine_nop(), 1)
                    v.wait_ge(sDV.sem, w)
                    sDV.bump(v.bn_aggr(aggr[p][:, :], stats[p][:, :]), 1)
                    v.wait_ge(sEV.sem, eb + 2 * i + 2)
                    v.scalar_tensor_tensor(nmr[p][:, :], aggr[p][:, 0:1], -1.0,
                                           rstd[p][:, 0:1], ALU.mult, ALU.mult)
                    w = sDV.bump(v.engine_nop(), 1)
                    v.wait_ge(sDV.sem, w)
                    if i >= 2:
                        v.wait_ge(gs[p].sem, st(i - 2))
                    sDV.bump(v.tensor_scalar(xa[p][:, :], xsum[p][:, :],
                                             rstd[p][:, 0:1], nmr[p][:, 0:1],
                                             ALU.mult, ALU.add), 1)

            @b.scalar
            def _(s):
                for i in range(nt):
                    p = i % 2
                    s.wait_ge(sDV.sem, d_aggr(i))
                    w = sEV.bump(s.activation(lnv[p][:, :], aggr[p][:, 1:2],
                                              AF.Ln, bias=EPS_F), 1)
                    s.wait_ge(sEV.sem, w)
                    sEV.bump(s.activation(rstd[p][:, :], lnv[p][:, :],
                                          AF.Exp, scale=np.float32(-0.5)), 1)

            @b.gpsimd
            def _(g):
                self.flush(g, gs)
                for i in range(nt):
                    g.wait_ge(sDV.sem, d_norm(i))
                    if src2 is not None:
                        gs[i % 2].bump(g.dma_start(
                            sum_dst[i * 128:(i + 1) * 128, :], xsum[i % 2][:, :]), 16)
                    gs[i % 2].bump(g.dma_start(
                        dst[i * 128:(i + 1) * 128, :], xa[i % 2][:, :]), 16)
        ctx0.close()
        self.last_ln_dve = sDV.n
        return [(gs[0], gs[0].n), (gs[1], gs[1].n)]

    def transpose_resident(self, name, src, resident, rows, src_ready):
        nc = self.nc
        am = self.am
        with nc.Block(name) as b:
            @b.scalar
            def _(s):
                self.flush(s, [am])
                for rd in src_ready:
                    s.wait_ge(rd[0].sem, rd[1])
                for kc in range(KC):
                    am.bump(s.dma_start(resident[:, kc * rows:(kc + 1) * rows],
                                        src[0:rows, kc * 128:(kc + 1) * 128],
                                        transpose=True), 16)
        return [(am, am.n)]

    def gemm_token(self, name, resident, res_ready, wsrc_kind, wsrc, resid_src,
                   out_dst, scale, sync_guard=(), rows=None, nkc=KC, out_dt=F32,
                   strip_guards=None):
        nc = self.nc
        qx, qw, gw, asx = self.qx, self.qw, self.gw, self.asx
        sMM, sDV = self.sMM, self.sDV
        rows = rows if rows is not None else self.TO
        MC = rows // 128
        NS = D // 512
        ctx0 = ExitStack()
        sb = lambda n, s, d: ctx0.enter_context(nc.sbuf_tensor(n, s, d))
        wst = [sb(f"{name}_w{i}", [128, nkc * 512], BF16) for i in range(2)]
        rbuf = [sb(f"{name}_r{i}", [128, 512], F32) for i in range(2)] \
            if resid_src is not None else None
        obuf = [sb(f"{name}_o{i}", [128, 512], out_dt) for i in range(2)]
        ps = [ctx0.enter_context(nc.psum_tensor(f"{name}_p{i}", [128, 512], F32))
              for i in range(2)]
        mb, db = sMM.n, sDV.n
        qwb = [qw[0].n, qw[1].n]
        gwb = [gw[0].n, gw[1].n]
        qxb = [qx[0].n, qx[1].n]
        asb = [asx[0].n, asx[1].n]
        strip_t = lambda ns: (qwb if wsrc_kind == 'bf16' else gwb)[ns % 2] + 16 * (ns // 2 + 1)
        resid_t = lambda g: qxb[g % 2] + 16 * (g // 2 + 1)
        store_t = lambda g: asb[g % 2] + 16 * (g // 2 + 1)

        def strip_dma(eng, ns):
            if isinstance(wsrc, list):
                nh = NS // 2
                w_ap = wsrc[ns // nh][:, (ns % nh) * 512:((ns % nh) + 1) * 512]
            else:
                w_ap = wsrc[:, ns * 512:(ns + 1) * 512]
            return eng.dma_start(
                wst[ns % 2][:, :].rearrange("p (kc n) -> p kc n", kc=nkc),
                w_ap.rearrange("(kc p) n -> p kc n", p=128))

        with nc.Block(name) as b:
            @b.sync
            def _(sync):
                self.flush(sync, qx + (qw if wsrc_kind == 'bf16' else []))
                for gd in sync_guard:
                    sync.wait_ge(gd[0].sem, gd[1])
                for ns in range(NS):
                    if wsrc_kind == 'bf16':
                        if strip_guards is not None:
                            gd = strip_guards[ns // (NS // 2)]
                            sync.wait_ge(gd[0].sem, gd[1])
                        if ns >= 2:
                            sync.wait_ge(sMM.sem, mb + (ns - 1) * MC)
                        qw[ns % 2].bump(strip_dma(sync, ns), 16)
                    if resid_src is not None:
                        for m in range(MC):
                            g = ns * MC + m
                            if g >= 2:
                                sync.wait_ge(sDV.sem, db + g - 1)
                            qx[g % 2].bump(sync.dma_start(
                                rbuf[g % 2][:, :],
                                resid_src[m * 128:(m + 1) * 128,
                                          ns * 512:(ns + 1) * 512]), 16)

            if wsrc_kind == 'f32':
                @b.gpsimd
                def _(gp):
                    self.flush(gp, gw)
                    for ns in range(NS):
                        if ns >= 2:
                            gp.wait_ge(sMM.sem, mb + (ns - 1) * MC)
                        gw[ns % 2].bump(strip_dma(gp, ns), 16)

            @b.tensor
            def _(t):
                for rd in res_ready:
                    t.wait_ge(rd[0].sem, rd[1])
                for ns in range(NS):
                    sctr = (qw if wsrc_kind == 'bf16' else gw)[ns % 2]
                    t.wait_ge(sctr.sem, strip_t(ns))
                    for m in range(MC):
                        g = ns * MC + m
                        if g >= 2:
                            t.wait_ge(sDV.sem, db + g - 1)
                        for k in range(nkc):
                            inst = t.matmul(
                                ps[g % 2][:, :],
                                resident[:, k * rows + m * 128: k * rows + (m + 1) * 128],
                                wst[ns % 2][:, k * 512:(k + 1) * 512],
                                start=(k == 0), stop=(k == nkc - 1))
                        sMM.bump(inst, 1)

            @b.vector
            def _(v):
                for ns in range(NS):
                    for m in range(MC):
                        g = ns * MC + m
                        v.wait_ge(sMM.sem, mb + g + 1)
                        if resid_src is not None:
                            v.wait_ge(qx[g % 2].sem, resid_t(g))
                        if g >= 2:
                            v.wait_ge(asx[g % 2].sem, store_t(g - 2))
                        if resid_src is not None:
                            sDV.bump(v.scalar_tensor_tensor(
                                obuf[g % 2][:, :], ps[g % 2][:, :], scale,
                                rbuf[g % 2][:, :], ALU.mult, ALU.add), 1)
                        else:
                            sDV.bump(v.tensor_scalar(
                                obuf[g % 2][:, :], ps[g % 2][:, :], scale, None,
                                ALU.mult), 1)

            @b.scalar
            def _(s):
                self.flush(s, asx)
                for ns in range(NS):
                    if isinstance(out_dst, list):
                        nh = NS // 2
                        od = out_dst[ns // nh]
                        oc = (ns % nh) * 512
                    else:
                        od = out_dst
                        oc = ns * 512
                    for m in range(MC):
                        g = ns * MC + m
                        s.wait_ge(sDV.sem, db + g + 1)
                        asx[g % 2].bump(s.dma_start(
                            od[m * 128:(m + 1) * 128, oc:oc + 512],
                            obuf[g % 2][:, :]), 16)
        ctx0.close()
        return [(asx[0], asx[0].n), (asx[1], asx[1].n)]

    def p2c_half(self, th, xbT, xbT_ready):
        nc = self.nc
        T, TH = self.T, self.T // 2
        FO, NH = self.FO, self.NH
        qm, gw, asx = self.qm, self.gw, self.asx
        sMM, sEV, sDV = self.sMM, self.sEV, self.sDV
        tc0 = th * TH

        jobs = [("k", hl) for hl in range(NH)] + [("q", hl) for hl in range(NH)]
        sin_ready = []
        ctxb = ExitStack()
        sbb = lambda n, s, d: ctxb.enter_context(nc.sbuf_tensor(n, s, d))
        sinb = sbb(f"sin{th}", [128, TH], BF16)
        cosb = sbb(f"cos{th}", [128, TH], BF16)
        wst = [sbb(f"j{th}w{i}", [128, KC * HD], BF16) for i in range(3)]
        kqo = [sbb(f"j{th}o{i}", [128, TH], BF16) for i in range(2)]
        rt = [sbb(f"j{th}t{i}", [64, TH], F32) for i in range(4)]
        psK = [ctxb.enter_context(nc.psum_tensor(f"j{th}p{i}", [128, TH], F32))
               for i in range(2)]
        mb, db = sMM.n, sDV.n
        gwb = [gw[0].n, gw[1].n, gw[2].n]
        asb = [asx[0].n, asx[1].n]
        strip_t = lambda i: gwb[i % 3] + 16 * (i // 3 + 1)
        store_t = lambda i: asb[i % 2] + 16 * (i // 2 + 1)

        gm = self.gm
        with nc.Block(f"p2b{th}") as b:
            @b.gpsimd
            def _(gp):
                self.flush(gp, [gm])
                gm.bump(gp.dma_start(sinb[:, :],
                                     self.sinTd[:, tc0:tc0 + TH]), 16)
                gm.bump(gp.dma_start(cosb[:, :],
                                     self.cosTd[:, tc0:tc0 + TH]), 16)
                sin_ready.append((gm, gm.n))
                self.flush(gp, gw)
                for i, (kind, hl) in enumerate(jobs):
                    wsrc = self.wkT if kind == "k" else self.wqT
                    if i >= 3:
                        gp.wait_ge(sMM.sem, mb + i - 2)
                    gw[i % 3].bump(gp.dma_start(
                        wst[i % 3][:, :].rearrange("p (kc n) -> p kc n", kc=KC),
                        wsrc[:, hl * HD:(hl + 1) * HD].rearrange(
                            "(kc p) n -> p kc n", p=128)), 16)

            @b.tensor
            def _(t):
                for rd in xbT_ready:
                    t.wait_ge(rd[0].sem, rd[1])
                for i, (kind, hl) in enumerate(jobs):
                    t.wait_ge(gw[i % 3].sem, strip_t(i))
                    if i >= 2:
                        t.wait_ge(sDV.sem, db + i - 1)
                    for k in range(KC):
                        for nb in range(TH // 512):
                            inst = t.matmul(
                                psK[i % 2][:, nb * 512:(nb + 1) * 512],
                                wst[i % 3][:, k * HD:(k + 1) * HD],
                                xbT[:, k * TH + nb * 512: k * TH + (nb + 1) * 512],
                                start=(k == 0), stop=(k == KC - 1))
                    sMM.bump(inst, 1)

            @b.vector
            def _(v):
                for rd in sin_ready:
                    v.wait_ge(rd[0].sem, rd[1])
                for i, (kind, hl) in enumerate(jobs):
                    p = i % 2
                    v.wait_ge(sMM.sem, mb + i + 1)
                    if i >= 2:
                        v.wait_ge(asx[p].sem, store_t(i - 2))
                    pk = psK[p]
                    v.tensor_tensor(rt[0][:, :], sinb[0:64, :], pk[64:128, :], ALU.mult)
                    v.tensor_tensor(rt[1][:, :], cosb[0:64, :], pk[0:64, :], ALU.mult)
                    v.tensor_tensor(rt[2][:, :], sinb[64:128, :], pk[0:64, :], ALU.mult)
                    v.tensor_tensor(rt[3][:, :], cosb[64:128, :], pk[64:128, :], ALU.mult)
                    v.tensor_tensor(kqo[p][0:64, :], rt[1][:, :], rt[0][:, :], ALU.subtract)
                    sDV.bump(v.tensor_tensor(kqo[p][64:128, :], rt[3][:, :],
                                             rt[2][:, :], ALU.add), 1)

            @b.scalar
            def _(s):
                self.flush(s, asx)
                for i, (kind, hl) in enumerate(jobs):
                    s.wait_ge(sDV.sem, db + i + 1)
                    dst = self.kts if kind == "k" else self.qts
                    asx[i % 2].bump(s.dma_start(
                        dst[hl * HD:(hl + 1) * HD, tc0:tc0 + TH],
                        kqo[i % 2][:, :]), 16)
        ctxb.close()

        ctxc = ExitStack()
        sbc = lambda n, s, d: ctxc.enter_context(nc.sbuf_tensor(n, s, d))
        vw = [sbc(f"v{th}w{i}", [128, KC * 512], BF16) for i in range(2)]
        vo = [sbc(f"v{th}o{i}", [128, 512], BF16) for i in range(2)]
        psV = [ctxc.enter_context(nc.psum_tensor(f"v{th}p{i}", [128, 512], F32))
               for i in range(2)]
        NSv = FO // 512
        MCv = TH // 128
        mb2, eb2 = sMM.n, sEV.n
        gwb2 = [gw[0].n, gw[1].n]
        asb2 = [asx[0].n, asx[1].n]
        vstrip_t = lambda ns: gwb2[ns % 2] + 16 * (ns // 2 + 1)
        vstore_t = lambda g: asb2[g % 2] + 16 * (g // 2 + 1)
        with nc.Block(f"p2v{th}") as b:
            @b.gpsimd
            def _(gp):
                self.flush(gp, gw)
                for ns in range(NSv):
                    if ns >= 2:
                        gp.wait_ge(sMM.sem, mb2 + (ns - 1) * MCv)
                    gw[ns % 2].bump(gp.dma_start(
                        vw[ns % 2][:, :].rearrange("p (kc n) -> p kc n", kc=KC),
                        self.wvT[:, ns * 512:(ns + 1) * 512].rearrange(
                            "(kc p) n -> p kc n", p=128)), 16)

            @b.tensor
            def _(t):
                for ns in range(NSv):
                    t.wait_ge(gw[ns % 2].sem, vstrip_t(ns))
                    for m in range(MCv):
                        g = ns * MCv + m
                        if g >= 2:
                            t.wait_ge(sEV.sem, eb2 + g - 1)
                        for k in range(KC):
                            inst = t.matmul(
                                psV[g % 2][:, :],
                                xbT[:, k * TH + m * 128: k * TH + (m + 1) * 128],
                                vw[ns % 2][:, k * 512:(k + 1) * 512],
                                start=(k == 0), stop=(k == KC - 1))
                        sMM.bump(inst, 1)

            @b.scalar
            def _(s):
                self.flush(s, asx)
                for ns in range(NSv):
                    for m in range(MCv):
                        g = ns * MCv + m
                        s.wait_ge(sMM.sem, mb2 + g + 1)
                        if g >= 2:
                            s.wait_ge(asx[g % 2].sem, vstore_t(g - 2))
                        w = sEV.bump(s.copy(vo[g % 2][:, :], psV[g % 2][:, :]), 1)
                        s.wait_ge(sEV.sem, w)
                        asx[g % 2].bump(s.dma_start(
                            self.vts[tc0 + m * 128:tc0 + (m + 1) * 128,
                                     ns * 512:(ns + 1) * 512],
                            vo[g % 2][:, :]), 16)
        ctxc.close()

    def p3_attention(self, max_groups=99, max_qt=99):
        nc = self.nc
        T, NH, FO = self.T, self.NH, self.FO
        NQ = min(self.NQ, max_qt)
        qm, qv, asx = self.qm, self.qv, self.asx
        sMM, sEV, sDV, sPL = self.sMM, self.sEV, self.sDV, self.sPL
        groups = [list(range(0, 6)), list(range(6, 12)), list(range(12, 16))]
        groups = groups[:max_groups]
        JPQ = QTW // 128
        NC8 = T // 512
        ctx0 = ExitStack()
        sb = lambda n, s, d: ctx0.enter_context(nc.sbuf_tensor(n, s, d))
        Ksb = [sb(f"at_k{i}", [128, T], BF16) for i in range(6)]
        Qful = [sb(f"at_q{i}", [128, T], BF16) for i in range(6)]
        Vsb = [sb(f"at_v{i}", [128, FO], BF16) for i in range(3)]
        Esb = [sb(f"at_e{i}", [128, 512], BF16) for i in range(6)]
        Zac = [[sb(f"at_z{i}_{p}", [128, 512], F32) for p in range(2)] for i in range(6)]
        Zfin = [sb(f"at_zf{i}", [128, 512], F32) for i in range(6)]
        Qsq = sb("at_qs", [128, T], BF16)
        Ksq = sb("at_ks", [128, T], BF16)
        qnr = sb("at_qn", [1, 512], F32)
        knr = sb("at_kn", [1, 512], F32)
        tmpmx = sb("at_tm", [1, 2], F32)
        mq2 = sb("at_mq2", [1, 6], F32)
        mk2 = sb("at_mk2", [1, 6], F32)
        sumb = sb("at_sumb", [1, 6], F32)
        negbf = sb("at_negbf", [1, 6], F32)
        negbc = [sb(f"at_nb{i}", [128, 1], F32) for i in range(6)]
        zrow = [sb(f"at_zr{i}", [1, 512], F32) for i in range(6)]
        rzrow = [sb(f"at_rz{i}", [1, 512], F32) for i in range(6)]
        rzbc = [sb(f"at_rb{i}", [128, 512], F32) for i in range(2)]
        aout = [sb(f"at_ao{i}", [128, 512], BF16) for i in range(2)]
        psO = [ctx0.enter_context(nc.psum_tensor(f"at_pO{i}", [128, 512], F32))
               for i in range(6)]
        psS = [ctx0.enter_context(nc.psum_tensor(f"at_pS{i}", [128, 512], F32))
               for i in range(2)]

        def round_seq(H):
            seq = []
            for sl in range(H):
                seq.append(("S", sl))
                if sl >= 2:
                    seq.append(("PV", sl - 2))
            seq.append(("PV", H - 2))
            seq.append(("PV", H - 1))
            return seq

        zeng = lambda slot: "dve" if slot % 2 == 0 else "pl"

        C = {}
        q = {"pe": sMM.n, "act": sEV.n, "dve": sDV.n, "pl": sPL.n,
             "qm": qm.n, "qv": [qv[0].n, qv[1].n, qv[2].n],
             "as": [asx[0].n, asx[1].n]}
        last_sto = {0: None, 1: None}
        all_rounds = []
        for gi, hg in enumerate(groups):
            H = len(hg)
            q["qm"] += 16 * 2 * H
            C[("kq", gi)] = q["qm"]
            for sl in range(H):
                for c in range(NC8):
                    q["pe"] += 1
                    C[("qnc", gi, sl, c)] = q["pe"]
                    q["pe"] += 1
                    C[("knc", gi, sl, c)] = q["pe"]
            for sl in range(H):
                q["pe"] += 1
                C[("nb", gi, sl)] = q["pe"]
            for sl in range(H):
                for c in range(NC8):
                    q["act"] += 1
                    C[("eqc", gi, sl, c)] = q["act"]
                    q["act"] += 1
                    C[("ekc", gi, sl, c)] = q["act"]
            for sl in range(H):
                q["act"] += 1
                C[("enb", gi, sl)] = q["act"]
            for sl in range(H):
                q["dve"] += 1
                C[("dqs", gi, sl)] = q["dve"]
                q["dve"] += 1
                C[("dks", gi, sl)] = q["dve"]
                for c in range(NC8):
                    q["dve"] += 1
                    C[("drq", gi, sl, c)] = q["dve"]
                    q["dve"] += 1
                    C[("drk", gi, sl, c)] = q["dve"]
            for sl in range(H):
                q["dve"] += 1
                C[("dnb", gi, sl)] = q["dve"]
            for qt in range(NQ):
                J = (qt + 1) * JPQ
                for jc in range(J):
                    sl3 = jc % 3
                    q["qv"][sl3] += 16
                    C[("ldv", gi, qt, jc)] = (sl3, q["qv"][sl3])
                    all_rounds.append((gi, qt, jc, H))
                for jc in range(J):
                    for kind, sl in round_seq(H):
                        q["pe"] += 1
                        C[(kind, gi, qt, jc, sl)] = q["pe"]
                for sl in range(H):
                    q["pe"] += 1
                    C[("zmm", gi, qt, sl)] = q["pe"]
                for sl in range(H):
                    q["pe"] += 1
                    C[("rzb", gi, qt, sl)] = q["pe"]
                for jc in range(J):
                    for sl in range(H):
                        q["act"] += 1
                        C[("exp", gi, qt, jc, sl)] = q["act"]
                for sl in range(H):
                    q["act"] += 1
                    C[("ezr", gi, qt, sl)] = q["act"]
                for sl in range(H):
                    q["act"] += 1
                    C[("erz", gi, qt, sl)] = q["act"]
                    p = sl % 2
                    q["as"][p] += 16
                    C[("sto", gi, qt, sl)] = (p, q["as"][p])
                for jc in range(J):
                    diag = jc >= qt * JPQ
                    for sl in range(H):
                        if diag:
                            q["dve"] += 1
                            C[("pm", gi, qt, jc, sl)] = q["dve"]
                        e = zeng(sl)
                        q[e] += 1
                        C[("za", gi, qt, jc, sl)] = q[e]
                for sl in range(H):
                    e = zeng(sl)
                    q[e] += 1
                    C[("zf", gi, qt, sl)] = q[e]
                for sl in range(H):
                    q["dve"] += 1
                    C[("rcp", gi, qt, sl)] = q["dve"]
                for sl in range(H):
                    q["dve"] += 1
                    C[("emu", gi, qt, sl)] = q["dve"]
                for sl in range(H):
                    C[("aoutw", gi, qt, sl)] = last_sto[sl % 2]
                    last_sto[sl % 2] = C[("sto", gi, qt, sl)]

        vprev = {}
        by_slot = {0: None, 1: None, 2: None}
        for (gi, qt, jc, H) in all_rounds:
            vprev[(gi, qt, jc)] = by_slot[jc % 3]
            by_slot[jc % 3] = (gi, qt, jc, H)

        sem_of = {"pe": sMM, "act": sEV, "dve": sDV, "pl": sPL}

        def prev_group_pe_done(gi):
            Hp = len(groups[gi - 1])
            return C[("rzb", gi - 1, NQ - 1, Hp - 1)]

        with nc.Block("p3") as b:
            @b.sync
            def _(sync):
                self.flush(sync, [qm] + qv)
                for gd in self.p2_stores_final:
                    sync.wait_ge(gd[0].sem, gd[1])
                for gi, hg in enumerate(groups):
                    H = len(hg)
                    if gi >= 1:
                        sync.wait_ge(sMM.sem, prev_group_pe_done(gi))
                    for sl in range(H):
                        hl = hg[sl]
                        qm.bump(sync.dma_start(
                            Ksb[sl][:, :], self.kts[hl * HD:(hl + 1) * HD, :]), 16)
                        qm.bump(sync.dma_start(
                            Qful[sl][:, :], self.qts[hl * HD:(hl + 1) * HD, :]), 16)
                    assert qm.n == C[("kq", gi)]
                    for qt in range(NQ):
                        J = (qt + 1) * JPQ
                        for jc in range(J):
                            pv = vprev[(gi, qt, jc)]
                            if pv is not None:
                                pgi, pqt, pjc, pH = pv
                                sync.wait_ge(sMM.sem,
                                             C[("PV", pgi, pqt, pjc, pH - 1)])
                            sl3, tgt = C[("ldv", gi, qt, jc)]
                            qv[sl3].bump(sync.dma_start(
                                Vsb[sl3][:, :],
                                self.vts[jc * 128:(jc + 1) * 128, :]), 16)
                            assert qv[sl3].n == tgt

            @b.tensor
            def _(t):
                psS_last_evac = [(None, 0), (None, 0)]
                psO_free = [None] * 6
                for gi, hg in enumerate(groups):
                    H = len(hg)
                    for bank in range(2):
                        if psO_free[bank] is not None:
                            e, cval = psO_free[bank]
                            t.wait_ge(sem_of[e].sem, cval)
                    for sl in range(H):
                        t.wait_ge(sDV.sem, C[("dqs", gi, sl)])
                        t.wait_ge(sDV.sem, C[("dks", gi, sl)])
                        for c in range(NC8):
                            if (sl, c) != (0, 0):
                                pc = (sl, c - 1) if c > 0 else (sl - 1, NC8 - 1)
                                t.wait_ge(sEV.sem, C[("eqc", gi) + pc])
                            t.matmul(psO[0][0:1, :], self.ones_col_bf[:, :],
                                     Qsq[:, c * 512:(c + 1) * 512],
                                     start=True, stop=True).then_inc(sMM.sem, 1)
                            if (sl, c) != (0, 0):
                                pc = (sl, c - 1) if c > 0 else (sl - 1, NC8 - 1)
                                t.wait_ge(sEV.sem, C[("ekc", gi) + pc])
                            t.matmul(psO[1][0:1, :], self.ones_col_bf[:, :],
                                     Ksq[:, c * 512:(c + 1) * 512],
                                     start=True, stop=True).then_inc(sMM.sem, 1)
                    for sl in range(H):
                        if sl == 0 and psO_free[2] is not None:
                            e, cval = psO_free[2]
                            t.wait_ge(sem_of[e].sem, cval)
                        t.wait_ge(sDV.sem, C[("dnb", gi, sl)])
                        if sl >= 1:
                            t.wait_ge(sEV.sem, C[("enb", gi, sl - 1)])
                        t.matmul(psO[2][:, 0:1], self.ones_row[:, :],
                                 negbf[0:1, sl:sl + 1], start=True, stop=True
                                 ).then_inc(sMM.sem, 1)
                    for qt in range(NQ):
                        J = (qt + 1) * JPQ
                        for jc in range(J):
                            for kind, sl in round_seq(H):
                                if kind == "S":
                                    e, cval = psS_last_evac[sl % 2]
                                    if e is not None:
                                        t.wait_ge(sem_of[e].sem, cval)
                                    t.wait_ge(qm.sem, C[("kq", gi)])
                                    t.matmul(psS[sl % 2][:, :],
                                             Ksb[sl][:, jc * 128:(jc + 1) * 128],
                                             Qful[sl][:, qt * QTW:(qt + 1) * QTW],
                                             start=True, stop=True
                                             ).then_inc(sMM.sem, 1)
                                    psS_last_evac[sl % 2] = (
                                        "act", C[("exp", gi, qt, jc, sl)])
                                else:
                                    t.wait_ge(sEV.sem, C[("exp", gi, qt, jc, sl)])
                                    if qt == 0 and jc == 0:
                                        t.wait_ge(sEV.sem, C[("enb", gi, H - 1)])
                                    sl3, tgt = C[("ldv", gi, qt, jc)]
                                    t.wait_ge(qv[sl3].sem, tgt)
                                    hl = hg[sl]
                                    t.matmul(psO[sl][:, :],
                                             Vsb[sl3][:, hl * HD:(hl + 1) * HD],
                                             Esb[sl][:, :],
                                             start=(jc == 0), stop=(jc == J - 1)
                                             ).then_inc(sMM.sem, 1)
                        for sl in range(H):
                            e = zeng(sl)
                            t.wait_ge(sem_of[e].sem, C[("zf", gi, qt, sl)])
                            ep, cval = psS_last_evac[sl % 2]
                            if ep is not None:
                                t.wait_ge(sem_of[ep].sem, cval)
                            t.matmul(psS[sl % 2][0:1, :], self.ones_col[:, :],
                                     Zfin[sl][:, :], start=True, stop=True
                                     ).then_inc(sMM.sem, 1)
                            psS_last_evac[sl % 2] = ("act", C[("ezr", gi, qt, sl)])
                        for sl in range(H):
                            t.wait_ge(sDV.sem, C[("rcp", gi, qt, sl)])
                            ep, cval = psS_last_evac[sl % 2]
                            t.wait_ge(sem_of[ep].sem, cval)
                            t.matmul(psS[sl % 2][:, :], self.ones_row[:, :],
                                     rzrow[sl][0:1, :], start=True, stop=True
                                     ).then_inc(sMM.sem, 1)
                            psS_last_evac[sl % 2] = ("act", C[("erz", gi, qt, sl)])
                        for sl in range(H):
                            psO_free[sl] = ("dve", C[("emu", gi, qt, sl)])
                sMM.n = q["pe"]

            @b.scalar
            def _(s):
                self.flush(s, asx)
                for gi, hg in enumerate(groups):
                    H = len(hg)
                    for sl in range(H):
                        for c in range(NC8):
                            s.wait_ge(sMM.sem, C[("qnc", gi, sl, c)])
                            pq = (sl, c - 1) if c > 0 else \
                                ((sl - 1, NC8 - 1) if sl >= 1 else None)
                            if pq is not None:
                                s.wait_ge(sDV.sem, C[("drq", gi) + pq])
                            s.copy(qnr[0:1, :], psO[0][0:1, :]
                                   ).then_inc(sEV.sem, 1)
                            s.wait_ge(sMM.sem, C[("knc", gi, sl, c)])
                            if pq is not None:
                                s.wait_ge(sDV.sem, C[("drk", gi) + pq])
                            s.copy(knr[0:1, :], psO[1][0:1, :]
                                   ).then_inc(sEV.sem, 1)
                    for sl in range(H):
                        s.wait_ge(sMM.sem, C[("nb", gi, sl)])
                        s.copy(negbc[sl][:, :], psO[2][:, 0:1]
                               ).then_inc(sEV.sem, 1)
                    for qt in range(NQ):
                        J = (qt + 1) * JPQ
                        for jc in range(J):
                            diag = jc >= qt * JPQ
                            for sl in range(H):
                                s.wait_ge(sMM.sem, C[("S", gi, qt, jc, sl)])
                                if diag:
                                    s.wait_ge(sDV.sem, C[("pm", gi, qt, jc, sl)])
                                if jc >= 1:
                                    s.wait_ge(sMM.sem, C[("PV", gi, qt, jc - 1, sl)])
                                    e = zeng(sl)
                                    s.wait_ge(sem_of[e].sem,
                                              C[("za", gi, qt, jc - 1, sl)])
                                elif qt >= 1:
                                    Jp = qt * JPQ
                                    s.wait_ge(sMM.sem, C[("PV", gi, qt - 1, Jp - 1, sl)])
                                    e = zeng(sl)
                                    s.wait_ge(sem_of[e].sem,
                                              C[("za", gi, qt - 1, Jp - 1, sl)])
                                elif gi >= 1 and sl < len(groups[gi - 1]):
                                    Jp = NQ * JPQ
                                    s.wait_ge(sMM.sem,
                                              C[("PV", gi - 1, NQ - 1, Jp - 1, sl)])
                                    e = zeng(sl)
                                    s.wait_ge(sem_of[e].sem,
                                              C[("za", gi - 1, NQ - 1, Jp - 1, sl)])
                                s.activation(Esb[sl][:, :], psS[sl % 2][:, :],
                                             AF.Exp, bias=negbc[sl][:, 0:1],
                                             scale=INV_SQRT_HD).then_inc(sEV.sem, 1)
                        for sl in range(H):
                            s.wait_ge(sMM.sem, C[("zmm", gi, qt, sl)])
                            s.copy(zrow[sl][:, :], psS[sl % 2][0:1, :]
                                   ).then_inc(sEV.sem, 1)
                        for sl in range(H):
                            hl = hg[sl]
                            s.wait_ge(sMM.sem, C[("rzb", gi, qt, sl)])
                            if sl >= 2:
                                s.wait_ge(sDV.sem, C[("emu", gi, qt, sl - 2)])
                            s.copy(rzbc[sl % 2][:, :], psS[sl % 2][:, :]
                                   ).then_inc(sEV.sem, 1)
                            s.wait_ge(sDV.sem, C[("emu", gi, qt, sl)])
                            p, tgt = C[("sto", gi, qt, sl)]
                            asx[p].bump(s.dma_start(
                                self.atns[hl * HD:(hl + 1) * HD,
                                          qt * QTW:(qt + 1) * QTW],
                                aout[sl % 2][:, :]), 16)
                            assert asx[p].n == tgt
                sEV.n = q["act"]

            @b.vector
            def _(v):
                for gi, hg in enumerate(groups):
                    H = len(hg)
                    for sl in range(H):
                        v.wait_ge(qm.sem, C[("kq", gi)])
                        if sl >= 1:
                            v.wait_ge(sMM.sem, C[("qnc", gi, sl - 1, NC8 - 1)])
                            v.wait_ge(sMM.sem, C[("knc", gi, sl - 1, NC8 - 1)])
                        v.tensor_tensor(Qsq[:, :], Qful[sl][:, :],
                                        Qful[sl][:, :], ALU.mult
                                        ).then_inc(sDV.sem, 1)
                        v.tensor_tensor(Ksq[:, :], Ksb[sl][:, :],
                                        Ksb[sl][:, :], ALU.mult
                                        ).then_inc(sDV.sem, 1)
                        for c in range(NC8):
                            v.wait_ge(sEV.sem, C[("eqc", gi, sl, c)])
                            v.reduce_max(tmpmx[0:1, 0:1], qnr[0:1, :], AX
                                         ).then_inc(sDV.sem, 1)
                            v.wait_ge(sEV.sem, C[("ekc", gi, sl, c)])
                            v.reduce_max(tmpmx[0:1, 1:2], knr[0:1, :], AX
                                         ).then_inc(sDV.sem, 1)
                            if c == 0:
                                v.tensor_copy(mq2[0:1, sl:sl + 1],
                                              tmpmx[0:1, 0:1])
                                v.tensor_copy(mk2[0:1, sl:sl + 1],
                                              tmpmx[0:1, 1:2])
                            else:
                                v.tensor_tensor(mq2[0:1, sl:sl + 1],
                                                mq2[0:1, sl:sl + 1],
                                                tmpmx[0:1, 0:1], ALU.max)
                                v.tensor_tensor(mk2[0:1, sl:sl + 1],
                                                mk2[0:1, sl:sl + 1],
                                                tmpmx[0:1, 1:2], ALU.max)
                    for sl in range(H):
                        v.tensor_tensor(sumb[0:1, sl:sl + 1],
                                        mq2[0:1, sl:sl + 1],
                                        mk2[0:1, sl:sl + 1], ALU.add)
                    for sl in range(H):
                        v.tensor_scalar(negbf[0:1, sl:sl + 1],
                                        sumb[0:1, sl:sl + 1], C_BOUND, None,
                                        ALU.mult).then_inc(sDV.sem, 1)
                    for qt in range(NQ):
                        J = (qt + 1) * JPQ
                        for jc in range(J):
                            diag = jc >= qt * JPQ
                            for sl in range(H):
                                if diag:
                                    dlt = jc * 128 - qt * QTW
                                    v.wait_ge(sMM.sem, C[("S", gi, qt, jc, sl)])
                                    v.tensor_tensor(
                                        psS[sl % 2][:, :], psS[sl % 2][:, :],
                                        self.mask_sb[:, 384 - dlt:896 - dlt],
                                        ALU.add).then_inc(sDV.sem, 1)
                                if zeng(sl) == "dve":
                                    v.wait_ge(sEV.sem, C[("exp", gi, qt, jc, sl)])
                                    if jc < 2:
                                        v.tensor_copy(Zac[sl][jc % 2][:, :],
                                                      Esb[sl][:, :]
                                                      ).then_inc(sDV.sem, 1)
                                    else:
                                        v.tensor_tensor(Zac[sl][jc % 2][:, :],
                                                        Esb[sl][:, :],
                                                        Zac[sl][jc % 2][:, :],
                                                        ALU.add).then_inc(sDV.sem, 1)
                        for sl in range(H):
                            if zeng(sl) == "dve":
                                v.tensor_tensor(Zfin[sl][:, :], Zac[sl][0][:, :],
                                                Zac[sl][1][:, :], ALU.add
                                                ).then_inc(sDV.sem, 1)
                        for sl in range(H):
                            v.wait_ge(sEV.sem, C[("ezr", gi, qt, sl)])
                            v.reciprocal(rzrow[sl][:, :], zrow[sl][:, :]
                                         ).then_inc(sDV.sem, 1)
                        for sl in range(H):
                            v.wait_ge(sEV.sem, C[("erz", gi, qt, sl)])
                            v.wait_ge(sMM.sem, C[("PV", gi, qt, J - 1, sl)])
                            if C[("aoutw", gi, qt, sl)] is not None:
                                p, tgt = C[("aoutw", gi, qt, sl)]
                                v.wait_ge(asx[p].sem, tgt)
                            v.tensor_tensor(aout[sl % 2][:, :], psO[sl][:, :],
                                            rzbc[sl % 2][:, :], ALU.mult
                                            ).then_inc(sDV.sem, 1)
                sDV.n = q["dve"]

            @b.gpsimd
            def _(g):
                for gi, hg in enumerate(groups):
                    H = len(hg)
                    for qt in range(NQ):
                        J = (qt + 1) * JPQ
                        for jc in range(J):
                            for sl in range(H):
                                if zeng(sl) != "pl":
                                    continue
                                g.wait_ge(sEV.sem, C[("exp", gi, qt, jc, sl)])
                                if jc < 2:
                                    g.tensor_copy(Zac[sl][jc % 2][:, :],
                                                  Esb[sl][:, :]).then_inc(sPL.sem, 1)
                                else:
                                    g.tensor_tensor(Zac[sl][jc % 2][:, :],
                                                    Esb[sl][:, :],
                                                    Zac[sl][jc % 2][:, :],
                                                    ALU.add).then_inc(sPL.sem, 1)
                        for sl in range(H):
                            if zeng(sl) == "pl":
                                g.tensor_tensor(Zfin[sl][:, :], Zac[sl][0][:, :],
                                                Zac[sl][1][:, :], ALU.add
                                                ).then_inc(sPL.sem, 1)
                sPL.n = q["pl"]
        ctx0.close()
        self.p3_stores_final = [(asx[0], asx[0].n), (asx[1], asx[1].n)]


    def build(self, upto=99):
        nc = self.nc
        T, TO, FO = self.T, self.TO, self.FO
        qm, gm, sCC = self.qm, self.gm, self.sCC

        self.prologue()
        if upto >= 1:
            self.p0_dequant()
        else:
            self.cc_wdqL = self.cc_wdqR = 0
        if upto < 2:
            return self.finish(None)

        ln1_done = self.ln_phase("ln1", self.xin, self.xabs)
        ctxr = ExitStack()
        resident = ctxr.enter_context(nc.sbuf_tensor("res1", [128, KC * TO], BF16))
        rr = self.transpose_resident("tr1", self.xabs, resident, TO, ln1_done)
        p1_done = self.gemm_token("p1", resident, rr, 'bf16', self.wdqH,
                                  self.xin, self.x1s, 0.5,
                                  sync_guard=[(self.sDV, self.last_ln_dve)],
                                  strip_guards=[(sCC, self.cc_wdqL),
                                                (sCC, self.cc_wdqR)])
        ctxr.close()
        if upto < 3:
            return self.finish(p1_done)

        ln2_done = self.ln_phase("ln2", self.x1s, self.xbo, guards=p1_done)
        with nc.Block("p2x") as b:
            @b.gpsimd
            def _(g):
                for gd in ln2_done:
                    g.wait_ge(gd[0].sem, gd[1])
                sCC.bump(g.collective_compute(
                    "AllGather", ALU.bypass,
                    replica_groups=[[0, 1], [2, 3], [4, 5], [6, 7]],
                    ins=[self.xbo.opt()],
                    outs=[self.xbf.opt()]), 1)
        cc_xbf = sCC.n
        if upto < 4:
            return self.finish(None)

        for th in range(2):
            ctxh = ExitStack()
            xbT = ctxh.enter_context(
                nc.sbuf_tensor(f"xbT{th}", [128, KC * (T // 2)], BF16))
            TH = T // 2
            xbT_ready = self.transpose_resident(
                f"tr2{th}", self.xbf[th * TH:(th + 1) * TH, :],
                xbT, TH, [(sCC, cc_xbf)])
            self.p2c_half(th, xbT, xbT_ready)
            ctxh.close()
        self.p2_stores_final = [(self.asx[0], self.asx[0].n),
                                (self.asx[1], self.asx[1].n)]
        if upto < 5:
            return self.finish(None)

        self.p3_attention(max_groups=getattr(self, 'p3_groups', 99),
                          max_qt=getattr(self, 'p3_qt', 99))
        if upto < 6:
            return self.finish(self.p3_stores_final)

        ctxr4 = ExitStack()
        res4 = ctxr4.enter_context(
            nc.sbuf_tensor("res4", [128, (FO // 128) * T], BF16))
        with nc.Block("p4r") as b:
            @b.sync
            def _(sync):
                self.flush(sync, [qm])
                for gd in self.p3_stores_final:
                    sync.wait_ge(gd[0].sem, gd[1])
                for kc in range(FO // 128):
                    qm.bump(sync.dma_start(
                        res4[:, kc * T:(kc + 1) * T],
                        self.atns[kc * 128:(kc + 1) * 128, :]), 16)
        asb4 = [self.asx[0].n, self.asx[1].n]
        p4_done = self.gemm_token("p4", res4, [(qm, qm.n)], 'f32', self.woT,
                                  None, self.op4H, 1.0, rows=T, nkc=FO // 128,
                                  out_dt=BF16)
        ctxr4.close()
        nleft = (D // 512 // 2) * (T // 128) // 2
        left_done = [(self.asx[0], asb4[0] + 16 * nleft),
                     (self.asx[1], asb4[1] + 16 * nleft)]
        with nc.Block("p4s") as b:
            @b.gpsimd
            def _(g):
                for gd in left_done:
                    g.wait_ge(gd[0].sem, gd[1])
                sCC.bump(g.collective_compute(
                    "ReduceScatter", ALU.add,
                    replica_groups=[[0, 1], [2, 3], [4, 5], [6, 7]],
                    ins=[self.op4H[0].opt()],
                    outs=[self.x2pH[0].opt()]), 1)
                for gd in p4_done:
                    g.wait_ge(gd[0].sem, gd[1])
                sCC.bump(g.collective_compute(
                    "ReduceScatter", ALU.add,
                    replica_groups=[[0, 1], [2, 3], [4, 5], [6, 7]],
                    ins=[self.op4H[1].opt()],
                    outs=[self.x2pH[1].opt()]), 1)
        cc_rs = sCC.n
        if upto < 7:
            return self.finish(None)

        ln3_done = self.ln_phase("ln3", self.x2pH, self.xabs,
                                 guards=[(sCC, cc_rs)],
                                 src2=self.x1s, sum_dst=self.x2s)
        ctxr5 = ExitStack()
        res5 = ctxr5.enter_context(nc.sbuf_tensor("res5", [128, KC * TO], BF16))
        rr5 = self.transpose_resident("tr5", self.xabs, res5, TO, ln3_done)
        p5_done = self.gemm_token("p5", res5, rr5, 'bf16', self.wdqH,
                                  self.x2s, self.xout, 0.5,
                                  sync_guard=[(self.sDV, self.last_ln_dve)] + ln3_done)
        ctxr5.close()
        return self.finish(p5_done)

    def finish(self, out_done):
        nc = self.nc
        gm = self.gm
        with nc.Block("fin") as b:
            @b.scalar
            def _(s):
                if out_done is not None:
                    for gd in out_done:
                        s.wait_ge(gd[0].sem, gd[1])

            if self.dbg_views:
                @b.gpsimd
                def _(g):
                    if out_done is not None:
                        for gd in out_done:
                            g.wait_ge(gd[0].sem, gd[1])
                    srcs = {"xbf": self.xbf, "x1s": self.x1s,
                            "x2s": self.x2s, "kts": self.kts, "qts": self.qts,
                            "vts": self.vts, "atns": self.atns}
                    for nm, dst in self.dbg_views.items():
                        gm.bump(g.dma_start(dst[:, :], srcs[nm][:, :]), 16)
                    g.wait_ge(gm.sem, gm.n)
        self.ctx.close()
        return nc


def build_program(T, dbg=False):
    return Builder(T, dbg)


def host_prep(L0_attn_x, L0_attn_wq, L0_attn_wk, L0_attn_wv, L0_attn_wo,
              L0_ffn_weight_packed, L0_ffn_centroids, L0_ffn_scales):
    T = L0_attn_x.shape[1]
    TO = T // 2
    x = np.ascontiguousarray(np.asarray(L0_attn_x))
    wqT = np.ascontiguousarray(np.asarray(L0_attn_wq).T)
    wkT = np.ascontiguousarray(np.asarray(L0_attn_wk).T)
    wvT = np.ascontiguousarray(np.asarray(L0_attn_wv).T)
    woT = np.ascontiguousarray(np.asarray(L0_attn_wo).T)
    pakT = np.ascontiguousarray(np.asarray(L0_ffn_weight_packed).T)
    scales = np.asarray(L0_ffn_scales)
    cent = np.asarray(L0_ffn_centroids).reshape(1, 16).astype(np.float32)
    sinT, cosT, maskneg = host_tables(T)

    in_maps = []
    for c in range(8):
        s, par = c // 2, c % 2
        fsl = np.arange(c * (D // 8), (c + 1) * (D // 8))
        sclx = np.ascontiguousarray(scales.T[fsl // 64, :])
        hb = par * (NHG // 2)
        hc = slice(hb * HD, (hb + NHG // 2) * HD)
        in_maps.append({
            "xin": np.ascontiguousarray(x[s, par * TO:(par + 1) * TO, :]),
            "wkT": np.ascontiguousarray(wkT[:, hc]),
            "wvT": np.ascontiguousarray(wvT[:, hc]),
            "wqT": np.ascontiguousarray(wqT[:, hc]),
            "woT": np.ascontiguousarray(woT[hb * HD:(hb + NHG // 2) * HD, :]),
            "pakT": np.ascontiguousarray(pakT[c * (D // 16):(c + 1) * (D // 16), :]),
            "sclx": sclx,
            "cent": cent,
            "sinT": sinT,
            "cosT": cosT,
            "maskneg": maskneg,
            "cinfo": np.array([[par * TO, par, c, s, 0, 0, 0, 0]], np.int32),
        })
    return T, in_maps


def kernel(L0_attn_x, L0_attn_wq, L0_attn_wk, L0_attn_wv, L0_attn_wo,
           L0_ffn_weight_packed, L0_ffn_centroids, L0_ffn_scales):
    T, in_maps = host_prep(L0_attn_x, L0_attn_wq, L0_attn_wk, L0_attn_wv,
                           L0_attn_wo, L0_ffn_weight_packed, L0_ffn_centroids,
                           L0_ffn_scales)
    TO = T // 2
    bld = build_program(T, dbg=False)
    nc = bld.build()
    res = run_bass_kernel_spmd(nc, in_maps, core_ids=list(range(8)))
    out = np.empty((4, T, D), np.float32)
    for c in range(8):
        s, par = c // 2, c % 2
        out[s, par * TO:(par + 1) * TO, :] = res.results[c]["xout"]
    return out


# version 06
# speedup vs baseline: 1.0365x; 1.0014x over previous
import sys
if '/opt/trn_rl_repo' not in sys.path:
    sys.path.insert(0, '/opt/trn_rl_repo')
import math
from contextlib import ExitStack

import numpy as np
import ml_dtypes

import concourse.bass as bass
import concourse.mybir as mybir
from concourse.bass_utils import run_bass_kernel_spmd

F32 = mybir.dt.float32
BF16 = mybir.dt.bfloat16
I32 = mybir.dt.int32
AF = mybir.ActivationFunctionType
ALU = mybir.AluOpType
AX = mybir.AxisListType.X

D = 4096
HD = 128
NHG = 32
KC = D // 128
QTW = 512
INV_SQRT_HD = np.float32(1.0 / math.sqrt(HD))
C_BOUND = float(-1.0 / (2.0 * math.sqrt(HD)))
EPS_F = np.float32(1e-5)


class Ctr:

    def __init__(self, sem):
        self.sem = sem
        self.n = 0

    def bump(self, inst, k=1):
        inst.then_inc(self.sem, k)
        self.n += k
        return self.n


def host_tables(T):
    pos = np.arange(T, dtype=np.float32)
    thetas = (1.0 / (10000.0 ** (np.arange(0, HD, 2, dtype=np.float32) / HD))).astype(np.float32)
    pt = np.outer(pos, thetas).astype(np.float32)
    pt = np.concatenate([pt, pt], axis=-1)
    sinT = np.ascontiguousarray(np.sin(pt).astype(np.float32).T)
    cosT = np.ascontiguousarray(np.cos(pt).astype(np.float32).T)
    j = np.arange(128)[:, None]
    u = np.arange(896)[None, :]
    maskneg = np.where(j <= u - 384, 0.0, -1e9).astype(np.float32)
    return sinT, cosT, maskneg


class Builder:
    def __init__(self, T, dbg=False):
        self.T = T
        self.TO = T // 2
        self.NH = NHG // 2
        self.FO = self.NH * HD
        self.NQ = T // QTW
        self.dbg = dbg
        nc = bass.Bass("TRN2", target_bir_lowering=False,
                       detect_race_conditions=False)
        self.nc = nc
        self.ctx = ExitStack()
        TO, FO = self.TO, self.FO

        di = lambda name, shape, dt: nc.dram_tensor(name, shape, dt, kind="ExternalInput").ap()
        self.xin = di("xin", [TO, D], F32)
        self.wkT = di("wkT", [D, FO], F32)
        self.wvT = di("wvT", [D, FO], F32)
        self.wqT = di("wqT", [D, FO], F32)
        self.woT = di("woT", [FO, D], F32)
        self.pakT = di("pakT", [D // 16, D], I32)
        self.sclx = di("sclx", [D // 8, D], F32)
        self.cent = di("cent", [1, 16], F32)
        self.sinTd = di("sinT", [HD, T], F32)
        self.cosTd = di("cosT", [HD, T], F32)
        self.maskd = di("maskneg", [128, 896], F32)
        self.cinfo = di("cinfo", [1, 8], I32)
        self.xout = nc.dram_tensor("xout", [TO, D], F32, kind="ExternalOutput").ap()

        dsc = lambda name, shape, dt: nc.dram_tensor(name, shape, dt, kind="Internal").ap()
        self.wdq_inH = [dsc("wdq_in0", [D // 8, D // 2], BF16),
                        dsc("wdq_in1", [D // 8, D // 2], BF16)]
        self.wdqH = [dsc("wdq0", [D, D // 2], BF16),
                     dsc("wdq1", [D, D // 2], BF16)]
        self.xabs = dsc("xabs", [TO, D], BF16)
        self.xbo = dsc("xbo", [TO, D], BF16)
        self.xbf = dsc("xbf", [T, D], BF16)
        self.x1s = dsc("x1s", [TO, D], F32)
        self.x2s = dsc("x2s", [TO, D], F32)
        self.kts = dsc("kts", [FO, T], BF16)
        self.qts = dsc("qts", [FO, T], BF16)
        self.vts = dsc("vts", [T, FO], BF16)
        self.atns = dsc("atns", [FO, T], BF16)
        self.op4H = [dsc("op4l", [T, D // 2], BF16), dsc("op4r", [T, D // 2], BF16)]
        self.x2pH = [dsc("x2pl", [TO, D // 2], BF16), dsc("x2pr", [TO, D // 2], BF16)]
        self.dbg_views = {}
        if dbg:
            for nm, ap_ in [("xbf", self.xbf),
                            ("x1s", self.x1s), ("x2s", self.x2s),
                            ("kts", self.kts), ("qts", self.qts),
                            ("vts", self.vts), ("atns", self.atns)]:
                self.dbg_views[nm] = nc.dram_tensor(
                    "dbg_" + nm, list(ap_.tensor.shape), ap_.dtype,
                    kind="ExternalOutput").ap()

        sem = lambda name: self.ctx.enter_context(nc.semaphore(name))
        self.qm = Ctr(sem("qm"))
        self.qx = [Ctr(sem("qx0")), Ctr(sem("qx1"))]
        self.qw = [Ctr(sem("qw0")), Ctr(sem("qw1"))]
        self.qv = [Ctr(sem("qv0")), Ctr(sem("qv1")), Ctr(sem("qv2"))]
        self.gm = Ctr(sem("gm"))
        self.gw = [Ctr(sem("gw0")), Ctr(sem("gw1")), Ctr(sem("gw2"))]
        self.gs = [Ctr(sem("gs0")), Ctr(sem("gs1"))]
        self.am = Ctr(sem("am"))
        self.asx = [Ctr(sem("as0")), Ctr(sem("as1"))]
        self.sMM = Ctr(sem("sMM"))
        self.sEV = Ctr(sem("sEV"))
        self.sDV = Ctr(sem("sDV"))
        self.sPL = Ctr(sem("sPL"))
        self.sCC = Ctr(sem("sCC"))
        self.qsc = [Ctr(sem("qsc0")), Ctr(sem("qsc1"))]
        self.gst = [Ctr(sem("gst0")), Ctr(sem("gst1"))]

        sb = lambda name, shape, dt: self.ctx.enter_context(nc.sbuf_tensor(name, shape, dt))
        self.ones_col = sb("ones_col", [128, 1], F32)
        self.ones_col_bf = sb("ones_col_bf", [128, 1], BF16)
        self.ones_row = sb("ones_row", [1, 128], F32)
        self.mask_sb = sb("mask_sb", [128, 896], F32)
        self.cent_bc = sb("cent_bc", [128, 16], F32)
        self.cinfo_sb = sb("cinfo_sb", [1, 8], I32)

    @staticmethod
    def flush(eng, ctrs):
        for c in ctrs:
            if c.n > 0:
                eng.wait_ge(c.sem, c.n)

    def prologue(self):
        nc = self.nc
        qm, gm, sMM, sEV, sDV = self.qm, self.gm, self.sMM, self.sEV, self.sDV
        ctxp = ExitStack()
        pb = ctxp.enter_context(nc.psum_tensor("pb_cent", [128, 16], F32))
        with nc.Block("prologue") as b:
            @b.sync
            def _(sync):
                qm.bump(sync.dma_start(self.mask_sb[:, :], self.maskd[:, :]), 16)
                qm.bump(sync.dma_start(self.cinfo_sb[:, :], self.cinfo[:, :]), 16)
                sync.wait_ge(qm.sem, qm.n)

            @b.vector
            def _(v):
                v.memset(self.ones_col[:, :], 1.0)
                v.memset(self.ones_col_bf[:, :], 1.0)
                sDV.bump(v.memset(self.ones_row[:, :], 1.0), 1)

            @b.gpsimd
            def _(g):
                gm.bump(g.dma_start(self.cent_bc[0:1, 0:16], self.cent[:, :]), 16)

            @b.tensor
            def _(t):
                t.wait_ge(gm.sem, gm.n)
                t.wait_ge(sDV.sem, sDV.n)
                sMM.bump(t.matmul(pb[:, :], self.ones_row[:, :],
                                  self.cent_bc[0:1, 0:16], start=True, stop=True), 1)

            @b.scalar
            def _(s):
                s.wait_ge(sMM.sem, sMM.n)
                sEV.bump(s.copy(self.cent_bc[:, :], pb[:, :]), 1)
        ctxp.close()
        self.ev_cent = sEV.n

    def _deq_slab(self, eng, ctr, bufset, nibb_ci, g0, g1, l0, whlh,
                  sclh, scll, q_scl, ext_t, gm_st):
        terms, accs = bufset
        w = g1 - g0
        eng.wait_ge(q_scl[0], q_scl[1])
        eng.wait_ge(self.sDV.sem, ext_t)
        eng.wait_ge(self.sEV.sem, self.ev_cent)
        if gm_st is not None:
            eng.wait_ge(gm_st[0], gm_st[1])
        nh = nibb_ci[:, g0:g1]
        nl = nibb_ci[:, D + g0:D + g1]
        tb = lambda j: terms[j % 4]
        ab = lambda j: accs[j % 2]
        done = 0
        for j in range(16):
            eng.tensor_scalar(tb(j)[:, 0:w], nh, float(j),
                              self.cent_bc[:, j:j + 1], ALU.is_equal, ALU.mult)
            eng.tensor_scalar(tb(j)[:, w:2 * w], nl, float(j),
                              self.cent_bc[:, j:j + 1], ALU.is_equal, ALU.mult)
            if j >= 2:
                jj = done
                if jj < 2:
                    eng.tensor_copy(ab(jj)[:, 0:2 * w], tb(jj)[:, 0:2 * w])
                else:
                    eng.tensor_tensor(ab(jj)[:, 0:2 * w], ab(jj)[:, 0:2 * w],
                                      tb(jj)[:, 0:2 * w], ALU.add)
                done += 1
        v = ctr.bump(eng.engine_nop(), 1)
        eng.wait_ge(ctr.sem, v)
        while done < 16:
            jj = done
            eng.tensor_tensor(ab(jj)[:, 0:2 * w], ab(jj)[:, 0:2 * w],
                              tb(jj)[:, 0:2 * w], ALU.add)
            done += 1
        v = ctr.bump(eng.engine_nop(), 1)
        eng.wait_ge(ctr.sem, v)
        eng.tensor_tensor(accs[0][:, 0:2 * w], accs[0][:, 0:2 * w],
                          accs[1][:, 0:2 * w], ALU.add)
        v = ctr.bump(eng.engine_nop(), 1)
        eng.wait_ge(ctr.sem, v)
        eng.tensor_tensor(whlh[:, l0:l0 + w], accs[0][:, 0:w],
                          sclh[:, l0:l0 + w], ALU.mult)
        ctr.bump(eng.tensor_tensor(whlh[:, 2048 + l0:2048 + l0 + w],
                                   accs[0][:, w:2 * w],
                                   scll[:, l0:l0 + w], ALU.mult), 1)

    def p0_dequant(self):
        nc = self.nc
        qm, gm, sDV, sPL, sCC = self.qm, self.gm, self.sDV, self.sPL, self.sCC
        qsc, gst = self.qsc, self.gst
        CW = 512
        HW2 = 2048
        ctx0 = ExitStack()
        sb = lambda n, s, d: ctx0.enter_context(nc.sbuf_tensor(n, s, d))
        pak = [sb(f"pak{i}", [128, D], I32) for i in range(2)]
        nibi = sb("nibi", [128, 2 * D], I32)
        nibb = [sb(f"nibb{i}", [128, 2 * D], BF16) for i in range(2)]
        whlh = [sb(f"whlh{i}", [128, 2 * HW2], BF16) for i in range(2)]
        sclh = [sb(f"sclh{i}", [128, HW2], F32) for i in range(2)]
        scll = [sb(f"scll{i}", [128, HW2], F32) for i in range(2)]

        def mkset(tag):
            terms = [sb(f"{tag}_t{i}", [128, 2 * CW], F32) for i in range(4)]
            accs = [sb(f"{tag}_a{i}", [128, 2 * CW], F32) for i in range(2)]
            return (terms, accs)

        set_dv = mkset("ddv")
        set_pl = mkset("dpl")

        qb, gb = qm.n, gm.n
        db, plb = sDV.n, sPL.n
        NDS, NPS = 3, 1
        ext_t = lambda ci: db + ci + 1
        dv_done = lambda s: db + 2 + 4 * NDS * (s + 1)
        pl_done = lambda s: plb + 4 * NPS * (s + 1)
        q_pak = lambda ci: qb + 16 * (ci + 1)
        q_scl = lambda s: (qsc[s % 2].sem, 32 * (s // 2 + 1))
        gm_st = lambda s: (gst[s % 2].sem, 32 * (s // 2 + 1))

        with nc.Block("p0") as b:
            @b.sync
            def _(sync):
                for ci in range(2):
                    qm.bump(sync.dma_start(pak[ci][:, :],
                                           self.pakT[ci * 128:(ci + 1) * 128, :]), 16)
                for s in range(4):
                    hf, ci = s // 2, s % 2
                    if s >= 2:
                        sync.wait_ge(sDV.sem, dv_done(s - 2))
                        sync.wait_ge(sPL.sem, pl_done(s - 2))
                    qsc[s % 2].bump(sync.dma_start(
                        sclh[s % 2][:, :],
                        self.sclx[ci * 256:(ci + 1) * 256:2,
                                  hf * HW2:(hf + 1) * HW2]), 16)
                    qsc[s % 2].bump(sync.dma_start(
                        scll[s % 2][:, :],
                        self.sclx[ci * 256 + 1:(ci + 1) * 256:2,
                                  hf * HW2:(hf + 1) * HW2]), 16)

            @b.vector
            def _(v):
                for ci in range(2):
                    v.wait_ge(qm.sem, q_pak(ci))
                    v.tensor_scalar(nibi[:, 0:D], pak[ci][:, :], 4, 15,
                                    ALU.logical_shift_right, ALU.bitwise_and)
                    v.tensor_scalar(nibi[:, D:2 * D], pak[ci][:, :], 15, None,
                                    ALU.bitwise_and)
                    v.tensor_copy(nibb[ci][:, 0:D], nibi[:, 0:D])
                    sDV.bump(v.tensor_copy(nibb[ci][:, D:2 * D],
                                           nibi[:, D:2 * D]), 1)
                for s in range(4):
                    hf, ci = s // 2, s % 2
                    for k in range(NDS):
                        self._deq_slab(v, sDV, set_dv, nibb[ci],
                                       hf * HW2 + k * CW, hf * HW2 + (k + 1) * CW,
                                       k * CW, whlh[s % 2], sclh[s % 2], scll[s % 2],
                                       q_scl(s), ext_t(ci),
                                       gm_st(s - 2) if s >= 2 else None)

            @b.gpsimd
            def _(g):
                for s in range(4):
                    hf, ci = s // 2, s % 2
                    for k in range(NPS):
                        self._deq_slab(g, sPL, set_pl, nibb[ci],
                                       hf * HW2 + (NDS + k) * CW,
                                       hf * HW2 + (NDS + k + 1) * CW,
                                       (NDS + k) * CW, whlh[s % 2],
                                       sclh[s % 2], scll[s % 2],
                                       q_scl(s), ext_t(ci),
                                       gm_st(s - 2) if s >= 2 else None)
                    g.wait_ge(sDV.sem, dv_done(s))
                    g.wait_ge(sPL.sem, pl_done(s))
                    base = ci * 256
                    gst[s % 2].bump(g.dma_start(
                        self.wdq_inH[hf][base:base + 256:2, :],
                        whlh[s % 2][:, 0:HW2]), 16)
                    gst[s % 2].bump(g.dma_start(
                        self.wdq_inH[hf][base + 1:base + 256:2, :],
                        whlh[s % 2][:, HW2:2 * HW2]), 16)
                    if ci == 1:
                        g.wait_ge(gst[0].sem, 32 * (s // 2 + 1))
                        g.wait_ge(gst[1].sem, 32 * (s // 2 + 1))
                        sCC.bump(g.collective_compute(
                            "AllGather", ALU.bypass,
                            replica_groups=[list(range(8))],
                            ins=[self.wdq_inH[hf].opt()],
                            outs=[self.wdqH[hf].opt()]), 1)
        ctx0.close()
        self.cc_wdqL = sCC.n - 1
        self.cc_wdqR = sCC.n

    def ln_phase(self, name, src, dst, guards=(), src2=None, sum_dst=None):
        nc = self.nc
        qx, qw, gs, sEV, sDV = self.qx, self.qw, self.gs, self.sEV, self.sDV
        nt = self.TO // 128
        KI = 5 if src2 is not None else 4
        NSR = 2 if src2 is not None else 1
        ctx0 = ExitStack()
        sb = lambda n, s, d: ctx0.enter_context(nc.sbuf_tensor(n, s, d))
        src_halves = isinstance(src, list)
        sdt = (src[0] if src_halves else src).dtype
        src_dt = F32 if sdt == F32 else BF16
        xln = [sb(f"{name}_x{i}", [128, D], src_dt) for i in range(2)]
        xln2 = [sb(f"{name}_y{i}", [128, D], F32) for i in range(2)] if src2 is not None else None
        xsum = xln if src2 is None else [sb(f"{name}_u{i}", [128, D], F32) for i in range(2)]
        xa = [sb(f"{name}_a{i}", [128, D], BF16) for i in range(2)]
        stats = [sb(f"{name}_s{i}", [128, 48], F32) for i in range(2)]
        aggr = [sb(f"{name}_g{i}", [128, 2], F32) for i in range(2)]
        lnv = [sb(f"{name}_l{i}", [128, 1], F32) for i in range(2)]
        rstd = [sb(f"{name}_r{i}", [128, 1], F32) for i in range(2)]
        nmr = [sb(f"{name}_n{i}", [128, 1], F32) for i in range(2)]
        eb, db = sEV.n, sDV.n
        qxb = [qx[0].n, qx[1].n]
        qwb = [qw[0].n, qw[1].n]
        gsb = [gs[0].n, gs[1].n]
        NLD = 2 if src_halves else 1
        xt = lambda i: qxb[i % 2] + 16 * NLD * (i // 2 + 1)
        x2t = lambda i: qwb[i % 2] + 16 * (i // 2 + 1)
        st = lambda i: gsb[i % 2] + 16 * NSR * (i // 2 + 1)
        d_sum = lambda i: db + KI * i + 1
        d_statsnop = lambda i: db + KI * i + KI - 3
        d_aggr = lambda i: db + KI * i + KI - 2
        d_nmrnop = lambda i: db + KI * i + KI - 1
        d_norm = lambda i: db + KI * i + KI
        with nc.Block(name) as b:
            @b.sync
            def _(sync):
                self.flush(sync, qx + (qw if src2 is not None else []))
                for gd in guards:
                    sync.wait_ge(gd[0].sem, gd[1])
                for i in range(nt):
                    if i >= 2:
                        sync.wait_ge(sDV.sem, d_norm(i - 2))
                    if src_halves:
                        qx[i % 2].bump(sync.dma_start(
                            xln[i % 2][:, 0:D // 2],
                            src[0][i * 128:(i + 1) * 128, :]), 16)
                        qx[i % 2].bump(sync.dma_start(
                            xln[i % 2][:, D // 2:D],
                            src[1][i * 128:(i + 1) * 128, :]), 16)
                    else:
                        qx[i % 2].bump(sync.dma_start(
                            xln[i % 2][:, :], src[i * 128:(i + 1) * 128, :]), 16)
                    if src2 is not None:
                        qw[i % 2].bump(sync.dma_start(
                            xln2[i % 2][:, :], src2[i * 128:(i + 1) * 128, :]), 16)

            @b.vector
            def _(v):
                for i in range(nt):
                    p = i % 2
                    v.wait_ge(qx[p].sem, xt(i))
                    if src2 is not None:
                        v.wait_ge(qw[p].sem, x2t(i))
                        if i >= 2:
                            v.wait_ge(gs[p].sem, st(i - 2))
                        w = sDV.bump(v.tensor_tensor(xsum[p][:, :], xln[p][:, :],
                                                     xln2[p][:, :], ALU.add), 1)
                        v.wait_ge(sDV.sem, w)
                    for c in range(8):
                        v.bn_stats(stats[p][:, 6 * c:6 * c + 6],
                                   xsum[p][:, 512 * c:512 * (c + 1)])
                    w = sDV.bump(v.engine_nop(), 1)
                    v.wait_ge(sDV.sem, w)
                    sDV.bump(v.bn_aggr(aggr[p][:, :], stats[p][:, :]), 1)
                    v.wait_ge(sEV.sem, eb + 2 * i + 2)
                    v.scalar_tensor_tensor(nmr[p][:, :], aggr[p][:, 0:1], -1.0,
                                           rstd[p][:, 0:1], ALU.mult, ALU.mult)
                    w = sDV.bump(v.engine_nop(), 1)
                    v.wait_ge(sDV.sem, w)
                    if i >= 2:
                        v.wait_ge(gs[p].sem, st(i - 2))
                    sDV.bump(v.tensor_scalar(xa[p][:, :], xsum[p][:, :],
                                             rstd[p][:, 0:1], nmr[p][:, 0:1],
                                             ALU.mult, ALU.add), 1)

            @b.scalar
            def _(s):
                for i in range(nt):
                    p = i % 2
                    s.wait_ge(sDV.sem, d_aggr(i))
                    w = sEV.bump(s.activation(lnv[p][:, :], aggr[p][:, 1:2],
                                              AF.Ln, bias=EPS_F), 1)
                    s.wait_ge(sEV.sem, w)
                    sEV.bump(s.activation(rstd[p][:, :], lnv[p][:, :],
                                          AF.Exp, scale=np.float32(-0.5)), 1)

            @b.gpsimd
            def _(g):
                self.flush(g, gs)
                for i in range(nt):
                    g.wait_ge(sDV.sem, d_norm(i))
                    if src2 is not None:
                        gs[i % 2].bump(g.dma_start(
                            sum_dst[i * 128:(i + 1) * 128, :], xsum[i % 2][:, :]), 16)
                    gs[i % 2].bump(g.dma_start(
                        dst[i * 128:(i + 1) * 128, :], xa[i % 2][:, :]), 16)
        ctx0.close()
        self.last_ln_dve = sDV.n
        return [(gs[0], gs[0].n), (gs[1], gs[1].n)]

    def transpose_resident(self, name, src, resident, rows, src_ready):
        nc = self.nc
        am = self.am
        with nc.Block(name) as b:
            @b.scalar
            def _(s):
                self.flush(s, [am])
                for rd in src_ready:
                    s.wait_ge(rd[0].sem, rd[1])
                for kc in range(KC):
                    am.bump(s.dma_start(resident[:, kc * rows:(kc + 1) * rows],
                                        src[0:rows, kc * 128:(kc + 1) * 128],
                                        transpose=True), 16)
        return [(am, am.n)]

    def gemm_token(self, name, resident, res_ready, wsrc_kind, wsrc, resid_src,
                   out_dst, scale, sync_guard=(), rows=None, nkc=KC, out_dt=F32,
                   strip_guards=None):
        nc = self.nc
        qx, qw, gw, asx = self.qx, self.qw, self.gw, self.asx
        sMM, sDV = self.sMM, self.sDV
        rows = rows if rows is not None else self.TO
        MC = rows // 128
        NS = D // 512
        ctx0 = ExitStack()
        sb = lambda n, s, d: ctx0.enter_context(nc.sbuf_tensor(n, s, d))
        wst = [sb(f"{name}_w{i}", [128, nkc * 512], BF16) for i in range(2)]
        rbuf = [sb(f"{name}_r{i}", [128, 512], F32) for i in range(2)] \
            if resid_src is not None else None
        obuf = [sb(f"{name}_o{i}", [128, 512], out_dt) for i in range(2)]
        ps = [ctx0.enter_context(nc.psum_tensor(f"{name}_p{i}", [128, 512], F32))
              for i in range(2)]
        mb, db = sMM.n, sDV.n
        qwb = [qw[0].n, qw[1].n]
        gwb = [gw[0].n, gw[1].n]
        qxb = [qx[0].n, qx[1].n]
        asb = [asx[0].n, asx[1].n]
        strip_t = lambda ns: (qwb if wsrc_kind == 'bf16' else gwb)[ns % 2] + 16 * (ns // 2 + 1)
        resid_t = lambda g: qxb[g % 2] + 16 * (g // 2 + 1)
        store_t = lambda g: asb[g % 2] + 16 * (g // 2 + 1)

        def strip_dma(eng, ns):
            if isinstance(wsrc, list):
                nh = NS // 2
                w_ap = wsrc[ns // nh][:, (ns % nh) * 512:((ns % nh) + 1) * 512]
            else:
                w_ap = wsrc[:, ns * 512:(ns + 1) * 512]
            return eng.dma_start(
                wst[ns % 2][:, :].rearrange("p (kc n) -> p kc n", kc=nkc),
                w_ap.rearrange("(kc p) n -> p kc n", p=128))

        with nc.Block(name) as b:
            @b.sync
            def _(sync):
                self.flush(sync, qx + (qw if wsrc_kind == 'bf16' else []))
                for gd in sync_guard:
                    sync.wait_ge(gd[0].sem, gd[1])
                for ns in range(NS):
                    if wsrc_kind == 'bf16':
                        if strip_guards is not None:
                            gd = strip_guards[ns // (NS // 2)]
                            sync.wait_ge(gd[0].sem, gd[1])
                        if ns >= 2:
                            sync.wait_ge(sMM.sem, mb + (ns - 1) * MC)
                        qw[ns % 2].bump(strip_dma(sync, ns), 16)
                    if resid_src is not None:
                        for m in range(MC):
                            g = ns * MC + m
                            if g >= 2:
                                sync.wait_ge(sDV.sem, db + g - 1)
                            qx[g % 2].bump(sync.dma_start(
                                rbuf[g % 2][:, :],
                                resid_src[m * 128:(m + 1) * 128,
                                          ns * 512:(ns + 1) * 512]), 16)

            if wsrc_kind == 'f32':
                @b.gpsimd
                def _(gp):
                    self.flush(gp, gw)
                    for ns in range(NS):
                        if ns >= 2:
                            gp.wait_ge(sMM.sem, mb + (ns - 1) * MC)
                        gw[ns % 2].bump(strip_dma(gp, ns), 16)

            @b.tensor
            def _(t):
                for rd in res_ready:
                    t.wait_ge(rd[0].sem, rd[1])
                for ns in range(NS):
                    sctr = (qw if wsrc_kind == 'bf16' else gw)[ns % 2]
                    t.wait_ge(sctr.sem, strip_t(ns))
                    for m in range(MC):
                        g = ns * MC + m
                        if g >= 2:
                            t.wait_ge(sDV.sem, db + g - 1)
                        for k in range(nkc):
                            inst = t.matmul(
                                ps[g % 2][:, :],
                                resident[:, k * rows + m * 128: k * rows + (m + 1) * 128],
                                wst[ns % 2][:, k * 512:(k + 1) * 512],
                                start=(k == 0), stop=(k == nkc - 1))
                        sMM.bump(inst, 1)

            @b.vector
            def _(v):
                for ns in range(NS):
                    for m in range(MC):
                        g = ns * MC + m
                        v.wait_ge(sMM.sem, mb + g + 1)
                        if resid_src is not None:
                            v.wait_ge(qx[g % 2].sem, resid_t(g))
                        if g >= 2:
                            v.wait_ge(asx[g % 2].sem, store_t(g - 2))
                        if resid_src is not None:
                            sDV.bump(v.scalar_tensor_tensor(
                                obuf[g % 2][:, :], ps[g % 2][:, :], scale,
                                rbuf[g % 2][:, :], ALU.mult, ALU.add), 1)
                        else:
                            sDV.bump(v.tensor_scalar(
                                obuf[g % 2][:, :], ps[g % 2][:, :], scale, None,
                                ALU.mult), 1)

            @b.scalar
            def _(s):
                self.flush(s, asx)
                for ns in range(NS):
                    if isinstance(out_dst, list):
                        nh = NS // 2
                        od = out_dst[ns // nh]
                        oc = (ns % nh) * 512
                    else:
                        od = out_dst
                        oc = ns * 512
                    for m in range(MC):
                        g = ns * MC + m
                        s.wait_ge(sDV.sem, db + g + 1)
                        asx[g % 2].bump(s.dma_start(
                            od[m * 128:(m + 1) * 128, oc:oc + 512],
                            obuf[g % 2][:, :]), 16)
        ctx0.close()
        return [(asx[0], asx[0].n), (asx[1], asx[1].n)]

    def p2c_half(self, th, xbT, xbT_ready):
        nc = self.nc
        T, TH = self.T, self.T // 2
        FO, NH = self.FO, self.NH
        qm, gw, asx = self.qm, self.gw, self.asx
        sMM, sEV, sDV = self.sMM, self.sEV, self.sDV
        tc0 = th * TH

        jobs = [("k", hl) for hl in range(NH)] + [("q", hl) for hl in range(NH)]
        sin_ready = []
        ctxb = ExitStack()
        sbb = lambda n, s, d: ctxb.enter_context(nc.sbuf_tensor(n, s, d))
        sinb = sbb(f"sin{th}", [128, TH], BF16)
        cosb = sbb(f"cos{th}", [128, TH], BF16)
        wst = [sbb(f"j{th}w{i}", [128, KC * HD], BF16) for i in range(3)]
        kqo = [sbb(f"j{th}o{i}", [128, TH], BF16) for i in range(2)]
        rt = [sbb(f"j{th}t{i}", [64, TH], F32) for i in range(4)]
        psK = [ctxb.enter_context(nc.psum_tensor(f"j{th}p{i}", [128, TH], F32))
               for i in range(2)]
        mb, db = sMM.n, sDV.n
        gwb = [gw[0].n, gw[1].n, gw[2].n]
        asb = [asx[0].n, asx[1].n]
        strip_t = lambda i: gwb[i % 3] + 16 * (i // 3 + 1)
        store_t = lambda i: asb[i % 2] + 16 * (i // 2 + 1)

        gm = self.gm
        with nc.Block(f"p2b{th}") as b:
            @b.gpsimd
            def _(gp):
                self.flush(gp, [gm])
                gm.bump(gp.dma_start(sinb[:, :],
                                     self.sinTd[:, tc0:tc0 + TH]), 16)
                gm.bump(gp.dma_start(cosb[:, :],
                                     self.cosTd[:, tc0:tc0 + TH]), 16)
                sin_ready.append((gm, gm.n))
                self.flush(gp, gw)
                for i, (kind, hl) in enumerate(jobs):
                    wsrc = self.wkT if kind == "k" else self.wqT
                    if i >= 3:
                        gp.wait_ge(sMM.sem, mb + i - 2)
                    gw[i % 3].bump(gp.dma_start(
                        wst[i % 3][:, :].rearrange("p (kc n) -> p kc n", kc=KC),
                        wsrc[:, hl * HD:(hl + 1) * HD].rearrange(
                            "(kc p) n -> p kc n", p=128)), 16)

            @b.tensor
            def _(t):
                for rd in xbT_ready:
                    t.wait_ge(rd[0].sem, rd[1])
                for i, (kind, hl) in enumerate(jobs):
                    t.wait_ge(gw[i % 3].sem, strip_t(i))
                    if i >= 2:
                        t.wait_ge(sDV.sem, db + i - 1)
                    for k in range(KC):
                        for nb in range(TH // 512):
                            inst = t.matmul(
                                psK[i % 2][:, nb * 512:(nb + 1) * 512],
                                wst[i % 3][:, k * HD:(k + 1) * HD],
                                xbT[:, k * TH + nb * 512: k * TH + (nb + 1) * 512],
                                start=(k == 0), stop=(k == KC - 1))
                    sMM.bump(inst, 1)

            @b.vector
            def _(v):
                for rd in sin_ready:
                    v.wait_ge(rd[0].sem, rd[1])
                for i, (kind, hl) in enumerate(jobs):
                    p = i % 2
                    v.wait_ge(sMM.sem, mb + i + 1)
                    if i >= 2:
                        v.wait_ge(asx[p].sem, store_t(i - 2))
                    pk = psK[p]
                    v.tensor_tensor(rt[0][:, :], sinb[0:64, :], pk[64:128, :], ALU.mult)
                    v.tensor_tensor(rt[1][:, :], cosb[0:64, :], pk[0:64, :], ALU.mult)
                    v.tensor_tensor(rt[2][:, :], sinb[64:128, :], pk[0:64, :], ALU.mult)
                    v.tensor_tensor(rt[3][:, :], cosb[64:128, :], pk[64:128, :], ALU.mult)
                    v.tensor_tensor(kqo[p][0:64, :], rt[1][:, :], rt[0][:, :], ALU.subtract)
                    sDV.bump(v.tensor_tensor(kqo[p][64:128, :], rt[3][:, :],
                                             rt[2][:, :], ALU.add), 1)

            @b.scalar
            def _(s):
                self.flush(s, asx)
                for i, (kind, hl) in enumerate(jobs):
                    s.wait_ge(sDV.sem, db + i + 1)
                    dst = self.kts if kind == "k" else self.qts
                    asx[i % 2].bump(s.dma_start(
                        dst[hl * HD:(hl + 1) * HD, tc0:tc0 + TH],
                        kqo[i % 2][:, :]), 16)
        ctxb.close()

        ctxc = ExitStack()
        sbc = lambda n, s, d: ctxc.enter_context(nc.sbuf_tensor(n, s, d))
        vw = [sbc(f"v{th}w{i}", [128, KC * 512], BF16) for i in range(2)]
        vo = [sbc(f"v{th}o{i}", [128, 512], BF16) for i in range(2)]
        psV = [ctxc.enter_context(nc.psum_tensor(f"v{th}p{i}", [128, 512], F32))
               for i in range(2)]
        NSv = FO // 512
        MCv = TH // 128
        mb2, eb2 = sMM.n, sEV.n
        gwb2 = [gw[0].n, gw[1].n]
        asb2 = [asx[0].n, asx[1].n]
        vstrip_t = lambda ns: gwb2[ns % 2] + 16 * (ns // 2 + 1)
        vstore_t = lambda g: asb2[g % 2] + 16 * (g // 2 + 1)
        with nc.Block(f"p2v{th}") as b:
            @b.gpsimd
            def _(gp):
                self.flush(gp, gw)
                for ns in range(NSv):
                    if ns >= 2:
                        gp.wait_ge(sMM.sem, mb2 + (ns - 1) * MCv)
                    gw[ns % 2].bump(gp.dma_start(
                        vw[ns % 2][:, :].rearrange("p (kc n) -> p kc n", kc=KC),
                        self.wvT[:, ns * 512:(ns + 1) * 512].rearrange(
                            "(kc p) n -> p kc n", p=128)), 16)

            @b.tensor
            def _(t):
                for ns in range(NSv):
                    t.wait_ge(gw[ns % 2].sem, vstrip_t(ns))
                    for m in range(MCv):
                        g = ns * MCv + m
                        if g >= 2:
                            t.wait_ge(sEV.sem, eb2 + g - 1)
                        for k in range(KC):
                            inst = t.matmul(
                                psV[g % 2][:, :],
                                xbT[:, k * TH + m * 128: k * TH + (m + 1) * 128],
                                vw[ns % 2][:, k * 512:(k + 1) * 512],
                                start=(k == 0), stop=(k == KC - 1))
                        sMM.bump(inst, 1)

            @b.scalar
            def _(s):
                self.flush(s, asx)
                for ns in range(NSv):
                    for m in range(MCv):
                        g = ns * MCv + m
                        s.wait_ge(sMM.sem, mb2 + g + 1)
                        if g >= 2:
                            s.wait_ge(asx[g % 2].sem, vstore_t(g - 2))
                        w = sEV.bump(s.copy(vo[g % 2][:, :], psV[g % 2][:, :]), 1)
                        s.wait_ge(sEV.sem, w)
                        asx[g % 2].bump(s.dma_start(
                            self.vts[tc0 + m * 128:tc0 + (m + 1) * 128,
                                     ns * 512:(ns + 1) * 512],
                            vo[g % 2][:, :]), 16)
        ctxc.close()

    def p3_attention(self, max_groups=99, max_qt=99):
        nc = self.nc
        T, NH, FO = self.T, self.NH, self.FO
        NQ = min(self.NQ, max_qt)
        qm, qv, asx = self.qm, self.qv, self.asx
        sMM, sEV, sDV, sPL = self.sMM, self.sEV, self.sDV, self.sPL
        groups = [list(range(0, 6)), list(range(6, 12)), list(range(12, 16))]
        groups = groups[:max_groups]
        JPQ = QTW // 128
        NC8 = T // 512
        ctx0 = ExitStack()
        sb = lambda n, s, d: ctx0.enter_context(nc.sbuf_tensor(n, s, d))
        Ksb = [sb(f"at_k{i}", [128, T], BF16) for i in range(6)]
        Qful = [sb(f"at_q{i}", [128, T], BF16) for i in range(6)]
        Vsb = [sb(f"at_v{i}", [128, FO], BF16) for i in range(3)]
        Esb = [sb(f"at_e{i}", [128, 512], BF16) for i in range(6)]
        Zac = [[sb(f"at_z{i}_{p}", [128, 512], F32) for p in range(2)] for i in range(6)]
        Zfin = [sb(f"at_zf{i}", [128, 512], F32) for i in range(6)]
        Qsq = sb("at_qs", [128, T], BF16)
        Ksq = sb("at_ks", [128, T], BF16)
        qnr = sb("at_qn", [1, 512], F32)
        knr = sb("at_kn", [1, 512], F32)
        tmpmx = sb("at_tm", [1, 2], F32)
        mq2 = sb("at_mq2", [1, 6], F32)
        mk2 = sb("at_mk2", [1, 6], F32)
        sumb = sb("at_sumb", [1, 6], F32)
        negbf = sb("at_negbf", [1, 6], F32)
        negbc = [sb(f"at_nb{i}", [128, 1], F32) for i in range(6)]
        zrow = [sb(f"at_zr{i}", [1, 512], F32) for i in range(6)]
        rzrow = [sb(f"at_rz{i}", [1, 512], F32) for i in range(6)]
        rzbc = [sb(f"at_rb{i}", [128, 512], F32) for i in range(2)]
        aout = [sb(f"at_ao{i}", [128, 512], BF16) for i in range(2)]
        psO = [ctx0.enter_context(nc.psum_tensor(f"at_pO{i}", [128, 512], F32))
               for i in range(6)]
        psS = [ctx0.enter_context(nc.psum_tensor(f"at_pS{i}", [128, 512], F32))
               for i in range(2)]

        def round_seq(H):
            seq = []
            for sl in range(H):
                seq.append(("S", sl))
                if sl >= 2:
                    seq.append(("PV", sl - 2))
            seq.append(("PV", H - 2))
            seq.append(("PV", H - 1))
            return seq

        zeng = lambda slot: "dve" if slot % 2 == 0 else "pl"

        C = {}
        q = {"pe": sMM.n, "act": sEV.n, "dve": sDV.n, "pl": sPL.n,
             "qm": qm.n, "qv": [qv[0].n, qv[1].n, qv[2].n],
             "as": [asx[0].n, asx[1].n]}
        last_sto = {0: None, 1: None}
        all_rounds = []
        for gi, hg in enumerate(groups):
            H = len(hg)
            q["qm"] += 16 * 2 * H
            C[("kq", gi)] = q["qm"]
            for sl in range(H):
                for c in range(NC8):
                    q["pe"] += 1
                    C[("qnc", gi, sl, c)] = q["pe"]
                    q["pe"] += 1
                    C[("knc", gi, sl, c)] = q["pe"]
            for sl in range(H):
                q["pe"] += 1
                C[("nb", gi, sl)] = q["pe"]
            for sl in range(H):
                for c in range(NC8):
                    q["act"] += 1
                    C[("eqc", gi, sl, c)] = q["act"]
                    q["act"] += 1
                    C[("ekc", gi, sl, c)] = q["act"]
            for sl in range(H):
                q["act"] += 1
                C[("enb", gi, sl)] = q["act"]
            for sl in range(H):
                q["dve"] += 1
                C[("dqs", gi, sl)] = q["dve"]
                q["dve"] += 1
                C[("dks", gi, sl)] = q["dve"]
                for c in range(NC8):
                    q["dve"] += 1
                    C[("drq", gi, sl, c)] = q["dve"]
                    q["dve"] += 1
                    C[("drk", gi, sl, c)] = q["dve"]
            for sl in range(H):
                q["dve"] += 1
                C[("dnb", gi, sl)] = q["dve"]
            for qt in range(NQ):
                J = (qt + 1) * JPQ
                for jc in range(J):
                    sl3 = jc % 3
                    q["qv"][sl3] += 16
                    C[("ldv", gi, qt, jc)] = (sl3, q["qv"][sl3])
                    all_rounds.append((gi, qt, jc, H))
                for jc in range(J):
                    for kind, sl in round_seq(H):
                        q["pe"] += 1
                        C[(kind, gi, qt, jc, sl)] = q["pe"]
                for sl in range(H):
                    q["pe"] += 1
                    C[("zmm", gi, qt, sl)] = q["pe"]
                for sl in range(H):
                    q["pe"] += 1
                    C[("rzb", gi, qt, sl)] = q["pe"]
                for jc in range(J):
                    for sl in range(H):
                        q["act"] += 1
                        C[("exp", gi, qt, jc, sl)] = q["act"]
                for sl in range(H):
                    q["act"] += 1
                    C[("ezr", gi, qt, sl)] = q["act"]
                for sl in range(H):
                    q["act"] += 1
                    C[("erz", gi, qt, sl)] = q["act"]
                    p = sl % 2
                    q["as"][p] += 16
                    C[("sto", gi, qt, sl)] = (p, q["as"][p])
                for jc in range(J):
                    diag = jc >= qt * JPQ
                    for sl in range(H):
                        if diag:
                            q["dve"] += 1
                            C[("pm", gi, qt, jc, sl)] = q["dve"]
                        e = zeng(sl)
                        q[e] += 1
                        C[("za", gi, qt, jc, sl)] = q[e]
                for sl in range(H):
                    e = zeng(sl)
                    q[e] += 1
                    C[("zf", gi, qt, sl)] = q[e]
                for sl in range(H):
                    q["dve"] += 1
                    C[("rcp", gi, qt, sl)] = q["dve"]
                for sl in range(H):
                    q["dve"] += 1
                    C[("emu", gi, qt, sl)] = q["dve"]
                for sl in range(H):
                    C[("aoutw", gi, qt, sl)] = last_sto[sl % 2]
                    last_sto[sl % 2] = C[("sto", gi, qt, sl)]

        vprev = {}
        by_slot = {0: None, 1: None, 2: None}
        for (gi, qt, jc, H) in all_rounds:
            vprev[(gi, qt, jc)] = by_slot[jc % 3]
            by_slot[jc % 3] = (gi, qt, jc, H)

        sem_of = {"pe": sMM, "act": sEV, "dve": sDV, "pl": sPL}

        def prev_group_pe_done(gi):
            Hp = len(groups[gi - 1])
            return C[("rzb", gi - 1, NQ - 1, Hp - 1)]

        with nc.Block("p3") as b:
            @b.sync
            def _(sync):
                self.flush(sync, [qm] + qv)
                for gd in self.p2_stores_final:
                    sync.wait_ge(gd[0].sem, gd[1])
                for gi, hg in enumerate(groups):
                    H = len(hg)
                    if gi >= 1:
                        sync.wait_ge(sMM.sem, prev_group_pe_done(gi))
                    for sl in range(H):
                        hl = hg[sl]
                        qm.bump(sync.dma_start(
                            Ksb[sl][:, :], self.kts[hl * HD:(hl + 1) * HD, :]), 16)
                        qm.bump(sync.dma_start(
                            Qful[sl][:, :], self.qts[hl * HD:(hl + 1) * HD, :]), 16)
                    assert qm.n == C[("kq", gi)]
                    for qt in range(NQ):
                        J = (qt + 1) * JPQ
                        for jc in range(J):
                            pv = vprev[(gi, qt, jc)]
                            if pv is not None:
                                pgi, pqt, pjc, pH = pv
                                sync.wait_ge(sMM.sem,
                                             C[("PV", pgi, pqt, pjc, pH - 1)])
                            sl3, tgt = C[("ldv", gi, qt, jc)]
                            qv[sl3].bump(sync.dma_start(
                                Vsb[sl3][:, :],
                                self.vts[jc * 128:(jc + 1) * 128, :]), 16)
                            assert qv[sl3].n == tgt

            @b.tensor
            def _(t):
                psS_last_evac = [(None, 0), (None, 0)]
                psO_free = [None] * 6
                for gi, hg in enumerate(groups):
                    H = len(hg)
                    for bank in range(2):
                        if psO_free[bank] is not None:
                            e, cval = psO_free[bank]
                            t.wait_ge(sem_of[e].sem, cval)
                    for sl in range(H):
                        t.wait_ge(sDV.sem, C[("dqs", gi, sl)])
                        t.wait_ge(sDV.sem, C[("dks", gi, sl)])
                        for c in range(NC8):
                            if (sl, c) != (0, 0):
                                pc = (sl, c - 1) if c > 0 else (sl - 1, NC8 - 1)
                                t.wait_ge(sEV.sem, C[("eqc", gi) + pc])
                            t.matmul(psO[0][0:1, :], self.ones_col_bf[:, :],
                                     Qsq[:, c * 512:(c + 1) * 512],
                                     start=True, stop=True).then_inc(sMM.sem, 1)
                            if (sl, c) != (0, 0):
                                pc = (sl, c - 1) if c > 0 else (sl - 1, NC8 - 1)
                                t.wait_ge(sEV.sem, C[("ekc", gi) + pc])
                            t.matmul(psO[1][0:1, :], self.ones_col_bf[:, :],
                                     Ksq[:, c * 512:(c + 1) * 512],
                                     start=True, stop=True).then_inc(sMM.sem, 1)
                    for sl in range(H):
                        if sl == 0 and psO_free[2] is not None:
                            e, cval = psO_free[2]
                            t.wait_ge(sem_of[e].sem, cval)
                        t.wait_ge(sDV.sem, C[("dnb", gi, sl)])
                        if sl >= 1:
                            t.wait_ge(sEV.sem, C[("enb", gi, sl - 1)])
                        t.matmul(psO[2][:, 0:1], self.ones_row[:, :],
                                 negbf[0:1, sl:sl + 1], start=True, stop=True
                                 ).then_inc(sMM.sem, 1)
                    for qt in range(NQ):
                        J = (qt + 1) * JPQ
                        for jc in range(J):
                            for kind, sl in round_seq(H):
                                if kind == "S":
                                    e, cval = psS_last_evac[sl % 2]
                                    if e is not None:
                                        t.wait_ge(sem_of[e].sem, cval)
                                    t.wait_ge(qm.sem, C[("kq", gi)])
                                    t.matmul(psS[sl % 2][:, :],
                                             Ksb[sl][:, jc * 128:(jc + 1) * 128],
                                             Qful[sl][:, qt * QTW:(qt + 1) * QTW],
                                             start=True, stop=True
                                             ).then_inc(sMM.sem, 1)
                                    psS_last_evac[sl % 2] = (
                                        "act", C[("exp", gi, qt, jc, sl)])
                                else:
                                    t.wait_ge(sEV.sem, C[("exp", gi, qt, jc, sl)])
                                    if qt == 0 and jc == 0:
                                        t.wait_ge(sEV.sem, C[("enb", gi, H - 1)])
                                    sl3, tgt = C[("ldv", gi, qt, jc)]
                                    t.wait_ge(qv[sl3].sem, tgt)
                                    hl = hg[sl]
                                    t.matmul(psO[sl][:, :],
                                             Vsb[sl3][:, hl * HD:(hl + 1) * HD],
                                             Esb[sl][:, :],
                                             start=(jc == 0), stop=(jc == J - 1)
                                             ).then_inc(sMM.sem, 1)
                        for sl in range(H):
                            e = zeng(sl)
                            t.wait_ge(sem_of[e].sem, C[("zf", gi, qt, sl)])
                            ep, cval = psS_last_evac[sl % 2]
                            if ep is not None:
                                t.wait_ge(sem_of[ep].sem, cval)
                            t.matmul(psS[sl % 2][0:1, :], self.ones_col[:, :],
                                     Zfin[sl][:, :], start=True, stop=True
                                     ).then_inc(sMM.sem, 1)
                            psS_last_evac[sl % 2] = ("act", C[("ezr", gi, qt, sl)])
                        for sl in range(H):
                            t.wait_ge(sDV.sem, C[("rcp", gi, qt, sl)])
                            ep, cval = psS_last_evac[sl % 2]
                            t.wait_ge(sem_of[ep].sem, cval)
                            t.matmul(psS[sl % 2][:, :], self.ones_row[:, :],
                                     rzrow[sl][0:1, :], start=True, stop=True
                                     ).then_inc(sMM.sem, 1)
                            psS_last_evac[sl % 2] = ("act", C[("erz", gi, qt, sl)])
                        for sl in range(H):
                            psO_free[sl] = ("dve", C[("emu", gi, qt, sl)])
                sMM.n = q["pe"]

            @b.scalar
            def _(s):
                self.flush(s, asx)
                for gi, hg in enumerate(groups):
                    H = len(hg)
                    for sl in range(H):
                        for c in range(NC8):
                            s.wait_ge(sMM.sem, C[("qnc", gi, sl, c)])
                            pq = (sl, c - 1) if c > 0 else \
                                ((sl - 1, NC8 - 1) if sl >= 1 else None)
                            if pq is not None:
                                s.wait_ge(sDV.sem, C[("drq", gi) + pq])
                            s.copy(qnr[0:1, :], psO[0][0:1, :]
                                   ).then_inc(sEV.sem, 1)
                            s.wait_ge(sMM.sem, C[("knc", gi, sl, c)])
                            if pq is not None:
                                s.wait_ge(sDV.sem, C[("drk", gi) + pq])
                            s.copy(knr[0:1, :], psO[1][0:1, :]
                                   ).then_inc(sEV.sem, 1)
                    for sl in range(H):
                        s.wait_ge(sMM.sem, C[("nb", gi, sl)])
                        s.copy(negbc[sl][:, :], psO[2][:, 0:1]
                               ).then_inc(sEV.sem, 1)
                    for qt in range(NQ):
                        J = (qt + 1) * JPQ
                        for jc in range(J):
                            diag = jc >= qt * JPQ
                            for sl in range(H):
                                s.wait_ge(sMM.sem, C[("S", gi, qt, jc, sl)])
                                if diag:
                                    s.wait_ge(sDV.sem, C[("pm", gi, qt, jc, sl)])
                                if jc >= 1:
                                    s.wait_ge(sMM.sem, C[("PV", gi, qt, jc - 1, sl)])
                                    e = zeng(sl)
                                    s.wait_ge(sem_of[e].sem,
                                              C[("za", gi, qt, jc - 1, sl)])
                                elif qt >= 1:
                                    Jp = qt * JPQ
                                    s.wait_ge(sMM.sem, C[("PV", gi, qt - 1, Jp - 1, sl)])
                                    e = zeng(sl)
                                    s.wait_ge(sem_of[e].sem,
                                              C[("za", gi, qt - 1, Jp - 1, sl)])
                                elif gi >= 1 and sl < len(groups[gi - 1]):
                                    Jp = NQ * JPQ
                                    s.wait_ge(sMM.sem,
                                              C[("PV", gi - 1, NQ - 1, Jp - 1, sl)])
                                    e = zeng(sl)
                                    s.wait_ge(sem_of[e].sem,
                                              C[("za", gi - 1, NQ - 1, Jp - 1, sl)])
                                s.activation(Esb[sl][:, :], psS[sl % 2][:, :],
                                             AF.Exp, bias=negbc[sl][:, 0:1],
                                             scale=INV_SQRT_HD).then_inc(sEV.sem, 1)
                        for sl in range(H):
                            s.wait_ge(sMM.sem, C[("zmm", gi, qt, sl)])
                            s.copy(zrow[sl][:, :], psS[sl % 2][0:1, :]
                                   ).then_inc(sEV.sem, 1)
                        for sl in range(H):
                            hl = hg[sl]
                            s.wait_ge(sMM.sem, C[("rzb", gi, qt, sl)])
                            if sl >= 2:
                                s.wait_ge(sDV.sem, C[("emu", gi, qt, sl - 2)])
                            s.copy(rzbc[sl % 2][:, :], psS[sl % 2][:, :]
                                   ).then_inc(sEV.sem, 1)
                            s.wait_ge(sDV.sem, C[("emu", gi, qt, sl)])
                            p, tgt = C[("sto", gi, qt, sl)]
                            asx[p].bump(s.dma_start(
                                self.atns[hl * HD:(hl + 1) * HD,
                                          qt * QTW:(qt + 1) * QTW],
                                aout[sl % 2][:, :]), 16)
                            assert asx[p].n == tgt
                sEV.n = q["act"]

            @b.vector
            def _(v):
                for gi, hg in enumerate(groups):
                    H = len(hg)
                    for sl in range(H):
                        v.wait_ge(qm.sem, C[("kq", gi)])
                        if sl >= 1:
                            v.wait_ge(sMM.sem, C[("qnc", gi, sl - 1, NC8 - 1)])
                            v.wait_ge(sMM.sem, C[("knc", gi, sl - 1, NC8 - 1)])
                        v.tensor_tensor(Qsq[:, :], Qful[sl][:, :],
                                        Qful[sl][:, :], ALU.mult
                                        ).then_inc(sDV.sem, 1)
                        v.tensor_tensor(Ksq[:, :], Ksb[sl][:, :],
                                        Ksb[sl][:, :], ALU.mult
                                        ).then_inc(sDV.sem, 1)
                        for c in range(NC8):
                            v.wait_ge(sEV.sem, C[("eqc", gi, sl, c)])
                            v.reduce_max(tmpmx[0:1, 0:1], qnr[0:1, :], AX
                                         ).then_inc(sDV.sem, 1)
                            v.wait_ge(sEV.sem, C[("ekc", gi, sl, c)])
                            v.reduce_max(tmpmx[0:1, 1:2], knr[0:1, :], AX
                                         ).then_inc(sDV.sem, 1)
                            if c == 0:
                                v.tensor_copy(mq2[0:1, sl:sl + 1],
                                              tmpmx[0:1, 0:1])
                                v.tensor_copy(mk2[0:1, sl:sl + 1],
                                              tmpmx[0:1, 1:2])
                            else:
                                v.tensor_tensor(mq2[0:1, sl:sl + 1],
                                                mq2[0:1, sl:sl + 1],
                                                tmpmx[0:1, 0:1], ALU.max)
                                v.tensor_tensor(mk2[0:1, sl:sl + 1],
                                                mk2[0:1, sl:sl + 1],
                                                tmpmx[0:1, 1:2], ALU.max)
                    for sl in range(H):
                        v.tensor_tensor(sumb[0:1, sl:sl + 1],
                                        mq2[0:1, sl:sl + 1],
                                        mk2[0:1, sl:sl + 1], ALU.add)
                    for sl in range(H):
                        v.tensor_scalar(negbf[0:1, sl:sl + 1],
                                        sumb[0:1, sl:sl + 1], C_BOUND, None,
                                        ALU.mult).then_inc(sDV.sem, 1)
                    for qt in range(NQ):
                        J = (qt + 1) * JPQ
                        for jc in range(J):
                            diag = jc >= qt * JPQ
                            for sl in range(H):
                                if diag:
                                    dlt = jc * 128 - qt * QTW
                                    v.wait_ge(sMM.sem, C[("S", gi, qt, jc, sl)])
                                    v.tensor_tensor(
                                        psS[sl % 2][:, :], psS[sl % 2][:, :],
                                        self.mask_sb[:, 384 - dlt:896 - dlt],
                                        ALU.add).then_inc(sDV.sem, 1)
                                if zeng(sl) == "dve":
                                    v.wait_ge(sEV.sem, C[("exp", gi, qt, jc, sl)])
                                    if jc < 2:
                                        v.tensor_copy(Zac[sl][jc % 2][:, :],
                                                      Esb[sl][:, :]
                                                      ).then_inc(sDV.sem, 1)
                                    else:
                                        v.tensor_tensor(Zac[sl][jc % 2][:, :],
                                                        Esb[sl][:, :],
                                                        Zac[sl][jc % 2][:, :],
                                                        ALU.add).then_inc(sDV.sem, 1)
                        for sl in range(H):
                            if zeng(sl) == "dve":
                                v.tensor_tensor(Zfin[sl][:, :], Zac[sl][0][:, :],
                                                Zac[sl][1][:, :], ALU.add
                                                ).then_inc(sDV.sem, 1)
                        for sl in range(H):
                            v.wait_ge(sEV.sem, C[("ezr", gi, qt, sl)])
                            v.reciprocal(rzrow[sl][:, :], zrow[sl][:, :]
                                         ).then_inc(sDV.sem, 1)
                        for sl in range(H):
                            v.wait_ge(sEV.sem, C[("erz", gi, qt, sl)])
                            v.wait_ge(sMM.sem, C[("PV", gi, qt, J - 1, sl)])
                            if C[("aoutw", gi, qt, sl)] is not None:
                                p, tgt = C[("aoutw", gi, qt, sl)]
                                v.wait_ge(asx[p].sem, tgt)
                            v.tensor_tensor(aout[sl % 2][:, :], psO[sl][:, :],
                                            rzbc[sl % 2][:, :], ALU.mult
                                            ).then_inc(sDV.sem, 1)
                sDV.n = q["dve"]

            @b.gpsimd
            def _(g):
                for gi, hg in enumerate(groups):
                    H = len(hg)
                    for qt in range(NQ):
                        J = (qt + 1) * JPQ
                        for jc in range(J):
                            for sl in range(H):
                                if zeng(sl) != "pl":
                                    continue
                                g.wait_ge(sEV.sem, C[("exp", gi, qt, jc, sl)])
                                if jc < 2:
                                    g.tensor_copy(Zac[sl][jc % 2][:, :],
                                                  Esb[sl][:, :]).then_inc(sPL.sem, 1)
                                else:
                                    g.tensor_tensor(Zac[sl][jc % 2][:, :],
                                                    Esb[sl][:, :],
                                                    Zac[sl][jc % 2][:, :],
                                                    ALU.add).then_inc(sPL.sem, 1)
                        for sl in range(H):
                            if zeng(sl) == "pl":
                                g.tensor_tensor(Zfin[sl][:, :], Zac[sl][0][:, :],
                                                Zac[sl][1][:, :], ALU.add
                                                ).then_inc(sPL.sem, 1)
                sPL.n = q["pl"]
        ctx0.close()
        self.p3_stores_final = [(asx[0], asx[0].n), (asx[1], asx[1].n)]


    def build(self, upto=99):
        nc = self.nc
        T, TO, FO = self.T, self.TO, self.FO
        qm, gm, sCC = self.qm, self.gm, self.sCC

        self.prologue()
        if upto >= 1:
            self.p0_dequant()
        else:
            self.cc_wdqL = self.cc_wdqR = 0
        if upto < 2:
            return self.finish(None)

        ln1_done = self.ln_phase("ln1", self.xin, self.xabs)
        ctxr = ExitStack()
        resident = ctxr.enter_context(nc.sbuf_tensor("res1", [128, KC * TO], BF16))
        rr = self.transpose_resident("tr1", self.xabs, resident, TO, ln1_done)
        p1_done = self.gemm_token("p1", resident, rr, 'bf16', self.wdqH,
                                  self.xin, self.x1s, 0.5,
                                  sync_guard=[(self.sDV, self.last_ln_dve)],
                                  strip_guards=[(sCC, self.cc_wdqL),
                                                (sCC, self.cc_wdqR)])
        ctxr.close()
        if upto < 3:
            return self.finish(p1_done)

        ln2_done = self.ln_phase("ln2", self.x1s, self.xbo, guards=p1_done)
        with nc.Block("p2x") as b:
            @b.gpsimd
            def _(g):
                for gd in ln2_done:
                    g.wait_ge(gd[0].sem, gd[1])
                sCC.bump(g.collective_compute(
                    "AllGather", ALU.bypass,
                    replica_groups=[[0, 1], [2, 3], [4, 5], [6, 7]],
                    ins=[self.xbo.opt()],
                    outs=[self.xbf.opt()]), 1)
        cc_xbf = sCC.n
        if upto < 4:
            return self.finish(None)

        for th in range(2):
            ctxh = ExitStack()
            xbT = ctxh.enter_context(
                nc.sbuf_tensor(f"xbT{th}", [128, KC * (T // 2)], BF16))
            TH = T // 2
            xbT_ready = self.transpose_resident(
                f"tr2{th}", self.xbf[th * TH:(th + 1) * TH, :],
                xbT, TH, [(sCC, cc_xbf)])
            self.p2c_half(th, xbT, xbT_ready)
            ctxh.close()
        self.p2_stores_final = [(self.asx[0], self.asx[0].n),
                                (self.asx[1], self.asx[1].n)]
        if upto < 5:
            return self.finish(None)

        self.p3_attention(max_groups=getattr(self, 'p3_groups', 99),
                          max_qt=getattr(self, 'p3_qt', 99))
        if upto < 6:
            return self.finish(self.p3_stores_final)

        ctxr4 = ExitStack()
        res4 = ctxr4.enter_context(
            nc.sbuf_tensor("res4", [128, (FO // 128) * T], BF16))
        with nc.Block("p4r") as b:
            @b.sync
            def _(sync):
                self.flush(sync, [qm])
                for gd in self.p3_stores_final:
                    sync.wait_ge(gd[0].sem, gd[1])
                for kc in range(FO // 128):
                    qm.bump(sync.dma_start(
                        res4[:, kc * T:(kc + 1) * T],
                        self.atns[kc * 128:(kc + 1) * 128, :]), 16)
        asb4 = [self.asx[0].n, self.asx[1].n]
        p4_done = self.gemm_token("p4", res4, [(qm, qm.n)], 'f32', self.woT,
                                  None, self.op4H, 1.0, rows=T, nkc=FO // 128,
                                  out_dt=BF16)
        ctxr4.close()
        nleft = (D // 512 // 2) * (T // 128) // 2
        left_done = [(self.asx[0], asb4[0] + 16 * nleft),
                     (self.asx[1], asb4[1] + 16 * nleft)]
        with nc.Block("p4s") as b:
            @b.gpsimd
            def _(g):
                for gd in left_done:
                    g.wait_ge(gd[0].sem, gd[1])
                sCC.bump(g.collective_compute(
                    "ReduceScatter", ALU.add,
                    replica_groups=[[0, 1], [2, 3], [4, 5], [6, 7]],
                    ins=[self.op4H[0].opt()],
                    outs=[self.x2pH[0].opt()]), 1)
                for gd in p4_done:
                    g.wait_ge(gd[0].sem, gd[1])
                sCC.bump(g.collective_compute(
                    "ReduceScatter", ALU.add,
                    replica_groups=[[0, 1], [2, 3], [4, 5], [6, 7]],
                    ins=[self.op4H[1].opt()],
                    outs=[self.x2pH[1].opt()]), 1)
        cc_rs = sCC.n
        if upto < 7:
            return self.finish(None)

        ln3_done = self.ln_phase("ln3", self.x2pH, self.xabs,
                                 guards=[(sCC, cc_rs)],
                                 src2=self.x1s, sum_dst=self.x2s)
        ctxr5 = ExitStack()
        res5 = ctxr5.enter_context(nc.sbuf_tensor("res5", [128, KC * TO], BF16))
        rr5 = self.transpose_resident("tr5", self.xabs, res5, TO, ln3_done)
        p5_done = self.gemm_token("p5", res5, rr5, 'bf16', self.wdqH,
                                  self.x2s, self.xout, 0.5,
                                  sync_guard=[(self.sDV, self.last_ln_dve)] + ln3_done)
        ctxr5.close()
        return self.finish(p5_done)

    def finish(self, out_done):
        nc = self.nc
        gm = self.gm
        with nc.Block("fin") as b:
            @b.scalar
            def _(s):
                if out_done is not None:
                    for gd in out_done:
                        s.wait_ge(gd[0].sem, gd[1])

            if self.dbg_views:
                @b.gpsimd
                def _(g):
                    if out_done is not None:
                        for gd in out_done:
                            g.wait_ge(gd[0].sem, gd[1])
                    srcs = {"xbf": self.xbf, "x1s": self.x1s,
                            "x2s": self.x2s, "kts": self.kts, "qts": self.qts,
                            "vts": self.vts, "atns": self.atns}
                    for nm, dst in self.dbg_views.items():
                        gm.bump(g.dma_start(dst[:, :], srcs[nm][:, :]), 16)
                    g.wait_ge(gm.sem, gm.n)
        self.ctx.close()
        return nc


def build_program(T, dbg=False):
    return Builder(T, dbg)


def host_prep(L0_attn_x, L0_attn_wq, L0_attn_wk, L0_attn_wv, L0_attn_wo,
              L0_ffn_weight_packed, L0_ffn_centroids, L0_ffn_scales):
    T = L0_attn_x.shape[1]
    TO = T // 2
    x = np.ascontiguousarray(np.asarray(L0_attn_x))
    wqT = np.ascontiguousarray(np.asarray(L0_attn_wq).T)
    wkT = np.ascontiguousarray(np.asarray(L0_attn_wk).T)
    wvT = np.ascontiguousarray(np.asarray(L0_attn_wv).T)
    woT = np.ascontiguousarray(np.asarray(L0_attn_wo).T)
    pakT = np.ascontiguousarray(np.asarray(L0_ffn_weight_packed).T)
    scales = np.asarray(L0_ffn_scales)
    cent = np.asarray(L0_ffn_centroids).reshape(1, 16).astype(np.float32)
    sinT, cosT, maskneg = host_tables(T)

    in_maps = []
    for c in range(8):
        s, par = c // 2, c % 2
        fsl = np.arange(c * (D // 8), (c + 1) * (D // 8))
        sclx = np.ascontiguousarray(scales.T[fsl // 64, :])
        hb = par * (NHG // 2)
        hc = slice(hb * HD, (hb + NHG // 2) * HD)
        in_maps.append({
            "xin": np.ascontiguousarray(x[s, par * TO:(par + 1) * TO, :]),
            "wkT": np.ascontiguousarray(wkT[:, hc]),
            "wvT": np.ascontiguousarray(wvT[:, hc]),
            "wqT": np.ascontiguousarray(wqT[:, hc]),
            "woT": np.ascontiguousarray(woT[hb * HD:(hb + NHG // 2) * HD, :]),
            "pakT": np.ascontiguousarray(pakT[c * (D // 16):(c + 1) * (D // 16), :]),
            "sclx": sclx,
            "cent": cent,
            "sinT": sinT,
            "cosT": cosT,
            "maskneg": maskneg,
            "cinfo": np.array([[par * TO, par, c, s, 0, 0, 0, 0]], np.int32),
        })
    return T, in_maps


def kernel(L0_attn_x, L0_attn_wq, L0_attn_wk, L0_attn_wv, L0_attn_wo,
           L0_ffn_weight_packed, L0_ffn_centroids, L0_ffn_scales):
    T, in_maps = host_prep(L0_attn_x, L0_attn_wq, L0_attn_wk, L0_attn_wv,
                           L0_attn_wo, L0_ffn_weight_packed, L0_ffn_centroids,
                           L0_ffn_scales)
    TO = T // 2
    bld = build_program(T, dbg=False)
    nc = bld.build()
    res = run_bass_kernel_spmd(nc, in_maps, core_ids=list(range(8)))
    out = np.empty((4, T, D), np.float32)
    for c in range(8):
        s, par = c // 2, c % 2
        out[s, par * TO:(par + 1) * TO, :] = res.results[c]["xout"]
    return out


# version 07
# speedup vs baseline: 1.0399x; 1.0033x over previous
import sys
if '/opt/trn_rl_repo' not in sys.path:
    sys.path.insert(0, '/opt/trn_rl_repo')
import math
from contextlib import ExitStack

import numpy as np
import ml_dtypes

import concourse.bass as bass
import concourse.mybir as mybir
from concourse.bass_utils import run_bass_kernel_spmd

F32 = mybir.dt.float32
BF16 = mybir.dt.bfloat16
I32 = mybir.dt.int32
AF = mybir.ActivationFunctionType
ALU = mybir.AluOpType
AX = mybir.AxisListType.X

D = 4096
HD = 128
NHG = 32
KC = D // 128
QTW = 512
INV_SQRT_HD = np.float32(1.0 / math.sqrt(HD))
C_BOUND = float(-1.0 / (2.0 * math.sqrt(HD)))
EPS_F = np.float32(1e-5)


class Ctr:

    def __init__(self, sem):
        self.sem = sem
        self.n = 0

    def bump(self, inst, k=1):
        inst.then_inc(self.sem, k)
        self.n += k
        return self.n


def host_tables(T):
    pos = np.arange(T, dtype=np.float32)
    thetas = (1.0 / (10000.0 ** (np.arange(0, HD, 2, dtype=np.float32) / HD))).astype(np.float32)
    pt = np.outer(pos, thetas).astype(np.float32)
    pt = np.concatenate([pt, pt], axis=-1)
    sinT = np.ascontiguousarray(np.sin(pt).astype(np.float32).T)
    cosT = np.ascontiguousarray(np.cos(pt).astype(np.float32).T)
    j = np.arange(128)[:, None]
    u = np.arange(896)[None, :]
    maskneg = np.where(j <= u - 384, 0.0, -1e9).astype(np.float32)
    return sinT, cosT, maskneg


class Builder:
    def __init__(self, T, dbg=False):
        self.T = T
        self.TO = T // 2
        self.NH = NHG // 2
        self.FO = self.NH * HD
        self.NQ = T // QTW
        self.dbg = dbg
        nc = bass.Bass("TRN2", target_bir_lowering=False,
                       detect_race_conditions=False)
        self.nc = nc
        self.ctx = ExitStack()
        TO, FO = self.TO, self.FO

        di = lambda name, shape, dt: nc.dram_tensor(name, shape, dt, kind="ExternalInput").ap()
        self.xin = di("xin", [TO, D], F32)
        self.wkT = di("wkT", [D, FO], F32)
        self.wvT = di("wvT", [D, FO], F32)
        self.wqT = di("wqT", [D, FO], F32)
        self.woT = di("woT", [FO, D], F32)
        self.pakT = di("pakT", [D // 16, D], I32)
        self.sclx = di("sclx", [D // 8, D], F32)
        self.cent = di("cent", [1, 16], F32)
        self.sinTd = di("sinT", [HD, T], F32)
        self.cosTd = di("cosT", [HD, T], F32)
        self.maskd = di("maskneg", [128, 896], F32)
        self.cinfo = di("cinfo", [1, 8], I32)
        self.xout = nc.dram_tensor("xout", [TO, D], F32, kind="ExternalOutput").ap()

        dsc = lambda name, shape, dt: nc.dram_tensor(name, shape, dt, kind="Internal").ap()
        self.wdq_inH = [dsc("wdq_in0", [D // 8, D // 2], BF16),
                        dsc("wdq_in1", [D // 8, D // 2], BF16)]
        dsh = lambda name, shape, dt: nc.dram_tensor(
            name, shape, dt, kind="Internal", addr_space="Shared").ap()
        self.wdqH = [dsh("wdq0", [D, D // 2], BF16),
                     dsh("wdq1", [D, D // 2], BF16)]
        self.xabs = dsc("xabs", [TO, D], BF16)
        self.xbo = dsc("xbo", [TO, D], BF16)
        self.xbf = dsc("xbf", [T, D], BF16)
        self.x1s = dsc("x1s", [TO, D], F32)
        self.x2s = dsc("x2s", [TO, D], F32)
        self.kts = dsc("kts", [FO, T], BF16)
        self.qts = dsc("qts", [FO, T], BF16)
        self.vts = dsc("vts", [T, FO], BF16)
        self.atns = dsc("atns", [FO, T], BF16)
        self.op4H = [dsc("op4l", [T, D // 2], BF16), dsc("op4r", [T, D // 2], BF16)]
        self.x2pH = [dsc("x2pl", [TO, D // 2], BF16), dsc("x2pr", [TO, D // 2], BF16)]
        self.dbg_views = {}
        if dbg:
            for nm, ap_ in [("xbf", self.xbf),
                            ("x1s", self.x1s), ("x2s", self.x2s),
                            ("kts", self.kts), ("qts", self.qts),
                            ("vts", self.vts), ("atns", self.atns)]:
                self.dbg_views[nm] = nc.dram_tensor(
                    "dbg_" + nm, list(ap_.tensor.shape), ap_.dtype,
                    kind="ExternalOutput").ap()

        sem = lambda name: self.ctx.enter_context(nc.semaphore(name))
        self.qm = Ctr(sem("qm"))
        self.qx = [Ctr(sem("qx0")), Ctr(sem("qx1"))]
        self.qw = [Ctr(sem("qw0")), Ctr(sem("qw1"))]
        self.qv = [Ctr(sem("qv0")), Ctr(sem("qv1")), Ctr(sem("qv2"))]
        self.gm = Ctr(sem("gm"))
        self.gw = [Ctr(sem("gw0")), Ctr(sem("gw1")), Ctr(sem("gw2"))]
        self.gs = [Ctr(sem("gs0")), Ctr(sem("gs1"))]
        self.am = Ctr(sem("am"))
        self.asx = [Ctr(sem("as0")), Ctr(sem("as1"))]
        self.sMM = Ctr(sem("sMM"))
        self.sEV = Ctr(sem("sEV"))
        self.sDV = Ctr(sem("sDV"))
        self.sPL = Ctr(sem("sPL"))
        self.sCC = Ctr(sem("sCC"))

        sb = lambda name, shape, dt: self.ctx.enter_context(nc.sbuf_tensor(name, shape, dt))
        self.ones_col = sb("ones_col", [128, 1], F32)
        self.ones_col_bf = sb("ones_col_bf", [128, 1], BF16)
        self.ones_row = sb("ones_row", [1, 128], F32)
        self.mask_sb = sb("mask_sb", [128, 896], F32)
        self.cent_bc = sb("cent_bc", [128, 16], F32)
        self.cinfo_sb = sb("cinfo_sb", [1, 8], I32)

    @staticmethod
    def flush(eng, ctrs):
        for c in ctrs:
            if c.n > 0:
                eng.wait_ge(c.sem, c.n)

    def prologue(self):
        nc = self.nc
        qm, gm, sMM, sEV, sDV = self.qm, self.gm, self.sMM, self.sEV, self.sDV
        ctxp = ExitStack()
        pb = ctxp.enter_context(nc.psum_tensor("pb_cent", [128, 16], F32))
        with nc.Block("prologue") as b:
            @b.sync
            def _(sync):
                qm.bump(sync.dma_start(self.mask_sb[:, :], self.maskd[:, :]), 16)
                qm.bump(sync.dma_start(self.cinfo_sb[:, :], self.cinfo[:, :]), 16)
                sync.wait_ge(qm.sem, qm.n)

            @b.vector
            def _(v):
                v.memset(self.ones_col[:, :], 1.0)
                v.memset(self.ones_col_bf[:, :], 1.0)
                sDV.bump(v.memset(self.ones_row[:, :], 1.0), 1)

            @b.gpsimd
            def _(g):
                gm.bump(g.dma_start(self.cent_bc[0:1, 0:16], self.cent[:, :]), 16)

            @b.tensor
            def _(t):
                t.wait_ge(gm.sem, gm.n)
                t.wait_ge(sDV.sem, sDV.n)
                sMM.bump(t.matmul(pb[:, :], self.ones_row[:, :],
                                  self.cent_bc[0:1, 0:16], start=True, stop=True), 1)

            @b.scalar
            def _(s):
                s.wait_ge(sMM.sem, sMM.n)
                sEV.bump(s.copy(self.cent_bc[:, :], pb[:, :]), 1)
        ctxp.close()
        self.ev_cent = sEV.n

    def _deq_slab(self, eng, ctr, bufset, nibb_ci, g0, g1, l0, whlh,
                  sclh, scll, q_scl, ext_t, gm_st):
        qm, gm = self.qm, self.gm
        terms, accs = bufset
        w = g1 - g0
        eng.wait_ge(qm.sem, q_scl)
        eng.wait_ge(self.sDV.sem, ext_t)
        eng.wait_ge(self.sEV.sem, self.ev_cent)
        if gm_st is not None:
            eng.wait_ge(gm.sem, gm_st)
        nh = nibb_ci[:, g0:g1]
        nl = nibb_ci[:, D + g0:D + g1]
        tb = lambda j: terms[j % 4]
        ab = lambda j: accs[j % 2]
        done = 0
        for j in range(16):
            eng.tensor_scalar(tb(j)[:, 0:w], nh, float(j),
                              self.cent_bc[:, j:j + 1], ALU.is_equal, ALU.mult)
            eng.tensor_scalar(tb(j)[:, w:2 * w], nl, float(j),
                              self.cent_bc[:, j:j + 1], ALU.is_equal, ALU.mult)
            if j >= 2:
                jj = done
                if jj < 2:
                    eng.tensor_copy(ab(jj)[:, 0:2 * w], tb(jj)[:, 0:2 * w])
                else:
                    eng.tensor_tensor(ab(jj)[:, 0:2 * w], ab(jj)[:, 0:2 * w],
                                      tb(jj)[:, 0:2 * w], ALU.add)
                done += 1
        v = ctr.bump(eng.engine_nop(), 1)
        eng.wait_ge(ctr.sem, v)
        while done < 16:
            jj = done
            eng.tensor_tensor(ab(jj)[:, 0:2 * w], ab(jj)[:, 0:2 * w],
                              tb(jj)[:, 0:2 * w], ALU.add)
            done += 1
        v = ctr.bump(eng.engine_nop(), 1)
        eng.wait_ge(ctr.sem, v)
        eng.tensor_tensor(accs[0][:, 0:2 * w], accs[0][:, 0:2 * w],
                          accs[1][:, 0:2 * w], ALU.add)
        v = ctr.bump(eng.engine_nop(), 1)
        eng.wait_ge(ctr.sem, v)
        eng.tensor_tensor(whlh[:, l0:l0 + w], accs[0][:, 0:w],
                          sclh[:, l0:l0 + w], ALU.mult)
        ctr.bump(eng.tensor_tensor(whlh[:, 2048 + l0:2048 + l0 + w],
                                   accs[0][:, w:2 * w],
                                   scll[:, l0:l0 + w], ALU.mult), 1)

    def p0_dequant(self):
        nc = self.nc
        qm, gm, sDV, sPL, sCC = self.qm, self.gm, self.sDV, self.sPL, self.sCC
        CW = 512
        HW2 = 2048
        ctx0 = ExitStack()
        sb = lambda n, s, d: ctx0.enter_context(nc.sbuf_tensor(n, s, d))
        pak = [sb(f"pak{i}", [128, D], I32) for i in range(2)]
        nibi = sb("nibi", [128, 2 * D], I32)
        nibb = [sb(f"nibb{i}", [128, 2 * D], BF16) for i in range(2)]
        whlh = sb("whlh", [128, 2 * HW2], BF16)
        sclh = sb("sclh", [128, HW2], F32)
        scll = sb("scll", [128, HW2], F32)

        def mkset(tag):
            terms = [sb(f"{tag}_t{i}", [128, 2 * CW], F32) for i in range(4)]
            accs = [sb(f"{tag}_a{i}", [128, 2 * CW], F32) for i in range(2)]
            return (terms, accs)

        set_dv = mkset("ddv")
        set_pl = mkset("dpl")

        qb, gb = qm.n, gm.n
        db, plb = sDV.n, sPL.n
        NDS, NPS = 3, 1
        ext_t = lambda ci: db + ci + 1
        dv_done = lambda s: db + 2 + 4 * NDS * (s + 1)
        pl_done = lambda s: plb + 4 * NPS * (s + 1)
        q_pak = lambda ci: qb + 16 * (ci + 1)
        q_scl = lambda s: qb + 32 + 32 * (s + 1)
        gm_st = lambda s: gb + 32 * (s + 1)

        with nc.Block("p0") as b:
            @b.sync
            def _(sync):
                for ci in range(2):
                    qm.bump(sync.dma_start(pak[ci][:, :],
                                           self.pakT[ci * 128:(ci + 1) * 128, :]), 16)
                for s in range(4):
                    hf, ci = s // 2, s % 2
                    if s >= 1:
                        sync.wait_ge(sDV.sem, dv_done(s - 1))
                        sync.wait_ge(sPL.sem, pl_done(s - 1))
                    qm.bump(sync.dma_start(
                        sclh[:, :],
                        self.sclx[ci * 256:(ci + 1) * 256:2,
                                  hf * HW2:(hf + 1) * HW2]), 16)
                    qm.bump(sync.dma_start(
                        scll[:, :],
                        self.sclx[ci * 256 + 1:(ci + 1) * 256:2,
                                  hf * HW2:(hf + 1) * HW2]), 16)

            @b.vector
            def _(v):
                for ci in range(2):
                    v.wait_ge(qm.sem, q_pak(ci))
                    v.tensor_scalar(nibi[:, 0:D], pak[ci][:, :], 4, 15,
                                    ALU.logical_shift_right, ALU.bitwise_and)
                    v.tensor_scalar(nibi[:, D:2 * D], pak[ci][:, :], 15, None,
                                    ALU.bitwise_and)
                    v.tensor_copy(nibb[ci][:, 0:D], nibi[:, 0:D])
                    sDV.bump(v.tensor_copy(nibb[ci][:, D:2 * D],
                                           nibi[:, D:2 * D]), 1)
                for s in range(4):
                    hf, ci = s // 2, s % 2
                    for k in range(NDS):
                        self._deq_slab(v, sDV, set_dv, nibb[ci],
                                       hf * HW2 + k * CW, hf * HW2 + (k + 1) * CW,
                                       k * CW, whlh, sclh, scll,
                                       q_scl(s), ext_t(ci),
                                       gm_st(s - 1) if s >= 1 else None)

            @b.gpsimd
            def _(g):
                for s in range(4):
                    hf, ci = s // 2, s % 2
                    for k in range(NPS):
                        self._deq_slab(g, sPL, set_pl, nibb[ci],
                                       hf * HW2 + (NDS + k) * CW,
                                       hf * HW2 + (NDS + k + 1) * CW,
                                       (NDS + k) * CW, whlh, sclh, scll,
                                       q_scl(s), ext_t(ci),
                                       gm_st(s - 1) if s >= 1 else None)
                    g.wait_ge(sDV.sem, dv_done(s))
                    g.wait_ge(sPL.sem, pl_done(s))
                    base = ci * 256
                    gm.bump(g.dma_start(
                        self.wdq_inH[hf][base:base + 256:2, :],
                        whlh[:, 0:HW2]), 16)
                    gm.bump(g.dma_start(
                        self.wdq_inH[hf][base + 1:base + 256:2, :],
                        whlh[:, HW2:2 * HW2]), 16)
                    if ci == 1:
                        g.wait_ge(gm.sem, gm_st(s))
                        sCC.bump(g.collective_compute(
                            "AllGather", ALU.bypass,
                            replica_groups=[list(range(8))],
                            ins=[self.wdq_inH[hf].opt()],
                            outs=[self.wdqH[hf].opt()]), 1)
        ctx0.close()
        self.cc_wdqL = sCC.n - 1
        self.cc_wdqR = sCC.n

    def ln_phase(self, name, src, dst, guards=(), src2=None, sum_dst=None):
        nc = self.nc
        qx, qw, gs, sEV, sDV = self.qx, self.qw, self.gs, self.sEV, self.sDV
        nt = self.TO // 128
        KI = 5 if src2 is not None else 4
        NSR = 2 if src2 is not None else 1
        ctx0 = ExitStack()
        sb = lambda n, s, d: ctx0.enter_context(nc.sbuf_tensor(n, s, d))
        src_halves = isinstance(src, list)
        sdt = (src[0] if src_halves else src).dtype
        src_dt = F32 if sdt == F32 else BF16
        xln = [sb(f"{name}_x{i}", [128, D], src_dt) for i in range(2)]
        xln2 = [sb(f"{name}_y{i}", [128, D], F32) for i in range(2)] if src2 is not None else None
        xsum = xln if src2 is None else [sb(f"{name}_u{i}", [128, D], F32) for i in range(2)]
        xa = [sb(f"{name}_a{i}", [128, D], BF16) for i in range(2)]
        stats = [sb(f"{name}_s{i}", [128, 48], F32) for i in range(2)]
        aggr = [sb(f"{name}_g{i}", [128, 2], F32) for i in range(2)]
        lnv = [sb(f"{name}_l{i}", [128, 1], F32) for i in range(2)]
        rstd = [sb(f"{name}_r{i}", [128, 1], F32) for i in range(2)]
        nmr = [sb(f"{name}_n{i}", [128, 1], F32) for i in range(2)]
        eb, db = sEV.n, sDV.n
        qxb = [qx[0].n, qx[1].n]
        qwb = [qw[0].n, qw[1].n]
        gsb = [gs[0].n, gs[1].n]
        NLD = 2 if src_halves else 1
        xt = lambda i: qxb[i % 2] + 16 * NLD * (i // 2 + 1)
        x2t = lambda i: qwb[i % 2] + 16 * (i // 2 + 1)
        st = lambda i: gsb[i % 2] + 16 * NSR * (i // 2 + 1)
        d_sum = lambda i: db + KI * i + 1
        d_statsnop = lambda i: db + KI * i + KI - 3
        d_aggr = lambda i: db + KI * i + KI - 2
        d_nmrnop = lambda i: db + KI * i + KI - 1
        d_norm = lambda i: db + KI * i + KI
        with nc.Block(name) as b:
            @b.sync
            def _(sync):
                self.flush(sync, qx + (qw if src2 is not None else []))
                for gd in guards:
                    sync.wait_ge(gd[0].sem, gd[1])
                for i in range(nt):
                    if i >= 2:
                        sync.wait_ge(sDV.sem, d_norm(i - 2))
                    if src_halves:
                        qx[i % 2].bump(sync.dma_start(
                            xln[i % 2][:, 0:D // 2],
                            src[0][i * 128:(i + 1) * 128, :]), 16)
                        qx[i % 2].bump(sync.dma_start(
                            xln[i % 2][:, D // 2:D],
                            src[1][i * 128:(i + 1) * 128, :]), 16)
                    else:
                        qx[i % 2].bump(sync.dma_start(
                            xln[i % 2][:, :], src[i * 128:(i + 1) * 128, :]), 16)
                    if src2 is not None:
                        qw[i % 2].bump(sync.dma_start(
                            xln2[i % 2][:, :], src2[i * 128:(i + 1) * 128, :]), 16)

            @b.vector
            def _(v):
                for i in range(nt):
                    p = i % 2
                    v.wait_ge(qx[p].sem, xt(i))
                    if src2 is not None:
                        v.wait_ge(qw[p].sem, x2t(i))
                        if i >= 2:
                            v.wait_ge(gs[p].sem, st(i - 2))
                        w = sDV.bump(v.tensor_tensor(xsum[p][:, :], xln[p][:, :],
                                                     xln2[p][:, :], ALU.add), 1)
                        v.wait_ge(sDV.sem, w)
                    for c in range(8):
                        v.bn_stats(stats[p][:, 6 * c:6 * c + 6],
                                   xsum[p][:, 512 * c:512 * (c + 1)])
                    w = sDV.bump(v.engine_nop(), 1)
                    v.wait_ge(sDV.sem, w)
                    sDV.bump(v.bn_aggr(aggr[p][:, :], stats[p][:, :]), 1)
                    v.wait_ge(sEV.sem, eb + 2 * i + 2)
                    v.scalar_tensor_tensor(nmr[p][:, :], aggr[p][:, 0:1], -1.0,
                                           rstd[p][:, 0:1], ALU.mult, ALU.mult)
                    w = sDV.bump(v.engine_nop(), 1)
                    v.wait_ge(sDV.sem, w)
                    if i >= 2:
                        v.wait_ge(gs[p].sem, st(i - 2))
                    sDV.bump(v.tensor_scalar(xa[p][:, :], xsum[p][:, :],
                                             rstd[p][:, 0:1], nmr[p][:, 0:1],
                                             ALU.mult, ALU.add), 1)

            @b.scalar
            def _(s):
                for i in range(nt):
                    p = i % 2
                    s.wait_ge(sDV.sem, d_aggr(i))
                    w = sEV.bump(s.activation(lnv[p][:, :], aggr[p][:, 1:2],
                                              AF.Ln, bias=EPS_F), 1)
                    s.wait_ge(sEV.sem, w)
                    sEV.bump(s.activation(rstd[p][:, :], lnv[p][:, :],
                                          AF.Exp, scale=np.float32(-0.5)), 1)

            @b.gpsimd
            def _(g):
                self.flush(g, gs)
                for i in range(nt):
                    g.wait_ge(sDV.sem, d_norm(i))
                    if src2 is not None:
                        gs[i % 2].bump(g.dma_start(
                            sum_dst[i * 128:(i + 1) * 128, :], xsum[i % 2][:, :]), 16)
                    gs[i % 2].bump(g.dma_start(
                        dst[i * 128:(i + 1) * 128, :], xa[i % 2][:, :]), 16)
        ctx0.close()
        self.last_ln_dve = sDV.n
        return [(gs[0], gs[0].n), (gs[1], gs[1].n)]

    def transpose_resident(self, name, src, resident, rows, src_ready):
        nc = self.nc
        am = self.am
        with nc.Block(name) as b:
            @b.scalar
            def _(s):
                self.flush(s, [am])
                for rd in src_ready:
                    s.wait_ge(rd[0].sem, rd[1])
                for kc in range(KC):
                    am.bump(s.dma_start(resident[:, kc * rows:(kc + 1) * rows],
                                        src[0:rows, kc * 128:(kc + 1) * 128],
                                        transpose=True), 16)
        return [(am, am.n)]

    def gemm_token(self, name, resident, res_ready, wsrc_kind, wsrc, resid_src,
                   out_dst, scale, sync_guard=(), rows=None, nkc=KC, out_dt=F32,
                   strip_guards=None):
        nc = self.nc
        qx, qw, gw, asx = self.qx, self.qw, self.gw, self.asx
        sMM, sDV = self.sMM, self.sDV
        rows = rows if rows is not None else self.TO
        MC = rows // 128
        NS = D // 512
        ctx0 = ExitStack()
        sb = lambda n, s, d: ctx0.enter_context(nc.sbuf_tensor(n, s, d))
        wst = [sb(f"{name}_w{i}", [128, nkc * 512], BF16) for i in range(2)]
        rbuf = [sb(f"{name}_r{i}", [128, 512], F32) for i in range(2)] \
            if resid_src is not None else None
        obuf = [sb(f"{name}_o{i}", [128, 512], out_dt) for i in range(2)]
        ps = [ctx0.enter_context(nc.psum_tensor(f"{name}_p{i}", [128, 512], F32))
              for i in range(2)]
        mb, db = sMM.n, sDV.n
        qwb = [qw[0].n, qw[1].n]
        gwb = [gw[0].n, gw[1].n]
        qxb = [qx[0].n, qx[1].n]
        asb = [asx[0].n, asx[1].n]
        strip_t = lambda ns: (qwb if wsrc_kind == 'bf16' else gwb)[ns % 2] + 16 * (ns // 2 + 1)
        resid_t = lambda g: qxb[g % 2] + 16 * (g // 2 + 1)
        store_t = lambda g: asb[g % 2] + 16 * (g // 2 + 1)

        def strip_dma(eng, ns):
            if isinstance(wsrc, list):
                nh = NS // 2
                w_ap = wsrc[ns // nh][:, (ns % nh) * 512:((ns % nh) + 1) * 512]
            else:
                w_ap = wsrc[:, ns * 512:(ns + 1) * 512]
            return eng.dma_start(
                wst[ns % 2][:, :].rearrange("p (kc n) -> p kc n", kc=nkc),
                w_ap.rearrange("(kc p) n -> p kc n", p=128))

        with nc.Block(name) as b:
            @b.sync
            def _(sync):
                self.flush(sync, qx + (qw if wsrc_kind == 'bf16' else []))
                for gd in sync_guard:
                    sync.wait_ge(gd[0].sem, gd[1])
                for ns in range(NS):
                    if wsrc_kind == 'bf16':
                        if strip_guards is not None:
                            gd = strip_guards[ns // (NS // 2)]
                            sync.wait_ge(gd[0].sem, gd[1])
                        if ns >= 2:
                            sync.wait_ge(sMM.sem, mb + (ns - 1) * MC)
                        qw[ns % 2].bump(strip_dma(sync, ns), 16)
                    if resid_src is not None:
                        for m in range(MC):
                            g = ns * MC + m
                            if g >= 2:
                                sync.wait_ge(sDV.sem, db + g - 1)
                            qx[g % 2].bump(sync.dma_start(
                                rbuf[g % 2][:, :],
                                resid_src[m * 128:(m + 1) * 128,
                                          ns * 512:(ns + 1) * 512]), 16)

            if wsrc_kind == 'f32':
                @b.gpsimd
                def _(gp):
                    self.flush(gp, gw)
                    for ns in range(NS):
                        if ns >= 2:
                            gp.wait_ge(sMM.sem, mb + (ns - 1) * MC)
                        gw[ns % 2].bump(strip_dma(gp, ns), 16)

            @b.tensor
            def _(t):
                for rd in res_ready:
                    t.wait_ge(rd[0].sem, rd[1])
                for ns in range(NS):
                    sctr = (qw if wsrc_kind == 'bf16' else gw)[ns % 2]
                    t.wait_ge(sctr.sem, strip_t(ns))
                    for m in range(MC):
                        g = ns * MC + m
                        if g >= 2:
                            t.wait_ge(sDV.sem, db + g - 1)
                        for k in range(nkc):
                            inst = t.matmul(
                                ps[g % 2][:, :],
                                resident[:, k * rows + m * 128: k * rows + (m + 1) * 128],
                                wst[ns % 2][:, k * 512:(k + 1) * 512],
                                start=(k == 0), stop=(k == nkc - 1))
                        sMM.bump(inst, 1)

            @b.vector
            def _(v):
                for ns in range(NS):
                    for m in range(MC):
                        g = ns * MC + m
                        v.wait_ge(sMM.sem, mb + g + 1)
                        if resid_src is not None:
                            v.wait_ge(qx[g % 2].sem, resid_t(g))
                        if g >= 2:
                            v.wait_ge(asx[g % 2].sem, store_t(g - 2))
                        if resid_src is not None:
                            sDV.bump(v.scalar_tensor_tensor(
                                obuf[g % 2][:, :], ps[g % 2][:, :], scale,
                                rbuf[g % 2][:, :], ALU.mult, ALU.add), 1)
                        else:
                            sDV.bump(v.tensor_scalar(
                                obuf[g % 2][:, :], ps[g % 2][:, :], scale, None,
                                ALU.mult), 1)

            @b.scalar
            def _(s):
                self.flush(s, asx)
                for ns in range(NS):
                    if isinstance(out_dst, list):
                        nh = NS // 2
                        od = out_dst[ns // nh]
                        oc = (ns % nh) * 512
                    else:
                        od = out_dst
                        oc = ns * 512
                    for m in range(MC):
                        g = ns * MC + m
                        s.wait_ge(sDV.sem, db + g + 1)
                        asx[g % 2].bump(s.dma_start(
                            od[m * 128:(m + 1) * 128, oc:oc + 512],
                            obuf[g % 2][:, :]), 16)
        ctx0.close()
        return [(asx[0], asx[0].n), (asx[1], asx[1].n)]

    def p2c_half(self, th, xbT, xbT_ready):
        nc = self.nc
        T, TH = self.T, self.T // 2
        FO, NH = self.FO, self.NH
        qm, gw, asx = self.qm, self.gw, self.asx
        sMM, sEV, sDV = self.sMM, self.sEV, self.sDV
        tc0 = th * TH

        jobs = [("k", hl) for hl in range(NH)] + [("q", hl) for hl in range(NH)]
        sin_ready = []
        ctxb = ExitStack()
        sbb = lambda n, s, d: ctxb.enter_context(nc.sbuf_tensor(n, s, d))
        sinb = sbb(f"sin{th}", [128, TH], BF16)
        cosb = sbb(f"cos{th}", [128, TH], BF16)
        wst = [sbb(f"j{th}w{i}", [128, KC * HD], BF16) for i in range(3)]
        kqo = [sbb(f"j{th}o{i}", [128, TH], BF16) for i in range(2)]
        rt = [sbb(f"j{th}t{i}", [64, TH], F32) for i in range(4)]
        psK = [ctxb.enter_context(nc.psum_tensor(f"j{th}p{i}", [128, TH], F32))
               for i in range(2)]
        mb, db = sMM.n, sDV.n
        gwb = [gw[0].n, gw[1].n, gw[2].n]
        asb = [asx[0].n, asx[1].n]
        strip_t = lambda i: gwb[i % 3] + 16 * (i // 3 + 1)
        store_t = lambda i: asb[i % 2] + 16 * (i // 2 + 1)

        gm = self.gm
        with nc.Block(f"p2b{th}") as b:
            @b.gpsimd
            def _(gp):
                self.flush(gp, [gm])
                gm.bump(gp.dma_start(sinb[:, :],
                                     self.sinTd[:, tc0:tc0 + TH]), 16)
                gm.bump(gp.dma_start(cosb[:, :],
                                     self.cosTd[:, tc0:tc0 + TH]), 16)
                sin_ready.append((gm, gm.n))
                self.flush(gp, gw)
                for i, (kind, hl) in enumerate(jobs):
                    wsrc = self.wkT if kind == "k" else self.wqT
                    if i >= 3:
                        gp.wait_ge(sMM.sem, mb + i - 2)
                    gw[i % 3].bump(gp.dma_start(
                        wst[i % 3][:, :].rearrange("p (kc n) -> p kc n", kc=KC),
                        wsrc[:, hl * HD:(hl + 1) * HD].rearrange(
                            "(kc p) n -> p kc n", p=128)), 16)

            @b.tensor
            def _(t):
                for rd in xbT_ready:
                    t.wait_ge(rd[0].sem, rd[1])
                for i, (kind, hl) in enumerate(jobs):
                    t.wait_ge(gw[i % 3].sem, strip_t(i))
                    if i >= 2:
                        t.wait_ge(sDV.sem, db + i - 1)
                    for k in range(KC):
                        for nb in range(TH // 512):
                            inst = t.matmul(
                                psK[i % 2][:, nb * 512:(nb + 1) * 512],
                                wst[i % 3][:, k * HD:(k + 1) * HD],
                                xbT[:, k * TH + nb * 512: k * TH + (nb + 1) * 512],
                                start=(k == 0), stop=(k == KC - 1))
                    sMM.bump(inst, 1)

            @b.vector
            def _(v):
                for rd in sin_ready:
                    v.wait_ge(rd[0].sem, rd[1])
                for i, (kind, hl) in enumerate(jobs):
                    p = i % 2
                    v.wait_ge(sMM.sem, mb + i + 1)
                    if i >= 2:
                        v.wait_ge(asx[p].sem, store_t(i - 2))
                    pk = psK[p]
                    v.tensor_tensor(rt[0][:, :], sinb[0:64, :], pk[64:128, :], ALU.mult)
                    v.tensor_tensor(rt[1][:, :], cosb[0:64, :], pk[0:64, :], ALU.mult)
                    v.tensor_tensor(rt[2][:, :], sinb[64:128, :], pk[0:64, :], ALU.mult)
                    v.tensor_tensor(rt[3][:, :], cosb[64:128, :], pk[64:128, :], ALU.mult)
                    v.tensor_tensor(kqo[p][0:64, :], rt[1][:, :], rt[0][:, :], ALU.subtract)
                    sDV.bump(v.tensor_tensor(kqo[p][64:128, :], rt[3][:, :],
                                             rt[2][:, :], ALU.add), 1)

            @b.scalar
            def _(s):
                self.flush(s, asx)
                for i, (kind, hl) in enumerate(jobs):
                    s.wait_ge(sDV.sem, db + i + 1)
                    dst = self.kts if kind == "k" else self.qts
                    asx[i % 2].bump(s.dma_start(
                        dst[hl * HD:(hl + 1) * HD, tc0:tc0 + TH],
                        kqo[i % 2][:, :]), 16)
        ctxb.close()

        ctxc = ExitStack()
        sbc = lambda n, s, d: ctxc.enter_context(nc.sbuf_tensor(n, s, d))
        vw = [sbc(f"v{th}w{i}", [128, KC * 512], BF16) for i in range(2)]
        vo = [sbc(f"v{th}o{i}", [128, 512], BF16) for i in range(2)]
        psV = [ctxc.enter_context(nc.psum_tensor(f"v{th}p{i}", [128, 512], F32))
               for i in range(2)]
        NSv = FO // 512
        MCv = TH // 128
        mb2, eb2 = sMM.n, sEV.n
        gwb2 = [gw[0].n, gw[1].n]
        asb2 = [asx[0].n, asx[1].n]
        vstrip_t = lambda ns: gwb2[ns % 2] + 16 * (ns // 2 + 1)
        vstore_t = lambda g: asb2[g % 2] + 16 * (g // 2 + 1)
        with nc.Block(f"p2v{th}") as b:
            @b.gpsimd
            def _(gp):
                self.flush(gp, gw)
                for ns in range(NSv):
                    if ns >= 2:
                        gp.wait_ge(sMM.sem, mb2 + (ns - 1) * MCv)
                    gw[ns % 2].bump(gp.dma_start(
                        vw[ns % 2][:, :].rearrange("p (kc n) -> p kc n", kc=KC),
                        self.wvT[:, ns * 512:(ns + 1) * 512].rearrange(
                            "(kc p) n -> p kc n", p=128)), 16)

            @b.tensor
            def _(t):
                for ns in range(NSv):
                    t.wait_ge(gw[ns % 2].sem, vstrip_t(ns))
                    for m in range(MCv):
                        g = ns * MCv + m
                        if g >= 2:
                            t.wait_ge(sEV.sem, eb2 + g - 1)
                        for k in range(KC):
                            inst = t.matmul(
                                psV[g % 2][:, :],
                                xbT[:, k * TH + m * 128: k * TH + (m + 1) * 128],
                                vw[ns % 2][:, k * 512:(k + 1) * 512],
                                start=(k == 0), stop=(k == KC - 1))
                        sMM.bump(inst, 1)

            @b.scalar
            def _(s):
                self.flush(s, asx)
                for ns in range(NSv):
                    for m in range(MCv):
                        g = ns * MCv + m
                        s.wait_ge(sMM.sem, mb2 + g + 1)
                        if g >= 2:
                            s.wait_ge(asx[g % 2].sem, vstore_t(g - 2))
                        w = sEV.bump(s.copy(vo[g % 2][:, :], psV[g % 2][:, :]), 1)
                        s.wait_ge(sEV.sem, w)
                        asx[g % 2].bump(s.dma_start(
                            self.vts[tc0 + m * 128:tc0 + (m + 1) * 128,
                                     ns * 512:(ns + 1) * 512],
                            vo[g % 2][:, :]), 16)
        ctxc.close()

    def p3_attention(self, max_groups=99, max_qt=99):
        nc = self.nc
        T, NH, FO = self.T, self.NH, self.FO
        NQ = min(self.NQ, max_qt)
        qm, qv, asx = self.qm, self.qv, self.asx
        sMM, sEV, sDV, sPL = self.sMM, self.sEV, self.sDV, self.sPL
        groups = [list(range(0, 6)), list(range(6, 12)), list(range(12, 16))]
        groups = groups[:max_groups]
        JPQ = QTW // 128
        NC8 = T // 512
        ctx0 = ExitStack()
        sb = lambda n, s, d: ctx0.enter_context(nc.sbuf_tensor(n, s, d))
        Ksb = [sb(f"at_k{i}", [128, T], BF16) for i in range(6)]
        Qful = [sb(f"at_q{i}", [128, T], BF16) for i in range(6)]
        Vsb = [sb(f"at_v{i}", [128, FO], BF16) for i in range(3)]
        Esb = [sb(f"at_e{i}", [128, 512], BF16) for i in range(6)]
        Zac = [[sb(f"at_z{i}_{p}", [128, 512], F32) for p in range(2)] for i in range(6)]
        Zfin = [sb(f"at_zf{i}", [128, 512], F32) for i in range(6)]
        Qsq = sb("at_qs", [128, T], BF16)
        Ksq = sb("at_ks", [128, T], BF16)
        qnr = sb("at_qn", [1, 512], F32)
        knr = sb("at_kn", [1, 512], F32)
        tmpmx = sb("at_tm", [1, 2], F32)
        mq2 = sb("at_mq2", [1, 6], F32)
        mk2 = sb("at_mk2", [1, 6], F32)
        sumb = sb("at_sumb", [1, 6], F32)
        negbf = sb("at_negbf", [1, 6], F32)
        negbc = [sb(f"at_nb{i}", [128, 1], F32) for i in range(6)]
        zrow = [sb(f"at_zr{i}", [1, 512], F32) for i in range(6)]
        rzrow = [sb(f"at_rz{i}", [1, 512], F32) for i in range(6)]
        rzbc = [sb(f"at_rb{i}", [128, 512], F32) for i in range(2)]
        aout = [sb(f"at_ao{i}", [128, 512], BF16) for i in range(2)]
        psO = [ctx0.enter_context(nc.psum_tensor(f"at_pO{i}", [128, 512], F32))
               for i in range(6)]
        psS = [ctx0.enter_context(nc.psum_tensor(f"at_pS{i}", [128, 512], F32))
               for i in range(2)]

        def round_seq(H):
            seq = []
            for sl in range(H):
                seq.append(("S", sl))
                if sl >= 2:
                    seq.append(("PV", sl - 2))
            seq.append(("PV", H - 2))
            seq.append(("PV", H - 1))
            return seq

        zeng = lambda slot: "dve" if slot % 2 == 0 else "pl"

        C = {}
        q = {"pe": sMM.n, "act": sEV.n, "dve": sDV.n, "pl": sPL.n,
             "qm": qm.n, "qv": [qv[0].n, qv[1].n, qv[2].n],
             "as": [asx[0].n, asx[1].n]}
        last_sto = {0: None, 1: None}
        all_rounds = []
        for gi, hg in enumerate(groups):
            H = len(hg)
            q["qm"] += 16 * 2 * H
            C[("kq", gi)] = q["qm"]
            for sl in range(H):
                for c in range(NC8):
                    q["pe"] += 1
                    C[("qnc", gi, sl, c)] = q["pe"]
                    q["pe"] += 1
                    C[("knc", gi, sl, c)] = q["pe"]
            for sl in range(H):
                q["pe"] += 1
                C[("nb", gi, sl)] = q["pe"]
            for sl in range(H):
                for c in range(NC8):
                    q["act"] += 1
                    C[("eqc", gi, sl, c)] = q["act"]
                    q["act"] += 1
                    C[("ekc", gi, sl, c)] = q["act"]
            for sl in range(H):
                q["act"] += 1
                C[("enb", gi, sl)] = q["act"]
            for sl in range(H):
                q["dve"] += 1
                C[("dqs", gi, sl)] = q["dve"]
                q["dve"] += 1
                C[("dks", gi, sl)] = q["dve"]
                for c in range(NC8):
                    q["dve"] += 1
                    C[("drq", gi, sl, c)] = q["dve"]
                    q["dve"] += 1
                    C[("drk", gi, sl, c)] = q["dve"]
            for sl in range(H):
                q["dve"] += 1
                C[("dnb", gi, sl)] = q["dve"]
            for qt in range(NQ):
                J = (qt + 1) * JPQ
                for jc in range(J):
                    sl3 = jc % 3
                    q["qv"][sl3] += 16
                    C[("ldv", gi, qt, jc)] = (sl3, q["qv"][sl3])
                    all_rounds.append((gi, qt, jc, H))
                for jc in range(J):
                    for kind, sl in round_seq(H):
                        q["pe"] += 1
                        C[(kind, gi, qt, jc, sl)] = q["pe"]
                for sl in range(H):
                    q["pe"] += 1
                    C[("zmm", gi, qt, sl)] = q["pe"]
                for sl in range(H):
                    q["pe"] += 1
                    C[("rzb", gi, qt, sl)] = q["pe"]
                for jc in range(J):
                    for sl in range(H):
                        q["act"] += 1
                        C[("exp", gi, qt, jc, sl)] = q["act"]
                for sl in range(H):
                    q["act"] += 1
                    C[("ezr", gi, qt, sl)] = q["act"]
                for sl in range(H):
                    q["act"] += 1
                    C[("erz", gi, qt, sl)] = q["act"]
                    p = sl % 2
                    q["as"][p] += 16
                    C[("sto", gi, qt, sl)] = (p, q["as"][p])
                for jc in range(J):
                    diag = jc >= qt * JPQ
                    for sl in range(H):
                        if diag:
                            q["dve"] += 1
                            C[("pm", gi, qt, jc, sl)] = q["dve"]
                        e = zeng(sl)
                        q[e] += 1
                        C[("za", gi, qt, jc, sl)] = q[e]
                for sl in range(H):
                    e = zeng(sl)
                    q[e] += 1
                    C[("zf", gi, qt, sl)] = q[e]
                for sl in range(H):
                    q["dve"] += 1
                    C[("rcp", gi, qt, sl)] = q["dve"]
                for sl in range(H):
                    q["dve"] += 1
                    C[("emu", gi, qt, sl)] = q["dve"]
                for sl in range(H):
                    C[("aoutw", gi, qt, sl)] = last_sto[sl % 2]
                    last_sto[sl % 2] = C[("sto", gi, qt, sl)]

        vprev = {}
        by_slot = {0: None, 1: None, 2: None}
        for (gi, qt, jc, H) in all_rounds:
            vprev[(gi, qt, jc)] = by_slot[jc % 3]
            by_slot[jc % 3] = (gi, qt, jc, H)

        sem_of = {"pe": sMM, "act": sEV, "dve": sDV, "pl": sPL}

        def prev_group_pe_done(gi):
            Hp = len(groups[gi - 1])
            return C[("rzb", gi - 1, NQ - 1, Hp - 1)]

        with nc.Block("p3") as b:
            @b.sync
            def _(sync):
                self.flush(sync, [qm] + qv)
                for gd in self.p2_stores_final:
                    sync.wait_ge(gd[0].sem, gd[1])
                for gi, hg in enumerate(groups):
                    H = len(hg)
                    if gi >= 1:
                        sync.wait_ge(sMM.sem, prev_group_pe_done(gi))
                    for sl in range(H):
                        hl = hg[sl]
                        qm.bump(sync.dma_start(
                            Ksb[sl][:, :], self.kts[hl * HD:(hl + 1) * HD, :]), 16)
                        qm.bump(sync.dma_start(
                            Qful[sl][:, :], self.qts[hl * HD:(hl + 1) * HD, :]), 16)
                    assert qm.n == C[("kq", gi)]
                    for qt in range(NQ):
                        J = (qt + 1) * JPQ
                        for jc in range(J):
                            pv = vprev[(gi, qt, jc)]
                            if pv is not None:
                                pgi, pqt, pjc, pH = pv
                                sync.wait_ge(sMM.sem,
                                             C[("PV", pgi, pqt, pjc, pH - 1)])
                            sl3, tgt = C[("ldv", gi, qt, jc)]
                            qv[sl3].bump(sync.dma_start(
                                Vsb[sl3][:, :],
                                self.vts[jc * 128:(jc + 1) * 128, :]), 16)
                            assert qv[sl3].n == tgt

            @b.tensor
            def _(t):
                psS_last_evac = [(None, 0), (None, 0)]
                psO_free = [None] * 6
                for gi, hg in enumerate(groups):
                    H = len(hg)
                    for bank in range(2):
                        if psO_free[bank] is not None:
                            e, cval = psO_free[bank]
                            t.wait_ge(sem_of[e].sem, cval)
                    for sl in range(H):
                        t.wait_ge(sDV.sem, C[("dqs", gi, sl)])
                        t.wait_ge(sDV.sem, C[("dks", gi, sl)])
                        for c in range(NC8):
                            if (sl, c) != (0, 0):
                                pc = (sl, c - 1) if c > 0 else (sl - 1, NC8 - 1)
                                t.wait_ge(sEV.sem, C[("eqc", gi) + pc])
                            t.matmul(psO[0][0:1, :], self.ones_col_bf[:, :],
                                     Qsq[:, c * 512:(c + 1) * 512],
                                     start=True, stop=True).then_inc(sMM.sem, 1)
                            if (sl, c) != (0, 0):
                                pc = (sl, c - 1) if c > 0 else (sl - 1, NC8 - 1)
                                t.wait_ge(sEV.sem, C[("ekc", gi) + pc])
                            t.matmul(psO[1][0:1, :], self.ones_col_bf[:, :],
                                     Ksq[:, c * 512:(c + 1) * 512],
                                     start=True, stop=True).then_inc(sMM.sem, 1)
                    for sl in range(H):
                        if sl == 0 and psO_free[2] is not None:
                            e, cval = psO_free[2]
                            t.wait_ge(sem_of[e].sem, cval)
                        t.wait_ge(sDV.sem, C[("dnb", gi, sl)])
                        if sl >= 1:
                            t.wait_ge(sEV.sem, C[("enb", gi, sl - 1)])
                        t.matmul(psO[2][:, 0:1], self.ones_row[:, :],
                                 negbf[0:1, sl:sl + 1], start=True, stop=True
                                 ).then_inc(sMM.sem, 1)
                    for qt in range(NQ):
                        J = (qt + 1) * JPQ
                        for jc in range(J):
                            for kind, sl in round_seq(H):
                                if kind == "S":
                                    e, cval = psS_last_evac[sl % 2]
                                    if e is not None:
                                        t.wait_ge(sem_of[e].sem, cval)
                                    t.wait_ge(qm.sem, C[("kq", gi)])
                                    t.matmul(psS[sl % 2][:, :],
                                             Ksb[sl][:, jc * 128:(jc + 1) * 128],
                                             Qful[sl][:, qt * QTW:(qt + 1) * QTW],
                                             start=True, stop=True
                                             ).then_inc(sMM.sem, 1)
                                    psS_last_evac[sl % 2] = (
                                        "act", C[("exp", gi, qt, jc, sl)])
                                else:
                                    t.wait_ge(sEV.sem, C[("exp", gi, qt, jc, sl)])
                                    if qt == 0 and jc == 0:
                                        t.wait_ge(sEV.sem, C[("enb", gi, H - 1)])
                                    sl3, tgt = C[("ldv", gi, qt, jc)]
                                    t.wait_ge(qv[sl3].sem, tgt)
                                    hl = hg[sl]
                                    t.matmul(psO[sl][:, :],
                                             Vsb[sl3][:, hl * HD:(hl + 1) * HD],
                                             Esb[sl][:, :],
                                             start=(jc == 0), stop=(jc == J - 1)
                                             ).then_inc(sMM.sem, 1)
                        for sl in range(H):
                            e = zeng(sl)
                            t.wait_ge(sem_of[e].sem, C[("zf", gi, qt, sl)])
                            ep, cval = psS_last_evac[sl % 2]
                            if ep is not None:
                                t.wait_ge(sem_of[ep].sem, cval)
                            t.matmul(psS[sl % 2][0:1, :], self.ones_col[:, :],
                                     Zfin[sl][:, :], start=True, stop=True
                                     ).then_inc(sMM.sem, 1)
                            psS_last_evac[sl % 2] = ("act", C[("ezr", gi, qt, sl)])
                        for sl in range(H):
                            t.wait_ge(sDV.sem, C[("rcp", gi, qt, sl)])
                            ep, cval = psS_last_evac[sl % 2]
                            t.wait_ge(sem_of[ep].sem, cval)
                            t.matmul(psS[sl % 2][:, :], self.ones_row[:, :],
                                     rzrow[sl][0:1, :], start=True, stop=True
                                     ).then_inc(sMM.sem, 1)
                            psS_last_evac[sl % 2] = ("act", C[("erz", gi, qt, sl)])
                        for sl in range(H):
                            psO_free[sl] = ("dve", C[("emu", gi, qt, sl)])
                sMM.n = q["pe"]

            @b.scalar
            def _(s):
                self.flush(s, asx)
                for gi, hg in enumerate(groups):
                    H = len(hg)
                    for sl in range(H):
                        for c in range(NC8):
                            s.wait_ge(sMM.sem, C[("qnc", gi, sl, c)])
                            pq = (sl, c - 1) if c > 0 else \
                                ((sl - 1, NC8 - 1) if sl >= 1 else None)
                            if pq is not None:
                                s.wait_ge(sDV.sem, C[("drq", gi) + pq])
                            s.copy(qnr[0:1, :], psO[0][0:1, :]
                                   ).then_inc(sEV.sem, 1)
                            s.wait_ge(sMM.sem, C[("knc", gi, sl, c)])
                            if pq is not None:
                                s.wait_ge(sDV.sem, C[("drk", gi) + pq])
                            s.copy(knr[0:1, :], psO[1][0:1, :]
                                   ).then_inc(sEV.sem, 1)
                    for sl in range(H):
                        s.wait_ge(sMM.sem, C[("nb", gi, sl)])
                        s.copy(negbc[sl][:, :], psO[2][:, 0:1]
                               ).then_inc(sEV.sem, 1)
                    for qt in range(NQ):
                        J = (qt + 1) * JPQ
                        for jc in range(J):
                            diag = jc >= qt * JPQ
                            for sl in range(H):
                                s.wait_ge(sMM.sem, C[("S", gi, qt, jc, sl)])
                                if diag:
                                    s.wait_ge(sDV.sem, C[("pm", gi, qt, jc, sl)])
                                if jc >= 1:
                                    s.wait_ge(sMM.sem, C[("PV", gi, qt, jc - 1, sl)])
                                    e = zeng(sl)
                                    s.wait_ge(sem_of[e].sem,
                                              C[("za", gi, qt, jc - 1, sl)])
                                elif qt >= 1:
                                    Jp = qt * JPQ
                                    s.wait_ge(sMM.sem, C[("PV", gi, qt - 1, Jp - 1, sl)])
                                    e = zeng(sl)
                                    s.wait_ge(sem_of[e].sem,
                                              C[("za", gi, qt - 1, Jp - 1, sl)])
                                elif gi >= 1 and sl < len(groups[gi - 1]):
                                    Jp = NQ * JPQ
                                    s.wait_ge(sMM.sem,
                                              C[("PV", gi - 1, NQ - 1, Jp - 1, sl)])
                                    e = zeng(sl)
                                    s.wait_ge(sem_of[e].sem,
                                              C[("za", gi - 1, NQ - 1, Jp - 1, sl)])
                                s.activation(Esb[sl][:, :], psS[sl % 2][:, :],
                                             AF.Exp, bias=negbc[sl][:, 0:1],
                                             scale=INV_SQRT_HD).then_inc(sEV.sem, 1)
                        for sl in range(H):
                            s.wait_ge(sMM.sem, C[("zmm", gi, qt, sl)])
                            s.copy(zrow[sl][:, :], psS[sl % 2][0:1, :]
                                   ).then_inc(sEV.sem, 1)
                        for sl in range(H):
                            hl = hg[sl]
                            s.wait_ge(sMM.sem, C[("rzb", gi, qt, sl)])
                            if sl >= 2:
                                s.wait_ge(sDV.sem, C[("emu", gi, qt, sl - 2)])
                            s.copy(rzbc[sl % 2][:, :], psS[sl % 2][:, :]
                                   ).then_inc(sEV.sem, 1)
                            s.wait_ge(sDV.sem, C[("emu", gi, qt, sl)])
                            p, tgt = C[("sto", gi, qt, sl)]
                            asx[p].bump(s.dma_start(
                                self.atns[hl * HD:(hl + 1) * HD,
                                          qt * QTW:(qt + 1) * QTW],
                                aout[sl % 2][:, :]), 16)
                            assert asx[p].n == tgt
                sEV.n = q["act"]

            @b.vector
            def _(v):
                for gi, hg in enumerate(groups):
                    H = len(hg)
                    for sl in range(H):
                        v.wait_ge(qm.sem, C[("kq", gi)])
                        if sl >= 1:
                            v.wait_ge(sMM.sem, C[("qnc", gi, sl - 1, NC8 - 1)])
                            v.wait_ge(sMM.sem, C[("knc", gi, sl - 1, NC8 - 1)])
                        v.tensor_tensor(Qsq[:, :], Qful[sl][:, :],
                                        Qful[sl][:, :], ALU.mult
                                        ).then_inc(sDV.sem, 1)
                        v.tensor_tensor(Ksq[:, :], Ksb[sl][:, :],
                                        Ksb[sl][:, :], ALU.mult
                                        ).then_inc(sDV.sem, 1)
                        for c in range(NC8):
                            v.wait_ge(sEV.sem, C[("eqc", gi, sl, c)])
                            v.reduce_max(tmpmx[0:1, 0:1], qnr[0:1, :], AX
                                         ).then_inc(sDV.sem, 1)
                            v.wait_ge(sEV.sem, C[("ekc", gi, sl, c)])
                            v.reduce_max(tmpmx[0:1, 1:2], knr[0:1, :], AX
                                         ).then_inc(sDV.sem, 1)
                            if c == 0:
                                v.tensor_copy(mq2[0:1, sl:sl + 1],
                                              tmpmx[0:1, 0:1])
                                v.tensor_copy(mk2[0:1, sl:sl + 1],
                                              tmpmx[0:1, 1:2])
                            else:
                                v.tensor_tensor(mq2[0:1, sl:sl + 1],
                                                mq2[0:1, sl:sl + 1],
                                                tmpmx[0:1, 0:1], ALU.max)
                                v.tensor_tensor(mk2[0:1, sl:sl + 1],
                                                mk2[0:1, sl:sl + 1],
                                                tmpmx[0:1, 1:2], ALU.max)
                    for sl in range(H):
                        v.tensor_tensor(sumb[0:1, sl:sl + 1],
                                        mq2[0:1, sl:sl + 1],
                                        mk2[0:1, sl:sl + 1], ALU.add)
                    for sl in range(H):
                        v.tensor_scalar(negbf[0:1, sl:sl + 1],
                                        sumb[0:1, sl:sl + 1], C_BOUND, None,
                                        ALU.mult).then_inc(sDV.sem, 1)
                    for qt in range(NQ):
                        J = (qt + 1) * JPQ
                        for jc in range(J):
                            diag = jc >= qt * JPQ
                            for sl in range(H):
                                if diag:
                                    dlt = jc * 128 - qt * QTW
                                    v.wait_ge(sMM.sem, C[("S", gi, qt, jc, sl)])
                                    v.tensor_tensor(
                                        psS[sl % 2][:, :], psS[sl % 2][:, :],
                                        self.mask_sb[:, 384 - dlt:896 - dlt],
                                        ALU.add).then_inc(sDV.sem, 1)
                                if zeng(sl) == "dve":
                                    v.wait_ge(sEV.sem, C[("exp", gi, qt, jc, sl)])
                                    if jc < 2:
                                        v.tensor_copy(Zac[sl][jc % 2][:, :],
                                                      Esb[sl][:, :]
                                                      ).then_inc(sDV.sem, 1)
                                    else:
                                        v.tensor_tensor(Zac[sl][jc % 2][:, :],
                                                        Esb[sl][:, :],
                                                        Zac[sl][jc % 2][:, :],
                                                        ALU.add).then_inc(sDV.sem, 1)
                        for sl in range(H):
                            if zeng(sl) == "dve":
                                v.tensor_tensor(Zfin[sl][:, :], Zac[sl][0][:, :],
                                                Zac[sl][1][:, :], ALU.add
                                                ).then_inc(sDV.sem, 1)
                        for sl in range(H):
                            v.wait_ge(sEV.sem, C[("ezr", gi, qt, sl)])
                            v.reciprocal(rzrow[sl][:, :], zrow[sl][:, :]
                                         ).then_inc(sDV.sem, 1)
                        for sl in range(H):
                            v.wait_ge(sEV.sem, C[("erz", gi, qt, sl)])
                            v.wait_ge(sMM.sem, C[("PV", gi, qt, J - 1, sl)])
                            if C[("aoutw", gi, qt, sl)] is not None:
                                p, tgt = C[("aoutw", gi, qt, sl)]
                                v.wait_ge(asx[p].sem, tgt)
                            v.tensor_tensor(aout[sl % 2][:, :], psO[sl][:, :],
                                            rzbc[sl % 2][:, :], ALU.mult
                                            ).then_inc(sDV.sem, 1)
                sDV.n = q["dve"]

            @b.gpsimd
            def _(g):
                for gi, hg in enumerate(groups):
                    H = len(hg)
                    for qt in range(NQ):
                        J = (qt + 1) * JPQ
                        for jc in range(J):
                            for sl in range(H):
                                if zeng(sl) != "pl":
                                    continue
                                g.wait_ge(sEV.sem, C[("exp", gi, qt, jc, sl)])
                                if jc < 2:
                                    g.tensor_copy(Zac[sl][jc % 2][:, :],
                                                  Esb[sl][:, :]).then_inc(sPL.sem, 1)
                                else:
                                    g.tensor_tensor(Zac[sl][jc % 2][:, :],
                                                    Esb[sl][:, :],
                                                    Zac[sl][jc % 2][:, :],
                                                    ALU.add).then_inc(sPL.sem, 1)
                        for sl in range(H):
                            if zeng(sl) == "pl":
                                g.tensor_tensor(Zfin[sl][:, :], Zac[sl][0][:, :],
                                                Zac[sl][1][:, :], ALU.add
                                                ).then_inc(sPL.sem, 1)
                sPL.n = q["pl"]
        ctx0.close()
        self.p3_stores_final = [(asx[0], asx[0].n), (asx[1], asx[1].n)]


    def build(self, upto=99):
        nc = self.nc
        T, TO, FO = self.T, self.TO, self.FO
        qm, gm, sCC = self.qm, self.gm, self.sCC

        self.prologue()
        if upto >= 1:
            self.p0_dequant()
        else:
            self.cc_wdqL = self.cc_wdqR = 0
        if upto < 2:
            return self.finish(None)

        ln1_done = self.ln_phase("ln1", self.xin, self.xabs)
        ctxr = ExitStack()
        resident = ctxr.enter_context(nc.sbuf_tensor("res1", [128, KC * TO], BF16))
        rr = self.transpose_resident("tr1", self.xabs, resident, TO, ln1_done)
        p1_done = self.gemm_token("p1", resident, rr, 'bf16', self.wdqH,
                                  self.xin, self.x1s, 0.5,
                                  sync_guard=[(self.sDV, self.last_ln_dve)],
                                  strip_guards=[(sCC, self.cc_wdqL),
                                                (sCC, self.cc_wdqR)])
        ctxr.close()
        if upto < 3:
            return self.finish(p1_done)

        ln2_done = self.ln_phase("ln2", self.x1s, self.xbo, guards=p1_done)
        with nc.Block("p2x") as b:
            @b.gpsimd
            def _(g):
                for gd in ln2_done:
                    g.wait_ge(gd[0].sem, gd[1])
                sCC.bump(g.collective_compute(
                    "AllGather", ALU.bypass,
                    replica_groups=[[0, 1], [2, 3], [4, 5], [6, 7]],
                    ins=[self.xbo.opt()],
                    outs=[self.xbf.opt()]), 1)
        cc_xbf = sCC.n
        if upto < 4:
            return self.finish(None)

        for th in range(2):
            ctxh = ExitStack()
            xbT = ctxh.enter_context(
                nc.sbuf_tensor(f"xbT{th}", [128, KC * (T // 2)], BF16))
            TH = T // 2
            xbT_ready = self.transpose_resident(
                f"tr2{th}", self.xbf[th * TH:(th + 1) * TH, :],
                xbT, TH, [(sCC, cc_xbf)])
            self.p2c_half(th, xbT, xbT_ready)
            ctxh.close()
        self.p2_stores_final = [(self.asx[0], self.asx[0].n),
                                (self.asx[1], self.asx[1].n)]
        if upto < 5:
            return self.finish(None)

        self.p3_attention(max_groups=getattr(self, 'p3_groups', 99),
                          max_qt=getattr(self, 'p3_qt', 99))
        if upto < 6:
            return self.finish(self.p3_stores_final)

        ctxr4 = ExitStack()
        res4 = ctxr4.enter_context(
            nc.sbuf_tensor("res4", [128, (FO // 128) * T], BF16))
        with nc.Block("p4r") as b:
            @b.sync
            def _(sync):
                self.flush(sync, [qm])
                for gd in self.p3_stores_final:
                    sync.wait_ge(gd[0].sem, gd[1])
                for kc in range(FO // 128):
                    qm.bump(sync.dma_start(
                        res4[:, kc * T:(kc + 1) * T],
                        self.atns[kc * 128:(kc + 1) * 128, :]), 16)
        asb4 = [self.asx[0].n, self.asx[1].n]
        p4_done = self.gemm_token("p4", res4, [(qm, qm.n)], 'f32', self.woT,
                                  None, self.op4H, 1.0, rows=T, nkc=FO // 128,
                                  out_dt=BF16)
        ctxr4.close()
        nleft = (D // 512 // 2) * (T // 128) // 2
        left_done = [(self.asx[0], asb4[0] + 16 * nleft),
                     (self.asx[1], asb4[1] + 16 * nleft)]
        with nc.Block("p4s") as b:
            @b.gpsimd
            def _(g):
                for gd in left_done:
                    g.wait_ge(gd[0].sem, gd[1])
                sCC.bump(g.collective_compute(
                    "ReduceScatter", ALU.add,
                    replica_groups=[[0, 1], [2, 3], [4, 5], [6, 7]],
                    ins=[self.op4H[0].opt()],
                    outs=[self.x2pH[0].opt()]), 1)
                for gd in p4_done:
                    g.wait_ge(gd[0].sem, gd[1])
                sCC.bump(g.collective_compute(
                    "ReduceScatter", ALU.add,
                    replica_groups=[[0, 1], [2, 3], [4, 5], [6, 7]],
                    ins=[self.op4H[1].opt()],
                    outs=[self.x2pH[1].opt()]), 1)
        cc_rs = sCC.n
        if upto < 7:
            return self.finish(None)

        ln3_done = self.ln_phase("ln3", self.x2pH, self.xabs,
                                 guards=[(sCC, cc_rs)],
                                 src2=self.x1s, sum_dst=self.x2s)
        ctxr5 = ExitStack()
        res5 = ctxr5.enter_context(nc.sbuf_tensor("res5", [128, KC * TO], BF16))
        rr5 = self.transpose_resident("tr5", self.xabs, res5, TO, ln3_done)
        p5_done = self.gemm_token("p5", res5, rr5, 'bf16', self.wdqH,
                                  self.x2s, self.xout, 0.5,
                                  sync_guard=[(self.sDV, self.last_ln_dve)] + ln3_done)
        ctxr5.close()
        return self.finish(p5_done)

    def finish(self, out_done):
        nc = self.nc
        gm = self.gm
        with nc.Block("fin") as b:
            @b.scalar
            def _(s):
                if out_done is not None:
                    for gd in out_done:
                        s.wait_ge(gd[0].sem, gd[1])

            if self.dbg_views:
                @b.gpsimd
                def _(g):
                    if out_done is not None:
                        for gd in out_done:
                            g.wait_ge(gd[0].sem, gd[1])
                    srcs = {"xbf": self.xbf, "x1s": self.x1s,
                            "x2s": self.x2s, "kts": self.kts, "qts": self.qts,
                            "vts": self.vts, "atns": self.atns}
                    for nm, dst in self.dbg_views.items():
                        gm.bump(g.dma_start(dst[:, :], srcs[nm][:, :]), 16)
                    g.wait_ge(gm.sem, gm.n)
        self.ctx.close()
        return nc


def build_program(T, dbg=False):
    return Builder(T, dbg)


def host_prep(L0_attn_x, L0_attn_wq, L0_attn_wk, L0_attn_wv, L0_attn_wo,
              L0_ffn_weight_packed, L0_ffn_centroids, L0_ffn_scales):
    T = L0_attn_x.shape[1]
    TO = T // 2
    x = np.ascontiguousarray(np.asarray(L0_attn_x))
    wqT = np.ascontiguousarray(np.asarray(L0_attn_wq).T)
    wkT = np.ascontiguousarray(np.asarray(L0_attn_wk).T)
    wvT = np.ascontiguousarray(np.asarray(L0_attn_wv).T)
    woT = np.ascontiguousarray(np.asarray(L0_attn_wo).T)
    pakT = np.ascontiguousarray(np.asarray(L0_ffn_weight_packed).T)
    scales = np.asarray(L0_ffn_scales)
    cent = np.asarray(L0_ffn_centroids).reshape(1, 16).astype(np.float32)
    sinT, cosT, maskneg = host_tables(T)

    in_maps = []
    for c in range(8):
        s, par = c // 2, c % 2
        fsl = np.arange(c * (D // 8), (c + 1) * (D // 8))
        sclx = np.ascontiguousarray(scales.T[fsl // 64, :])
        hb = par * (NHG // 2)
        hc = slice(hb * HD, (hb + NHG // 2) * HD)
        in_maps.append({
            "xin": np.ascontiguousarray(x[s, par * TO:(par + 1) * TO, :]),
            "wkT": np.ascontiguousarray(wkT[:, hc]),
            "wvT": np.ascontiguousarray(wvT[:, hc]),
            "wqT": np.ascontiguousarray(wqT[:, hc]),
            "woT": np.ascontiguousarray(woT[hb * HD:(hb + NHG // 2) * HD, :]),
            "pakT": np.ascontiguousarray(pakT[c * (D // 16):(c + 1) * (D // 16), :]),
            "sclx": sclx,
            "cent": cent,
            "sinT": sinT,
            "cosT": cosT,
            "maskneg": maskneg,
            "cinfo": np.array([[par * TO, par, c, s, 0, 0, 0, 0]], np.int32),
        })
    return T, in_maps


def kernel(L0_attn_x, L0_attn_wq, L0_attn_wk, L0_attn_wv, L0_attn_wo,
           L0_ffn_weight_packed, L0_ffn_centroids, L0_ffn_scales):
    T, in_maps = host_prep(L0_attn_x, L0_attn_wq, L0_attn_wk, L0_attn_wv,
                           L0_attn_wo, L0_ffn_weight_packed, L0_ffn_centroids,
                           L0_ffn_scales)
    TO = T // 2
    bld = build_program(T, dbg=False)
    nc = bld.build()
    res = run_bass_kernel_spmd(nc, in_maps, core_ids=list(range(8)))
    out = np.empty((4, T, D), np.float32)
    for c in range(8):
        s, par = c // 2, c % 2
        out[s, par * TO:(par + 1) * TO, :] = res.results[c]["xout"]
    return out
